# version 12
# baseline (speedup 1.0000x reference)
import numpy as np
import ml_dtypes
from contextlib import ExitStack

import concourse.bass as bass
import concourse.mybir as mybir
from concourse.bass_utils import run_bass_kernel_spmd

F32 = mybir.dt.float32
BF16 = mybir.dt.bfloat16
FP8 = mybir.dt.float8e4
AF = mybir.ActivationFunctionType
ALU = mybir.AluOpType
AX = mybir.AxisListType

NSEQ, L, D, DIN = 2, 2048, 2048, 4096
CONVD, PROJ, INTER, CHUNK, GTOK = 4352, 8512, 5632, 128, 512
NCH, NGRP, EPS = GTOK // CHUNK, L // GTOK, 1e-5
NCT, NXT = CONVD // 128, DIN // 128
NGROUPS_FULL = NGRP * NSEQ


class Prog:
    def __init__(self):
        self.streams = {k: [] for k in ("sync", "tensor", "vector", "scalar", "gpsimd")}
        self.C = {k: 0 for k in ("ld", "l2", "wq", "wqa", "wqb", "xq", "fqa", "fqb",
                                 "fqc", "fqd", "gq", "t", "v", "s", "g", "y", "st",
                                 "wsa", "wsb", "wsc", "wsd")}
        self.sems = {}

    def emit(self, eng, fn):
        self.streams[eng].append(fn)

    SELF_SEM = {"vector": "v", "scalar": "s"}

    def op(self, eng, make, inc=None, inc_n=1):
        own = self.SELF_SEM.get(eng)
        if own is not None:
            self.wait(eng, own)
            if inc is None:
                inc, inc_n = own, 1
        if inc is not None:
            self.C[inc] += inc_n
        sems = self.sems

        def f(e, make=make, inc=inc, inc_n=inc_n):
            i = make(e)
            if inc is not None:
                i.then_inc(sems[inc], inc_n)
        self.emit(eng, f)

    def wait(self, eng, sem, val=None):
        if val is None:
            val = self.C[sem]
        if val <= 0:
            return
        sems = self.sems
        self.emit(eng, lambda e, sem=sem, val=val: e.wait_ge(sems[sem], val))

    def inc_only(self, eng, sem, n=1):
        self.C[sem] += n
        sems = self.sems
        def f(e, sem=sem, n=n):
            e.drain()
            e.sem_inc(sems[sem], n)
        self.emit(eng, f)

    def barrier(self):
        names = {"tensor": "t", "vector": "v", "scalar": "s", "gpsimd": "g", "sync": "y"}
        for eng, nm in names.items():
            self.inc_only(eng, nm)
        for eng in names:
            for other, onm in names.items():
                if other != eng:
                    self.wait(eng, onm)


def build_program(n_groups=NGROUPS_FULL, taps=()):
    nc = bass.Bass("TRN2", target_bir_lowering=False, debug=False)
    taps = list(taps)

    x_in = nc.dram_tensor("x", [NSEQ, L, D], F32, kind="ExternalInput").ap()
    out_t = nc.dram_tensor("out", [NSEQ, L, D], F32, kind="ExternalOutput").ap()
    wipT = nc.dram_tensor("wipT", [D, PROJ], F32, kind="ExternalInput").ap()
    wopT = nc.dram_tensor("wopT", [DIN, D], F32, kind="ExternalInput").ap()
    wgT = nc.dram_tensor("wgT", [D, INTER], F32, kind="ExternalInput").ap()
    wuT = nc.dram_tensor("wuT", [D, INTER], F32, kind="ExternalInput").ap()
    wdT = nc.dram_tensor("wdT", [INTER, D], F32, kind="ExternalInput").ap()
    convw_in = nc.dram_tensor("convw", [128, NCT * 4], F32, kind="ExternalInput").ap()
    convb_in = nc.dram_tensor("convb", [128, NCT], F32, kind="ExternalInput").ap()
    dtb_in = nc.dram_tensor("dtbias", [64, 1], F32, kind="ExternalInput").ap()
    alog_in = nc.dram_tensor("alog", [64, 1], F32, kind="ExternalInput").ap()
    drow_in = nc.dram_tensor("drow", [1, 64], F32, kind="ExternalInput").ap()
    nwrow_in = nc.dram_tensor("nwrow", [1, DIN], F32, kind="ExternalInput").ap()
    dsrow_in = nc.dram_tensor("dsrow", [1, D], F32, kind="ExternalInput").ap()
    gs_in = nc.dram_tensor("gs", [128, 44], F32, kind="ExternalInput").ap()
    us_in = nc.dram_tensor("us", [128, 44], F32, kind="ExternalInput").ap()
    idf_in = nc.dram_tensor("idf", [64, 64], F32, kind="ExternalInput").ap()
    idb_in = nc.dram_tensor("idb", [128, 128], F32, kind="ExternalInput").ap()
    mask_in = nc.dram_tensor("mask01", [128, 128], F32, kind="ExternalInput").ap()
    k2c_in = nc.dram_tensor("k2c", [10, 1280], BF16, kind="ExternalInput").ap()
    wbf = nc.dram_tensor("wbf", [99, 128, 2048], BF16, kind="Internal").ap()
    wbf8 = nc.dram_tensor("wbf8", [132, 128, 2048], FP8, kind="Internal").ap()

    dbg_shapes = {
        "xbcraw": ([128, NCT * 516], BF16), "dtTg": ([128, GTOK], F32),
        "sgate": ([128, NCH * DIN], BF16), "xbcact": ([128, NCT * 512], BF16),
        "yf": ([128, DIN], BF16), "ynT": ([128, NXT * GTOK], BF16),
        "S": ([128, DIN], F32), "x1": ([128, NCH * D], BF16),
        "hT": ([128, 44 * GTOK], BF16), "ffns": ([128, NCH * D], F32),
        "ln0T": ([128, 16 * GTOK], BF16), "xact": ([128, DIN], BF16),
        "attns": ([128, NCH * D], F32), "sm64": ([128, 64], F32), "sm64b": ([128, 64], BF16),
        "sm128": ([128, 128], F32), "mqd": ([128, 1024], BF16), "expd": ([128, 1024], F32),
    }
    dbg = {}
    for tag, gi in taps:
        sh, dt = dbg_shapes[tag]
        dbg[(tag, gi)] = nc.dram_tensor(f"dbg_{tag}_{gi}", sh, dt, kind="ExternalOutput").ap()

    P = Prog()
    op, wait, C = P.op, P.wait, P.C

    with ExitStack() as ctx:
        e = ctx.enter_context

        def sb(name, shape, dt):
            return e(nc.sbuf_tensor(name, shape, dt))

        r1 = sb("r1", [128, 16 * GTOK], BF16)
        ln0T = r1[:, :].rearrange("p (k t) -> p k t", t=GTOK)
        x_act = r1[:, 0:DIN]
        xs_t = r1[:, DIN:2 * DIN]

        R23B = 32768 + 36928
        r23 = sb("r23", [128, R23B], mybir.dt.uint8)
        r23b, r23f = r23.bitcast(BF16), r23.bitcast(F32)
        sgate = r23b[:, 0:NCH * DIN].rearrange("p (j o) -> p j o", o=DIN)
        attn_stage = r23f[:, 0:NCH * D].rearrange("p (j o) -> p j o", o=D)
        hT = r23b[:, 0:44 * GTOK].rearrange("p (k t) -> p k t", t=GTOK)
        xbcraw = r23b[:, 16384:16384 + NCT * 516].rearrange("p (c t) -> p c t", t=516)
        TB = 32768
        expM = r23f[:, TB // 4:TB // 4 + 1024]
        mq = [r23b[:, (TB + 4096 + i * 2048) // 2:(TB + 4096 + i * 2048) // 2 + 1024] for i in range(2)]
        tqf = r23f[:, (TB + 8192) // 4:(TB + 8192) // 4 + 1024]
        tqb = r23b[:, (TB + 8192) // 2:(TB + 8192) // 2 + 1024]
        yf = r23b[:, (TB + 12288) // 2:(TB + 12288) // 2 + DIN]
        sftmp = r23f[:, (TB + 20480) // 4:(TB + 20480) // 4 + 1024]
        outst = r23f[:, (TB + 24576) // 4:(TB + 24576) // 4 + D]
        lnchunk = r23b[:, (TB + 32768) // 2:(TB + 32768) // 2 + D]
        AG = r23f[:, (TB + 32768) // 4:(TB + 32768) // 4 + GTOK]
        BG = r23f[:, (TB + 32768 + 2048) // 4:(TB + 32768 + 2048) // 4 + GTOK]
        assert TB + 32768 + 2 * D <= R23B

        r4 = sb("r4", [128, NCT * 512], BF16)
        xbcact = r4[:, :].rearrange("p (c t) -> p c t", t=512)
        ynT = r4[:, 0:NXT * GTOK].rearrange("p (k t) -> p k t", t=GTOK)
        ffn_stage = r4.bitcast(F32)[:, 0:NCH * D].rearrange("p (j o) -> p j o", o=D)

        xg_flat = sb("xg", [128, NCH * D], BF16)[:, :]
        xg = xg_flat.rearrange("p (j o) -> p j o", o=D)
        S_f = sb("S_f", [128, DIN], F32)[:, :]
        S_b = sb("S_b", [128, DIN], BF16)[:, :]
        dtTg = sb("dtTg", [128, GTOK], F32)[:, :]
        acsb = sb("acsb", [128, 2048], F32)

        statW = [sb(f"statW{i}", [128, 16 * 128], BF16) for i in range(2)]
        movW = [sb(f"movW{i}", [128, 4 * 512], BF16) for i in range(2)]
        statWv = [w[:, :].rearrange("p (k o) -> p k o", o=128) for w in statW]
        movWv = [w[:, :].rearrange("p (k o) -> p k o", o=512) for w in movW]
        movW_as_stat = [w[:, 0:2048].rearrange("p (k o) -> p k o", o=128) for w in movW]
        statW_as_mov = [w[:, :].rearrange("p (k o) -> p k o", o=512) for w in statW]
        statW8f = [w.bitcast(FP8)[:, 0:2048] for w in statW]
        movW8f = [w.bitcast(FP8)[:, 0:2048] for w in movW]
        statW8_stat = [v.rearrange("p (k o) -> p k o", o=128) for v in statW8f]
        movW8_stat = [v.rearrange("p (k o) -> p k o", o=128) for v in movW8f]
        movW8_mov = [v.rearrange("p (k o) -> p k o", o=512) for v in movW8f]

        idf = sb("idf_c", [64, 64], F32)[:, :]
        idb = sb("idb_c", [128, 128], BF16)[:, :]
        mask01 = sb("mask_c", [128, 128], BF16)[:, :]
        convw = sb("convw_c", [128, NCT * 4], F32)[:, :].rearrange("p (c k) -> p c k", k=4)
        convb = sb("convb_c", [128, NCT], F32)[:, :]
        dtbias = sb("dtbias_c", [128, 1], F32)[:, :]
        alog = sb("alog_c", [128, 1], F32)[:, :]
        negA = sb("negA", [128, 1], F32)[:, :]
        gscale = sb("gscale", [128, 44], F32)[:, :]
        uscale = sb("uscale", [128, 44], F32)[:, :]
        D_b = sb("D_b", [128, 64], BF16)[:, :]
        normw_b = sb("normw_b", [128, DIN], BF16)[:, :]
        dsb = sb("dsb", [128, D], F32)[:, :]
        acsb_bf = acsb.bitcast(BF16)
        K2BASE = (0, 32, 0, 32)
        K2COL = (0, 0, 1280, 1280)
        K2SEM = ("fqa", "fqb", "fqc", "fqd")
        dtTg_bf = dtTg.bitcast(BF16)
        AGhi_g = dtTg_bf[0:64, 0:512]
        AGlo_g = dtTg_bf[0:64, 512:1024]
        BGhi_g = acsb_bf[0:64, 2560:3072]
        BGlo_g = acsb_bf[0:64, 3072:3584]
        onesrow = acsb[64:65, 0:128]
        zeros64 = acsb[64:128, 256:384]

        dtg4 = sb("dtg4", [128, GTOK], F32)[:, :]
        dtg5 = sb("dtg5", [128, GTOK], F32)[:, :]
        dsdt_b = sb("dsdt_b", [128, 64], BF16)[:, :]
        e1 = sb("e1c", [128, 64], F32)[:, :]
        gam = sb("gam", [128, 64], F32)[:, :]
        Gm = sb("Gm", [128, 128], F32)[:, :]
        B_sb = sb("B_sb", [128, 128], BF16)[:, :]
        stats = sb("stats", [128, 16], F32)[:, :]
        halo_sb = sb("halo_sb", [128, NCT * 3], BF16)[:, :].rearrange("p (c t) -> p c t", t=3)
        gmr = acsb[64:65, 0:192]
        c30 = sb("c30", [128, 1], F32)[:, :]
        sgtmp = r23b[:, (TB + 12288) // 2:(TB + 12288) // 2 + 512]
        sgtmp2 = r23b[:, (TB + 13312) // 2:(TB + 13312) // 2 + 512]
        convacc = None

        ps_tr = e(nc.psum_tensor("ps_tr", [128, 1024], BF16))[:, :]
        ps_a = e(nc.psum_tensor("ps_a", [128, 512], F32))[:, :]
        ps_b = e(nc.psum_tensor("ps_b", [128, 512], F32))[:, :]
        ps_q = e(nc.psum_tensor("ps_q", [128, 2048], F32))[:, :]
        ps_x = e(nc.psum_tensor("ps_x", [128, 512], F32))[:, :]
        convacc = [ps_tr.bitcast(F32)[:, 0:512], ps_x]

        for s_ in ("ld", "l2", "wq", "wqa", "wqb", "xq", "fqa", "fqb", "fqc", "fqd",
                   "gq", "t", "v", "s", "g", "y", "st", "wsa", "wsb", "wsc", "wsd"):
            P.sems[s_] = e(nc.semaphore(s_))

        def dma(queue, dst, src, sem):
            op(queue, lambda e, d=dst, s2=src: e.dma_start(out=d, in_=s2), inc=sem, inc_n=16)

        def ln_stats(src, W, nrm_out):
            wait("scalar", "v")
            wait("scalar", "t")
            wait("vector", "s")
            op("scalar", lambda e, src=src: e.activation(nrm_out, src, AF.Square,
                                                         accum_out=stats[:, 0:1]), inc="s")
            op("vector", lambda e, src=src: e.reduce_sum(stats[:, 1:2], src, AX.X), inc="v")
            op("vector", lambda e: e.tensor_scalar(stats[:, 2:3], stats[:, 1:2], 1.0 / W, None, ALU.mult), inc="v")
            op("vector", lambda e: e.tensor_tensor(stats[:, 3:4], stats[:, 2:3], stats[:, 2:3], ALU.mult), inc="v")
            wait("vector", "s")
            op("vector", lambda e: e.tensor_scalar(stats[:, 4:5], stats[:, 0:1], 1.0 / W, None, ALU.mult), inc="v")
            op("vector", lambda e: e.tensor_tensor(stats[:, 4:5], stats[:, 4:5], stats[:, 3:4], ALU.subtract), inc="v")
            op("vector", lambda e: e.tensor_scalar(stats[:, 4:5], stats[:, 4:5], EPS, None, ALU.add), inc="v")
            wait("scalar", "v")
            op("scalar", lambda e: e.activation(stats[:, 5:6], stats[:, 4:5], AF.Sqrt), inc="s")
            wait("vector", "s")
            op("vector", lambda e: e.reciprocal(stats[:, 6:7], stats[:, 5:6]), inc="v")
            op("vector", lambda e: e.tensor_tensor(stats[:, 7:8], stats[:, 2:3], stats[:, 6:7], ALU.mult), inc="v")
            op("vector", lambda e: e.tensor_scalar(stats[:, 7:8], stats[:, 7:8], -1.0, None, ALU.mult), inc="v")
            op("vector", lambda e, src=src: e.tensor_scalar(
                nrm_out, src, stats[:, 6:7], stats[:, 7:8], ALU.mult, ALU.add), inc="v")

        def transpose_batch(src_tiles, dst_fn):
            i, n = 0, len(src_tiles)
            while i < n:
                m = min(8, n - i)
                wait("tensor", "v")
                for k in range(m):
                    op("tensor", lambda e, s2=src_tiles[i + k], k=k:
                       e.transpose(ps_tr[:, k * 128:(k + 1) * 128], s2, idb), inc="t")
                wait("vector", "t")
                op("vector", lambda e, i0=i, m=m: e.tensor_copy(
                    dst_fn(i0, m), ps_tr[:, 0:m * 128].rearrange("p (n t) -> p n t", t=128)), inc="v")
                i += m

        def tap(tag, gi, src_ap):
            if (tag, gi) in dbg:
                P.barrier()
                dma("sync", dbg[(tag, gi)], src_ap, "st")
                wait("sync", "st")
                P.barrier()

        for dst, src in ((idf, idf_in),
                         (convw, convw_in.rearrange("p (c k) -> p c k", k=4)),
                         (convb, convb_in), (dtbias[0:64, :], dtb_in), (alog[0:64, :], alog_in),
                         (gscale, gs_in), (uscale, us_in)):
            dma("sync", dst, src, "ld")
        dma("gpsimd", idb, idb_in, "wq")
        dma("gpsimd", mask01, mask_in, "wq")
        wait("vector", "ld")
        for z in (r1[:, :], r23.bitcast(F32)[:, 0:R23B // 4], r4[:, :],
                  acsb[:, :], dtTg, S_f, S_b, stats, dtg4, dtg5,
                  dsdt_b, e1, gam, Gm, B_sb, xg_flat):
            op("vector", lambda e, z=z: e.memset(z, 0.0), inc="v")
        op("vector", lambda e: e.memset(onesrow, 1.0), inc="v")
        op("vector", lambda e: e.memset(c30, 30.0), inc="v")
        wait("scalar", "ld")
        op("scalar", lambda e: e.activation(negA[0:64, :], alog[0:64, :], AF.Exp), inc="s")
        wait("vector", "s")
        op("vector", lambda e: e.tensor_scalar(negA[0:64, :], negA[0:64, :], -1.0, None, ALU.mult), inc="v")
        stages = []
        for i in range(4):
            stages.append((nwrow_in[0:1, i * 1024:(i + 1) * 1024], 1024,
                           (lambda i=i: normw_b[:, i * 1024:(i + 1) * 1024])))
        for i in range(2):
            stages.append((dsrow_in[0:1, i * 1024:(i + 1) * 1024], 1024,
                           (lambda i=i: dsb[:, i * 1024:(i + 1) * 1024])))
        stages.append((drow_in, 64, (lambda: D_b)))
        datarow = acsb[64:65, 128:128 + 1024]
        wait("sync", "v")
        for src, W, dstf in stages:
            wait("sync", "t")
            dma("sync", datarow[0:1, 0:W], src, "l2")
            wait("tensor", "l2")
            wait("tensor", "v")
            nmm = (W + 511) // 512
            for i in range(nmm):
                w2 = min(512, W - i * 512)
                op("tensor", lambda e, i=i, w2=w2: e.matmul(
                    ps_q[:, i * 512:i * 512 + w2], onesrow,
                    datarow[0:1, i * 512:i * 512 + w2], start=True, stop=True), inc="t")
            wait("vector", "t")
            op("vector", lambda e, dstf=dstf, W=W: e.tensor_copy(dstf(), ps_q[:, 0:W]), inc="v")
        wait("vector", "t")
        wait("sync", "t")
        for k2base, k2c0 in ((0, 0), (32, 0), (0, 1280), (32, 1280)):
            dma("sync", acsb_bf[k2base:k2base + 10, k2c0:k2c0 + 1280], k2c_in, "l2")
            wait("sync", "l2")
        wait("vector", "l2")

        op("vector", lambda e: e.memset(zeros64, 0.0), inc="v")
        wait("tensor", "wq")
        P.barrier()

        K2T = {}

        def scan_chunk(j):
            js = slice(j * 128, (j + 1) * 128)
            jlast = j * 128 + 127

            op("vector", lambda e: e.tensor_scalar(dtg5[0:64, js], AG[0:64, js],
                                                   AG[0:64, jlast:jlast + 1], None, ALU.subtract), inc="v")
            wait("scalar", "v")
            op("scalar", lambda e: e.activation(dtg5[0:64, js], dtg5[0:64, js], AF.Exp, scale=-1.0), inc="s")
            wait("vector", "s")
            op("vector", lambda e: e.tensor_tensor(dtg5[0:64, js], dtg5[0:64, js],
                                                   dtg4[0:64, js], ALU.mult), inc="v")
            v_dsdt = C["v"]

            wait("tensor", "v", v_dsdt)
            wait("tensor", "s")
            op("tensor", lambda e: e.transpose(ps_x[:, 0:64], AG[0:64, js], idf), inc="t")
            op("tensor", lambda e: e.transpose(ps_x[:, 64:128], dtg5[0:64, js], idf), inc="t")
            t_tr = C["t"]
            wait("scalar", "t")
            dma("scalar", acsb[64:65, 128:192], AG[0:64, jlast:jlast + 1], "gq")
            fq_gam = C["gq"]
            wait("vector", "t", t_tr)
            op("vector", lambda e: e.tensor_copy(dsdt_b, ps_x[:, 64:128]), inc="v")
            v_tr = C["v"]
            wait("scalar", "t", t_tr)
            op("scalar", lambda e: e.activation(e1, ps_x[:, 0:64], AF.Exp), inc="s")
            s_e1 = C["s"]
            wait("tensor", "gq", fq_gam)
            wait("tensor", "v", v_tr)
            wait("tensor", "s", s_e1)
            op("tensor", lambda e: e.matmul(ps_x[:, 192:256], onesrow, acsb[64:65, 128:192], start=True, stop=True), inc="t")
            t_gam = C["t"]
            wait("scalar", "t", t_gam)
            op("scalar", lambda e: e.activation(gam, ps_x[:, 192:256], AF.Exp), inc="s")
            s_gam = C["s"]

            transpose_batch([xbcact[:, d, js] for d in range(NXT)],
                            lambda i0, m: x_act.rearrange("p (n t) -> p n t", t=128)[:, i0:i0 + m, :])
            wait("tensor", "v")
            op("tensor", lambda e: e.transpose(ps_tr[:, 0:128], xbcact[:, 32, js], idb), inc="t")
            wait("vector", "t")
            op("vector", lambda e: e.tensor_copy(B_sb, ps_tr[:, 0:128]), inc="v")
            op("tensor", lambda e: e.matmul(ps_x[:, 256:384], xbcact[:, 32, js], xbcact[:, 33, js],
                                            start=True, stop=True), inc="t")
            t_g = C["t"]
            wait("vector", "t", t_g)
            op("vector", lambda e: e.scalar_tensor_tensor(Gm, ps_x[:, 256:384], 1.0, mask01,
                                                          ALU.mult, ALU.mult), inc="v")
            op("vector", lambda e: e.tensor_tensor(
                xs_t.rearrange("p (h d) -> p h d", d=64), x_act.rearrange("p (h d) -> p h d", d=64),
                dsdt_b.broadcast_to((128, 64, 64)), ALU.mult), inc="v")
            v_gm = C["v"]

            k2_hist = {}
            v_mq_hist = {}

            def emit_flatten(og):
                oct_id = j * 8 + og
                st_ = oct_id % 4
                base, c0, fsem = K2BASE[st_], K2COL[st_], K2SEM[st_]
                wait("sync", "t", K2T.get(oct_id - 4, 0))
                for q in range(2):
                    hq = og * 8 + q * 4
                    dma("sync", acsb_bf[base:base + 4, c0 + q * 128:c0 + (q + 1) * 128],
                        BGhi_g[hq:hq + 4, js], fsem)
                    dma("sync", acsb_bf[base + 4:base + 8, c0 + q * 128:c0 + (q + 1) * 128],
                        BGlo_g[hq:hq + 4, js], fsem)
                    dma("sync", acsb_bf[base + 8:base + 9, c0 + 256 + q * 512:c0 + 768 + q * 512]
                        .rearrange("p (h t) -> p h t", t=128), AGhi_g[hq:hq + 4, js], fsem)
                    dma("sync", acsb_bf[base + 9:base + 10, c0 + 256 + q * 512:c0 + 768 + q * 512]
                        .rearrange("p (h t) -> p h t", t=128), AGlo_g[hq:hq + 4, js], fsem)
                return C[fsem]

            def emit_k2(og, fq_val):
                oct_id = j * 8 + og
                st_ = oct_id % 4
                base, c0, fsem = K2BASE[st_], K2COL[st_], K2SEM[st_]
                wait("tensor", fsem, fq_val)
                wait("tensor", "s")
                for q in range(2):
                    psd = ps_a if q == 0 else ps_b
                    op("tensor", lambda e, psd=psd, q=q, base=base, c0=c0: e.matmul(
                        psd[:, :], acsb_bf[base:base + 10, c0 + q * 128:c0 + (q + 1) * 128],
                        acsb_bf[base:base + 10, c0 + 256 + q * 512:c0 + 768 + q * 512],
                        start=True, stop=True), inc="t")
                K2T[oct_id] = C["t"]
                k2_hist[og] = C["t"]

            def emit_exps_mq(og):
                oc = og % 2
                wait("scalar", "t", k2_hist[og])
                wait("scalar", "v")
                op("scalar", lambda e: e.activation(ps_a, ps_a, AF.Relu, scale=-1.0, bias=c30), inc="s")
                op("scalar", lambda e: e.activation(ps_b, ps_b, AF.Relu, scale=-1.0, bias=c30), inc="s")
                op("scalar", lambda e: e.activation(expM[:, 0:512], ps_a, AF.Exp, scale=-1.0, bias=c30), inc="s")
                op("scalar", lambda e: e.activation(expM[:, 512:1024], ps_b, AF.Exp, scale=-1.0, bias=c30), inc="s")
                wait("vector", "s")
                wait("vector", "t")
                op("vector", lambda e, oc=oc: e.tensor_tensor(
                    mq[oc][0:128, 0:1024].rearrange("p (h t) -> p h t", t=128),
                    expM.rearrange("p (h t) -> p h t", t=128),
                    Gm.broadcast_to((128, 128, 8)).rearrange("p t h -> p h t"),
                    ALU.mult), inc="v")
                v_mq_hist[og] = C["v"]

            vstate = {"yf": 0}

            def emit_ydiag(og):
                oc = og % 2
                wait("tensor", "v", max(v_mq_hist[og], vstate["yf"]))
                for hh in range(8):
                    h = og * 8 + hh
                    slot = oc * 512 + hh * 64
                    op("tensor", lambda e, oc=oc, hh=hh, h=h, slot=slot: e.matmul(
                        ps_q[:, slot:slot + 64], mq[oc][:, hh * 128:(hh + 1) * 128],
                        x_act[:, h * 64:(h + 1) * 64], start=True, stop=True), inc="t")

            fqv = {}
            for og in range(3):
                fqv[og] = emit_flatten(og)
            emit_k2(0, fqv[0])
            emit_exps_mq(0)
            for og in range(1, 8):
                if og + 2 <= 7:
                    fqv[og + 2] = emit_flatten(og + 2)
                emit_k2(og, fqv[og])
                emit_ydiag(og - 1)
                emit_exps_mq(og)
                if og % 2 == 0:
                    quarter_tail(j, og // 2 - 1, s_gam, vstate)
            emit_ydiag(7)
            quarter_tail(j, 3, s_gam, vstate)
            chunk_norm(j)
            return

        def dt_batch():
            dtraw = dtTg[0:64, :]
            for s_ in ("fqa", "fqb", "fqc", "fqd", "gq"):
                wait("scalar", s_)
                wait("vector", s_)
            wait("vector", "t")
            op("vector", lambda e: e.tensor_scalar(dtg4[0:64, :], dtraw, dtbias[0:64, :],
                                                   None, ALU.add), inc="v")
            wait("scalar", "v")
            op("scalar", lambda e: e.activation(dtg5[0:64, :], dtg4[0:64, :], AF.Abs), inc="s")
            op("scalar", lambda e: e.activation(dtg5[0:64, :], dtg5[0:64, :], AF.Exp, scale=-1.0), inc="s")
            op("scalar", lambda e: e.activation(dtg5[0:64, :], dtg5[0:64, :], AF.Ln, bias=1.0), inc="s")
            wait("vector", "s")
            op("vector", lambda e: e.scalar_tensor_tensor(dtg4[0:64, :], dtg4[0:64, :], 0.0,
                                                          dtg5[0:64, :], ALU.max, ALU.add), inc="v")
            wait("scalar", "v")
            op("scalar", lambda e: e.activation(dtg5[0:64, :], dtg4[0:64, :], AF.Ln), inc="s")
            wait("vector", "s")
            op("vector", lambda e: e.tensor_scalar(BG[0:64, :], dtg4[0:64, :], negA[0:64, :],
                                                   None, ALU.mult), inc="v")
            op("vector", lambda e: e.memset(Gm[0:64, :], 0.0), inc="v")
            for jj in range(NCH):
                op("vector", lambda e, jj=jj: e.tensor_tensor_scan(
                    AG[0:64, jj * 128:(jj + 1) * 128], BG[0:64, jj * 128:(jj + 1) * 128],
                    Gm[0:64, :], 0.0, ALU.add, ALU.add), inc="v")
            op("vector", lambda e: e.tensor_tensor(BG[0:64, :], dtg5[0:64, :], AG[0:64, :],
                                                   ALU.subtract), inc="v")
            op("vector", lambda e: e.tensor_copy(AGhi_g, AG[0:64, :]), inc="v")
            op("vector", lambda e: e.tensor_tensor(AGlo_g, AG[0:64, :], AGhi_g, ALU.subtract), inc="v")
            op("vector", lambda e: e.tensor_copy(BGhi_g, BG[0:64, :]), inc="v")
            op("vector", lambda e: e.tensor_tensor(BGlo_g, BG[0:64, :], BGhi_g, ALU.subtract), inc="v")

        def chunk_norm(j):
            op("vector", lambda e: e.scalar_tensor_tensor(x_act, yf, 1.0, yf, ALU.mult, ALU.mult,
                                                          accum_out=stats[:, 8:9]), inc="v")
            op("vector", lambda e: e.tensor_scalar(stats[:, 9:10], stats[:, 8:9], 1.0 / DIN, EPS,
                                                   ALU.mult, ALU.add), inc="v")
            v_ss = C["v"]
            wait("scalar", "v", v_ss)
            op("scalar", lambda e: e.activation(stats[:, 10:11], stats[:, 9:10], AF.Ln), inc="s")
            op("scalar", lambda e: e.activation(stats[:, 11:12], stats[:, 10:11], AF.Exp, scale=-0.5), inc="s")
            s_rstd = C["s"]
            wait("vector", "s", s_rstd)
            op("vector", lambda e: e.scalar_tensor_tensor(xs_t, yf, stats[:, 11:12], normw_b,
                                                          ALU.mult, ALU.mult), inc="v")
            transpose_batch([xs_t[:, d * 128:(d + 1) * 128] for d in range(NXT)],
                            lambda i0, m, j=j: ynT[:, i0:i0 + m, j * 128:(j + 1) * 128])

        def quarter_tail(j, q, s_gam, vstate):
            js = slice(j * 128, (j + 1) * 128)
            qs = slice(q * 1024, (q + 1) * 1024)
            for half in range(2):
                op("tensor", lambda e, q=q, half=half: e.matmul(
                    ps_q[:, 1024 + half * 512:1536 + half * 512], xbcact[:, 33, js],
                    S_b[:, q * 1024 + half * 512:q * 1024 + (half + 1) * 512],
                    start=True, stop=True), inc="t")
            t_yoff = C["t"]
            wait("vector", "t", t_yoff)
            op("vector", lambda e, q=q: e.tensor_tensor(
                tqf.rearrange("p (h d) -> p h d", d=64),
                ps_q[:, 1024:2048].rearrange("p (h d) -> p h d", d=64),
                e1[:, q * 16:(q + 1) * 16].broadcast_to((128, 16, 64)), ALU.mult), inc="v")
            v_a1 = C["v"]
            op("vector", lambda e: e.tensor_tensor(ps_q[:, 0:1024], ps_q[:, 0:1024], tqf, ALU.add), inc="v")
            op("vector", lambda e, q=q: e.tensor_tensor(
                tqf.rearrange("p (h d) -> p h d", d=64),
                x_act[:, q * 1024:(q + 1) * 1024].rearrange("p (h d) -> p h d", d=64),
                D_b[:, q * 16:(q + 1) * 16].broadcast_to((128, 16, 64)), ALU.mult), inc="v")
            op("vector", lambda e: e.tensor_tensor(ps_q[:, 0:1024], ps_q[:, 0:1024],
                                                   tqf, ALU.add), inc="v")
            op("vector", lambda e, j=j, qs=qs: e.scalar_tensor_tensor(
                yf[:, qs], ps_q[:, 0:1024], 1.0, sgate[:, j, qs], ALU.mult, ALU.mult), inc="v")
            vstate["yf"] = C["v"]
            wait("tensor", "v", v_a1)
            for half in range(2):
                op("tensor", lambda e, q=q, half=half: e.matmul(
                    ps_q[:, 1024 + half * 512:1536 + half * 512], B_sb,
                    xs_t[:, q * 1024 + half * 512:q * 1024 + (half + 1) * 512],
                    start=True, stop=True), inc="t")
            t_st = C["t"]
            wait("vector", "t", t_st)
            wait("vector", "s", s_gam)
            op("vector", lambda e, q=q, qs=qs: e.tensor_tensor(
                sftmp.rearrange("p (h d) -> p h d", d=64),
                S_f[:, qs].rearrange("p (h d) -> p h d", d=64),
                gam[:, q * 16:(q + 1) * 16].broadcast_to((128, 16, 64)), ALU.mult), inc="v")
            op("vector", lambda e, qs=qs: e.tensor_tensor(S_f[:, qs], sftmp, ps_q[:, 1024:2048],
                                                          ALU.add), inc="v")
            op("vector", lambda e, qs=qs: e.tensor_copy(S_b[:, qs], S_f[:, qs]), inc="v")


        sid_map = {}
        store_pending = {}
        WSEM = {("stat", 0): "wsa", ("stat", 1): "wsb", ("mov", 0): "wsc", ("mov", 1): "wsd"}
        BUF_FLAT = {("stat", 0): statW[0][:, :], ("stat", 1): statW[1][:, :],
                    ("mov", 0): movW[0][:, :], ("mov", 1): movW[1][:, :]}
        BUF_FLAT8 = {("stat", 0): statW8f[0], ("stat", 1): statW8f[1],
                     ("mov", 0): movW8f[0], ("mov", 1): movW8f[1]}
        sid8_map = {}

        def war_wait(buf_id):
            if buf_id in store_pending:
                sem_, val_ = store_pending.pop(buf_id)
                wait("gpsimd", sem_, val_)

        def emit_store(key, buf_id, qsem):
            sid = sid_map.setdefault(key, len(sid_map))
            ssem = WSEM[buf_id]
            wait("sync", qsem, C[qsem])
            dma("sync", wbf[sid], BUF_FLAT[buf_id], ssem)
            store_pending[buf_id] = (ssem, C[ssem])

        t_slab = {}
        for g in range(n_groups):
            seq, grp = divmod(g, NGRP)
            t0 = grp * GTOK
            if g:
                wait("vector", "st", C["st"])
                wait("scalar", "st", C["st"])
            if grp == 0:
                op("vector", lambda e: e.memset(S_f, 0.0), inc="v")
                op("vector", lambda e: e.memset(S_b, 0.0), inc="v")
                op("vector", lambda e: e.memset(halo_sb, 0.0), inc="v")

            for j in range(NCH):
                dma("gpsimd", xg[:, j, :], x_in[seq, t0 + j * CHUNK:t0 + (j + 1) * CHUNK, :], "xq")
            ld_x = C["xq"]
            wait("scalar", "xq", ld_x)
            wait("vector", "xq", ld_x)
            for j in range(NCH):
                ln_stats(xg[:, j, :], D, lnchunk)
                wait("tensor", "s")
                transpose_batch([lnchunk[:, k * 128:(k + 1) * 128] for k in range(16)],
                                lambda i0, m, j=j: ln0T[:, i0:i0 + m, j * 128:(j + 1) * 128])
            tap("ln0T", g, r1[:, :])

            v_ln0 = C["v"]
            evac_cnt = {0: 0, 1: 0}
            for ot in range(NCT + 1):
                buf = ot % 2
                width = 128 if ot < NCT else 64
                c0 = 4096 + ot * 128
                wait("gpsimd", "t", t_slab.get(("p1a", g, ot - 2), 0))
                wsem = "wqa" if buf == 0 else "wqb"
                if g == 0:
                    war_wait(("stat", buf))
                    dma("gpsimd", statWv[buf][:, :, 0:width],
                        wipT[:, c0:c0 + width].rearrange("(k p) o -> p k o", p=128), wsem)
                    emit_store(("p1a", ot), ("stat", buf), wsem)
                else:
                    war_wait(("stat", buf))
                    dma("gpsimd", statW[buf][:, :], wbf[sid_map[("p1a", ot)]], wsem)
                wait("tensor", wsem, C[wsem])
                wait("tensor", "v", max(v_ln0, evac_cnt[buf]))
                psd = ps_a if buf == 0 else ps_b
                for k in range(16):
                    op("tensor", lambda e, k=k, buf=buf, w=width, psd=psd: e.matmul(
                        psd[0:w, :], statWv[buf][:, k, 0:w], ln0T[:, k, :],
                        start=(k == 0), stop=(k == 15)), inc="t")
                t_slab[("p1a", g, ot)] = C["t"]
                wait("vector", "t", C["t"])
                if ot < NCT:
                    op("vector", lambda e, ot=ot, psd=psd: e.tensor_copy(xbcraw[:, ot, 4:516], psd), inc="v")
                else:
                    op("vector", lambda e, psd=psd: e.tensor_copy(dtTg[0:64, :], psd[0:64, :]), inc="v")
                evac_cnt[buf] = C["v"]
            tap("xbcraw", g, r23b[:, 16384:16384 + NCT * 516])
            tap("dtTg", g, dtTg)

            op("vector", lambda e: e.tensor_copy(xbcraw[:, :, 1:4], halo_sb), inc="v")
            op("vector", lambda e: e.tensor_copy(halo_sb, xbcraw[:, :, 513:516]), inc="v")

            def conv_ct(ct):
                wait("vector", "s")
                op("vector", lambda e, ct=ct: e.tensor_scalar(
                    convacc[0], xbcraw[:, ct, 1:513], convw[:, ct, 0:1], None, ALU.mult), inc="v")
                op("vector", lambda e, ct=ct: e.scalar_tensor_tensor(
                    convacc[1], xbcraw[:, ct, 2:514], convw[:, ct, 1:2], convacc[0],
                    ALU.mult, ALU.add), inc="v")
                op("vector", lambda e, ct=ct: e.scalar_tensor_tensor(
                    convacc[0], xbcraw[:, ct, 3:515], convw[:, ct, 2:3], convacc[1],
                    ALU.mult, ALU.add), inc="v")
                op("vector", lambda e, ct=ct: e.scalar_tensor_tensor(
                    convacc[1], xbcraw[:, ct, 4:516], convw[:, ct, 3:4], convacc[0],
                    ALU.mult, ALU.add), inc="v")
                wait("scalar", "v")
                op("scalar", lambda e, ct=ct: e.activation(
                    xbcact[:, ct, :], convacc[1], AF.Silu, bias=convb[:, ct:ct + 1]), inc="s")

            conv_plan = [[31, 32, 33, 0]] + [list(range(4 * o - 3, 4 * o + 1)) for o in range(1, 8)]
            conv_plan[7] = list(range(25, 31))
            s_gate_evac = 0
            for ot in range(8):
                for ks in range(4):
                    buf = ks % 2
                    wait("gpsimd", "t", t_slab.get(("p1b", ot, ks - 2),
                                                   t_slab.get(("p1b", ot - 1, ks + 2),
                                                              t_slab.get(("p1a", g, NCT - ks), 0))))
                    wsem = "wqa" if buf == 0 else "wqb"
                    if g == 0:
                        war_wait(("mov", buf))
                        dma("gpsimd", movWv[buf],
                            wipT[ks * 512:(ks + 1) * 512, ot * 512:(ot + 1) * 512]
                            .rearrange("(k p) o -> p k o", p=128), wsem)
                        emit_store(("p1b", ot, ks), ("mov", buf), wsem)
                    else:
                        war_wait(("mov", buf))
                        dma("gpsimd", movW[buf][:, :], wbf[sid_map[("p1b", ot, ks)]], wsem)
                    wait("tensor", wsem, C[wsem])
                    if ks == 0 and s_gate_evac:
                        wait("tensor", "s", s_gate_evac)
                    for kk in range(4):
                        k = ks * 4 + kk
                        for j in range(NCH):
                            op("tensor", lambda e, k=k, kk=kk, j=j, buf=buf: e.matmul(
                                ps_q[:, j * 512:(j + 1) * 512],
                                ln0T[:, k, j * 128:(j + 1) * 128], movWv[buf][:, kk, :],
                                start=(k == 0), stop=(k == 15)), inc="t")
                    t_slab[("p1b", ot, ks)] = C["t"]
                for ct in conv_plan[ot]:
                    conv_ct(ct)
                if ot == 0:
                    dt_batch()
                wait("scalar", "t", C["t"])
                for j in range(NCH):
                    op("scalar", lambda e, ot=ot, j=j: e.activation(
                        sgate[:, j, ot * 512:(ot + 1) * 512], ps_q[:, j * 512:(j + 1) * 512],
                        AF.Silu), inc="s")
                s_gate_evac = C["s"]
            P.barrier()
            tap("sgate", g, r23b[:, 0:NCH * DIN])
            tap("xbcact", g, r4[:, :])
            tap("dtTg", 50 + g, AG)
            tap("dtTg", 60 + g, BG)
            tap("dtTg", 70 + g, dtg4)

            for j in range(NCH):
                scan_chunk(j)
                if any(k[1] == g * 10 + j for k in dbg):
                    P.barrier()
                tap("yf", g * 10 + j, yf)
                tap("xact", g * 10 + j, x_act)
                tap("S", g * 10 + j, S_f)
                tap("sm64", g * 10 + j, e1)
                tap("sm64", 100 + g * 10 + j, gam)
                tap("sm128", g * 10 + j, Gm)
                tap("mqd", g * 10 + j, mq[1])
                tap("expd", g * 10 + j, expM)
                tap("sm64b", 200 + g * 10 + j, dsdt_b)
                tap("xact", 300 + g * 10 + j, xs_t)
            tap("ynT", g, r4[:, 0:NXT * GTOK])

            v_ynT = C["v"]
            s_p3 = 0
            for ot in range(4):
                for ks in range(8):
                    buf = ks % 2
                    wait("gpsimd", "t", t_slab.get(("p3", ot, ks - 2), t_slab.get(("p3", ot - 1, ks + 6), 0)))
                    wsem = "wqa" if buf == 0 else "wqb"
                    if g == 0:
                        war_wait(("mov", buf))
                        dma("gpsimd", movWv[buf],
                            wopT[ks * 512:(ks + 1) * 512, ot * 512:(ot + 1) * 512]
                            .rearrange("(k p) o -> p k o", p=128), wsem)
                        emit_store(("p3", ot, ks), ("mov", buf), wsem)
                    else:
                        war_wait(("mov", buf))
                        dma("gpsimd", movW[buf][:, :], wbf[sid_map[("p3", ot, ks)]], wsem)
                    wait("tensor", wsem, C[wsem])
                    if ks == 0:
                        wait("tensor", "v", v_ynT)
                        if s_p3:
                            wait("tensor", "s", s_p3)
                    for kk in range(4):
                        k = ks * 4 + kk
                        for j in range(NCH):
                            op("tensor", lambda e, k=k, kk=kk, j=j, buf=buf: e.matmul(
                                ps_q[:, j * 512:(j + 1) * 512],
                                ynT[:, k, j * 128:(j + 1) * 128], movWv[buf][:, kk, :],
                                start=(k == 0), stop=(k == 31)), inc="t")
                    t_slab[("p3", ot, ks)] = C["t"]
                wait("scalar", "t", C["t"])
                for j in range(NCH):
                    op("scalar", lambda e, ot=ot, j=j: e.copy(
                        attn_stage[:, j, ot * 512:(ot + 1) * 512], ps_q[:, j * 512:(j + 1) * 512]), inc="s")
                s_p3 = C["s"]
            tap("attns", g, r23f[:, 0:NCH * D])
            for j in range(NCH):
                wait("scalar", "s", s_p3)
                wait("vector", "s", s_p3)
                ln_stats(attn_stage[:, j, :], D, lnchunk)
                wait("vector", "s")
                op("vector", lambda e, j=j: e.tensor_tensor(xg[:, j, :], xg[:, j, :], lnchunk, ALU.add), inc="v")
            tap("x1", g, xg_flat)

            for j in range(NCH):
                ln_stats(xg[:, j, :], D, lnchunk)
                wait("tensor", "s")
                transpose_batch([lnchunk[:, k * 128:(k + 1) * 128] for k in range(16)],
                                lambda i0, m, j=j: ln0T[:, i0:i0 + m, j * 128:(j + 1) * 128])
            tap("ln0T", 100 + g, r1[:, :])
            v_ln2 = C["v"]
            vu_evac = 0
            for ot in range(44):
                pair = ot % 2
                gsl = statW8_stat[0] if pair == 0 else movW8_stat[0]
                usl = statW8_stat[1] if pair == 0 else movW8_stat[1]
                wait("gpsimd", "t", t_slab.get(("p4gu", g, ot - 2),
                                               t_slab.get(("p3", 3, 6 + ot), 0)))
                wsem = "wqa" if pair == 0 else "wqb"
                gb = ("stat", 0) if pair == 0 else ("mov", 0)
                ub = ("stat", 1) if pair == 0 else ("mov", 1)
                if g == 0:
                    war_wait(gb)
                    war_wait(ub)
                    dma("gpsimd", gsl,
                        wgT[:, ot * 128:(ot + 1) * 128].rearrange("(k p) o -> p k o", p=128), wsem)
                    dma("gpsimd", usl,
                        wuT[:, ot * 128:(ot + 1) * 128].rearrange("(k p) o -> p k o", p=128), wsem)
                    sid_g = sid8_map.setdefault(("gu_g", ot), len(sid8_map))
                    sid_u = sid8_map.setdefault(("gu_u", ot), len(sid8_map))
                    wait("sync", wsem, C[wsem])
                    dma("sync", wbf8[sid_g], BUF_FLAT8[gb], WSEM[gb])
                    store_pending[gb] = (WSEM[gb], C[WSEM[gb]])
                    dma("sync", wbf8[sid_u], BUF_FLAT8[ub], WSEM[ub])
                    store_pending[ub] = (WSEM[ub], C[WSEM[ub]])
                else:
                    war_wait(gb)
                    war_wait(ub)
                    dma("gpsimd", BUF_FLAT8[gb], wbf8[sid8_map[("gu_g", ot)]], wsem)
                    dma("gpsimd", BUF_FLAT8[ub], wbf8[sid8_map[("gu_u", ot)]], wsem)
                wait("tensor", wsem, C[wsem])
                wait("tensor", "v", max(v_ln2, vu_evac))
                for k in range(16):
                    op("tensor", lambda e, k=k, gsl=gsl: e.matmul(
                        ps_a, gsl[:, k, :], ln0T[:, k, :], start=(k == 0), stop=(k == 15)), inc="t")
                for k in range(16):
                    op("tensor", lambda e, k=k, usl=usl: e.matmul(
                        ps_b, usl[:, k, :], ln0T[:, k, :], start=(k == 0), stop=(k == 15)), inc="t")
                t_slab[("p4gu", g, ot)] = C["t"]
                wait("scalar", "t", C["t"])
                if vu_evac:
                    wait("scalar", "v", vu_evac)
                op("scalar", lambda e, ot=ot: e.activation(sgtmp, ps_a, AF.Silu,
                                                           scale=gscale[:, ot:ot + 1]), inc="s")
                wait("vector", "s")
                op("vector", lambda e, ot=ot: e.tensor_scalar(sgtmp2, ps_b, uscale[:, ot:ot + 1],
                                                              None, ALU.mult), inc="v")
                op("vector", lambda e, ot=ot: e.tensor_tensor(hT[:, ot, :], sgtmp2, sgtmp, ALU.mult), inc="v")
                vu_evac = C["v"]
            tap("hT", g, r23b[:, 0:44 * GTOK])

            v_p4d = 0
            for ot in range(4):
                for ks in range(11):
                    buf = ks % 2
                    wait("gpsimd", "t", t_slab.get(("p4d", ot, ks - 2),
                                                   t_slab.get(("p4d", ot - 1, 10 - ks),
                                                              t_slab.get(("p4gu", g, 43), 0))))
                    wsem = "wqa" if buf == 0 else "wqb"
                    if g == 0:
                        war_wait(("mov", buf))
                        dma("gpsimd", movW8_mov[buf],
                            wdT[ks * 512:(ks + 1) * 512, ot * 512:(ot + 1) * 512]
                            .rearrange("(k p) o -> p k o", p=128), wsem)
                        sid8 = sid8_map.setdefault(("p4d", ot, ks), len(sid8_map))
                        wait("sync", wsem, C[wsem])
                        dma("sync", wbf8[sid8], BUF_FLAT8[("mov", buf)], WSEM[("mov", buf)])
                        store_pending[("mov", buf)] = (WSEM[("mov", buf)], C[WSEM[("mov", buf)]])
                    else:
                        war_wait(("mov", buf))
                        dma("gpsimd", BUF_FLAT8[("mov", buf)], wbf8[sid8_map[("p4d", ot, ks)]], wsem)
                    wait("tensor", wsem, C[wsem])
                    if ot == 0 and ks == 0:
                        wait("tensor", "v", vu_evac)
                    if ks == 0 and v_p4d:
                        wait("tensor", "v", v_p4d)
                    for kk in range(4):
                        k = ks * 4 + kk
                        for j in range(NCH):
                            op("tensor", lambda e, k=k, kk=kk, j=j, buf=buf: e.matmul(
                                ps_q[:, j * 512:(j + 1) * 512],
                                hT[:, k, j * 128:(j + 1) * 128], movW8_mov[buf][:, kk, :],
                                start=(k == 0), stop=(k == 43)), inc="t")
                    t_slab[("p4d", ot, ks)] = C["t"]
                wait("vector", "t", C["t"])
                for j in range(NCH):
                    op("vector", lambda e, ot=ot, j=j: e.tensor_tensor(
                        ffn_stage[:, j, ot * 512:(ot + 1) * 512], ps_q[:, j * 512:(j + 1) * 512],
                        dsb[:, ot * 512:(ot + 1) * 512], ALU.mult), inc="v")
                v_p4d = C["v"]
            tap("ffns", g, r4.bitcast(F32)[:, 0:NCH * D])
            st_prev = 0
            for j in range(NCH):
                wait("scalar", "v", v_p4d)
                ln_stats(ffn_stage[:, j, :], D, lnchunk)
                wait("vector", "s")
                wait("vector", "st", st_prev)
                op("vector", lambda e, j=j: e.tensor_tensor(outst, xg[:, j, :], lnchunk, ALU.add), inc="v")
                wait("sync", "v")
                dma("sync", out_t[seq, t0 + j * CHUNK:t0 + (j + 1) * CHUNK, :], outst, "st")
                st_prev = C["st"]
            P.barrier()

        wait("sync", "st", C["st"])

        blk = e(nc.Block())
        blk.sync(lambda eng: [f(eng) for f in P.streams["sync"]])
        blk.tensor(lambda eng: [f(eng) for f in P.streams["tensor"]])
        blk.vector(lambda eng: [f(eng) for f in P.streams["vector"]])
        blk.scalar(lambda eng: [f(eng) for f in P.streams["scalar"]])
        blk.gpsimd(lambda eng: [f(eng) for f in P.streams["gpsimd"]])

    return nc


def _host_arrays(inputs):
    f32 = np.float32
    am = {}
    am["wipT"] = np.ascontiguousarray(inputs["L0_attn_in_proj_w"].T.astype(f32, copy=False))
    am["wopT"] = np.ascontiguousarray(inputs["L0_attn_out_proj_w"].T.astype(f32, copy=False))
    am["wgT"] = np.ascontiguousarray(inputs["L0_ffn_gate_w"].T.astype(f32, copy=False))
    am["wuT"] = np.ascontiguousarray(inputs["L0_ffn_up_w"].T.astype(f32, copy=False))
    am["wdT"] = np.ascontiguousarray(inputs["L0_ffn_down_w"].T.astype(f32, copy=False))
    am["convw"] = np.ascontiguousarray(
        inputs["L0_attn_conv_w"].reshape(NCT, 128, 4).transpose(1, 0, 2).reshape(128, NCT * 4))
    am["convb"] = np.ascontiguousarray(inputs["L0_attn_conv_b"].reshape(NCT, 128).T)
    am["dtbias"] = np.ascontiguousarray(inputs["L0_attn_dt_bias"].reshape(64, 1))
    am["alog"] = np.ascontiguousarray(inputs["L0_attn_A_log"].reshape(64, 1))
    am["drow"] = np.ascontiguousarray(inputs["L0_attn_D"].reshape(1, 64))
    am["nwrow"] = np.ascontiguousarray(inputs["L0_attn_norm_w"].reshape(1, DIN))
    am["dsrow"] = np.ascontiguousarray(inputs["L0_ffn_down_s"].reshape(1, D))
    am["gs"] = np.ascontiguousarray(inputs["L0_ffn_gate_s"].reshape(44, 128).T)
    am["us"] = np.ascontiguousarray(inputs["L0_ffn_up_s"].reshape(44, 128).T)
    am["idf"] = np.eye(64, dtype=f32)
    k2c = np.zeros((10, 1280), np.float32)
    k2c[8:10, 0:256] = 1.0
    for m in range(4):
        k2c[m, 256 + m * 128:256 + (m + 1) * 128] = 1.0
        k2c[4 + m, 256 + m * 128:256 + (m + 1) * 128] = 1.0
        k2c[m, 768 + m * 128:768 + (m + 1) * 128] = 1.0
        k2c[4 + m, 768 + m * 128:768 + (m + 1) * 128] = 1.0
    am["k2c"] = k2c.astype(ml_dtypes.bfloat16)
    am["idb"] = np.eye(128, dtype=f32)
    am["mask01"] = np.triu(np.ones((128, 128), dtype=f32))
    return am


def kernel(**inputs):
    x = np.asarray(inputs["L0_attn_hidden_states"], dtype=np.float32)
    am = _host_arrays({k: np.asarray(v) for k, v in inputs.items()})
    nc = build_program()
    in_maps = []
    for c in range(8):
        m = dict(am)
        m["x"] = np.ascontiguousarray(x[2 * c:2 * c + 2])
        in_maps.append(m)
    res = run_bass_kernel_spmd(nc, in_maps, core_ids=list(range(8)))
    return np.concatenate([r["out"] for r in res.results], axis=0)


if __name__ == "__main__":
    pass


# version 13
# speedup vs baseline: 1.0181x; 1.0181x over previous
import numpy as np
import ml_dtypes
from contextlib import ExitStack

import concourse.bass as bass
import concourse.mybir as mybir
from concourse.bass_utils import run_bass_kernel_spmd

F32 = mybir.dt.float32
BF16 = mybir.dt.bfloat16
FP8 = mybir.dt.float8e4
AF = mybir.ActivationFunctionType
ALU = mybir.AluOpType
AX = mybir.AxisListType

NSEQ, L, D, DIN = 2, 2048, 2048, 4096
CONVD, PROJ, INTER, CHUNK, GTOK = 4352, 8512, 5632, 128, 512
NCH, NGRP, EPS = GTOK // CHUNK, L // GTOK, 1e-5
NCT, NXT = CONVD // 128, DIN // 128
NGROUPS_FULL = NGRP * NSEQ


class Prog:
    def __init__(self):
        self.streams = {k: [] for k in ("sync", "tensor", "vector", "scalar", "gpsimd")}
        self.C = {k: 0 for k in ("ld", "l2", "wq", "wqa", "wqb", "xq", "fqa", "fqb",
                                 "fqc", "fqd", "gq", "t", "v", "s", "g", "y", "st",
                                 "wsa", "wsb", "wsc", "wsd")}
        self.sems = {}

    def emit(self, eng, fn):
        self.streams[eng].append(fn)

    SELF_SEM = {"vector": "v", "scalar": "s"}

    def op(self, eng, make, inc=None, inc_n=1):
        own = self.SELF_SEM.get(eng)
        if own is not None:
            self.wait(eng, own)
            if inc is None:
                inc, inc_n = own, 1
        if inc is not None:
            self.C[inc] += inc_n
        sems = self.sems

        def f(e, make=make, inc=inc, inc_n=inc_n):
            i = make(e)
            if inc is not None:
                i.then_inc(sems[inc], inc_n)
        self.emit(eng, f)

    def wait(self, eng, sem, val=None):
        if val is None:
            val = self.C[sem]
        if val <= 0:
            return
        sems = self.sems
        self.emit(eng, lambda e, sem=sem, val=val: e.wait_ge(sems[sem], val))

    def inc_only(self, eng, sem, n=1):
        self.C[sem] += n
        sems = self.sems
        def f(e, sem=sem, n=n):
            e.drain()
            e.sem_inc(sems[sem], n)
        self.emit(eng, f)

    def barrier(self):
        names = {"tensor": "t", "vector": "v", "scalar": "s", "gpsimd": "g", "sync": "y"}
        for eng, nm in names.items():
            self.inc_only(eng, nm)
        for eng in names:
            for other, onm in names.items():
                if other != eng:
                    self.wait(eng, onm)


def build_program(n_groups=NGROUPS_FULL, taps=()):
    nc = bass.Bass("TRN2", target_bir_lowering=False, debug=False)
    taps = list(taps)

    x_in = nc.dram_tensor("x", [NSEQ, L, D], F32, kind="ExternalInput").ap()
    out_t = nc.dram_tensor("out", [NSEQ, L, D], F32, kind="ExternalOutput").ap()
    wipT = nc.dram_tensor("wipT", [D, PROJ], F32, kind="ExternalInput").ap()
    wopT = nc.dram_tensor("wopT", [DIN, D], F32, kind="ExternalInput").ap()
    wgT = nc.dram_tensor("wgT", [D, INTER], F32, kind="ExternalInput").ap()
    wuT = nc.dram_tensor("wuT", [D, INTER], F32, kind="ExternalInput").ap()
    wdT = nc.dram_tensor("wdT", [INTER, D], F32, kind="ExternalInput").ap()
    convw_in = nc.dram_tensor("convw", [128, NCT * 4], F32, kind="ExternalInput").ap()
    convb_in = nc.dram_tensor("convb", [128, NCT], F32, kind="ExternalInput").ap()
    dtb_in = nc.dram_tensor("dtbias", [64, 1], F32, kind="ExternalInput").ap()
    alog_in = nc.dram_tensor("alog", [64, 1], F32, kind="ExternalInput").ap()
    drow_in = nc.dram_tensor("drow", [1, 64], F32, kind="ExternalInput").ap()
    nwrow_in = nc.dram_tensor("nwrow", [1, DIN], F32, kind="ExternalInput").ap()
    dsrow_in = nc.dram_tensor("dsrow", [1, D], F32, kind="ExternalInput").ap()
    gs_in = nc.dram_tensor("gs", [128, 44], F32, kind="ExternalInput").ap()
    us_in = nc.dram_tensor("us", [128, 44], F32, kind="ExternalInput").ap()
    idf_in = nc.dram_tensor("idf", [64, 64], F32, kind="ExternalInput").ap()
    idb_in = nc.dram_tensor("idb", [128, 128], F32, kind="ExternalInput").ap()
    mask_in = nc.dram_tensor("mask01", [128, 128], F32, kind="ExternalInput").ap()
    k2c_in = nc.dram_tensor("k2c", [10, 1280], BF16, kind="ExternalInput").ap()
    wbf = nc.dram_tensor("wbf", [99, 128, 2048], BF16, kind="Internal").ap()
    wbf8 = nc.dram_tensor("wbf8", [132, 128, 2048], FP8, kind="Internal").ap()

    dbg_shapes = {
        "xbcraw": ([128, NCT * 516], BF16), "dtTg": ([128, GTOK], F32),
        "sgate": ([128, NCH * DIN], BF16), "xbcact": ([128, NCT * 512], BF16),
        "yf": ([128, DIN], BF16), "ynT": ([128, NXT * GTOK], BF16),
        "S": ([128, DIN], F32), "x1": ([128, NCH * D], BF16),
        "hT": ([128, 44 * GTOK], BF16), "ffns": ([128, NCH * D], F32),
        "ln0T": ([128, 16 * GTOK], BF16), "xact": ([128, DIN], BF16),
        "attns": ([128, NCH * D], F32), "sm64": ([128, 64], F32), "sm64b": ([128, 64], BF16),
        "sm128": ([128, 128], F32), "mqd": ([128, 1024], BF16), "expd": ([128, 1024], F32),
    }
    dbg = {}
    for tag, gi in taps:
        sh, dt = dbg_shapes[tag]
        dbg[(tag, gi)] = nc.dram_tensor(f"dbg_{tag}_{gi}", sh, dt, kind="ExternalOutput").ap()

    P = Prog()
    op, wait, C = P.op, P.wait, P.C

    with ExitStack() as ctx:
        e = ctx.enter_context

        def sb(name, shape, dt):
            return e(nc.sbuf_tensor(name, shape, dt))

        r1 = sb("r1", [128, 16 * GTOK], BF16)
        ln0T = r1[:, :].rearrange("p (k t) -> p k t", t=GTOK)
        x_act = r1[:, 0:DIN]
        xs_t = r1[:, DIN:2 * DIN]

        R23B = 32768 + 36928
        r23 = sb("r23", [128, R23B], mybir.dt.uint8)
        r23b, r23f = r23.bitcast(BF16), r23.bitcast(F32)
        sgate = r23b[:, 0:NCH * DIN].rearrange("p (j o) -> p j o", o=DIN)
        attn_stage = r23f[:, 0:NCH * D].rearrange("p (j o) -> p j o", o=D)
        hT = r23b[:, 0:44 * GTOK].rearrange("p (k t) -> p k t", t=GTOK)
        xbcraw = r23b[:, 16384:16384 + NCT * 516].rearrange("p (c t) -> p c t", t=516)
        TB = 32768
        expM = r23f[:, TB // 4:TB // 4 + 1024]
        mq = [r23b[:, (TB + 4096 + i * 2048) // 2:(TB + 4096 + i * 2048) // 2 + 1024] for i in range(2)]
        tqf = r23f[:, (TB + 8192) // 4:(TB + 8192) // 4 + 1024]
        tqb = r23b[:, (TB + 8192) // 2:(TB + 8192) // 2 + 2048]
        yf = r23b[:, (TB + 12288) // 2:(TB + 12288) // 2 + DIN]
        sftmp = r23f[:, (TB + 20480) // 4:(TB + 20480) // 4 + 1024]
        outst = r23f[:, (TB + 24576) // 4:(TB + 24576) // 4 + D]
        lnchunk = r23b[:, (TB + 32768) // 2:(TB + 32768) // 2 + D]
        AG = r23f[:, (TB + 32768) // 4:(TB + 32768) // 4 + GTOK]
        BG = r23f[:, (TB + 32768 + 2048) // 4:(TB + 32768 + 2048) // 4 + GTOK]
        assert TB + 32768 + 2 * D <= R23B

        r4 = sb("r4", [128, NCT * 512], BF16)
        xbcact = r4[:, :].rearrange("p (c t) -> p c t", t=512)
        ynT = r4[:, 0:NXT * GTOK].rearrange("p (k t) -> p k t", t=GTOK)
        ffn_stage = r4.bitcast(F32)[:, 0:NCH * D].rearrange("p (j o) -> p j o", o=D)

        xg_flat = sb("xg", [128, NCH * D], BF16)[:, :]
        xg = xg_flat.rearrange("p (j o) -> p j o", o=D)
        S_f = sb("S_f", [128, DIN], F32)[:, :]
        S_b = sb("S_b", [128, DIN], BF16)[:, :]
        dtTg = sb("dtTg", [128, GTOK], F32)[:, :]
        acsb = sb("acsb", [128, 2048], F32)

        statW = [sb(f"statW{i}", [128, 16 * 128], BF16) for i in range(2)]
        movW = [sb(f"movW{i}", [128, 4 * 512], BF16) for i in range(2)]
        statWv = [w[:, :].rearrange("p (k o) -> p k o", o=128) for w in statW]
        movWv = [w[:, :].rearrange("p (k o) -> p k o", o=512) for w in movW]
        movW_as_stat = [w[:, 0:2048].rearrange("p (k o) -> p k o", o=128) for w in movW]
        statW_as_mov = [w[:, :].rearrange("p (k o) -> p k o", o=512) for w in statW]
        statW8f = [w.bitcast(FP8)[:, 0:2048] for w in statW]
        movW8f = [w.bitcast(FP8)[:, 0:2048] for w in movW]
        statW8_stat = [v.rearrange("p (k o) -> p k o", o=128) for v in statW8f]
        movW8_stat = [v.rearrange("p (k o) -> p k o", o=128) for v in movW8f]
        movW8_mov = [v.rearrange("p (k o) -> p k o", o=512) for v in movW8f]

        idf = sb("idf_c", [64, 64], F32)[:, :]
        idb = sb("idb_c", [128, 128], BF16)[:, :]
        mask01 = sb("mask_c", [128, 128], BF16)[:, :]
        convw = sb("convw_c", [128, NCT * 4], F32)[:, :].rearrange("p (c k) -> p c k", k=4)
        convb = sb("convb_c", [128, NCT], F32)[:, :]
        dtbias = sb("dtbias_c", [128, 1], F32)[:, :]
        alog = sb("alog_c", [128, 1], F32)[:, :]
        negA = sb("negA", [128, 1], F32)[:, :]
        gscale = sb("gscale", [128, 44], F32)[:, :]
        uscale = sb("uscale", [128, 44], F32)[:, :]
        D_b = sb("D_b", [128, 64], BF16)[:, :]
        normw_b = sb("normw_b", [128, DIN], BF16)[:, :]
        dsb = sb("dsb", [128, D], F32)[:, :]
        acsb_bf = acsb.bitcast(BF16)
        K2BASE = (0, 32, 0, 32)
        K2COL = (0, 0, 1280, 1280)
        K2SEM = ("fqa", "fqb", "fqc", "fqd")
        dtTg_bf = dtTg.bitcast(BF16)
        AGhi_g = dtTg_bf[0:64, 0:512]
        AGlo_g = dtTg_bf[0:64, 512:1024]
        BGhi_g = acsb_bf[0:64, 2560:3072]
        BGlo_g = acsb_bf[0:64, 3072:3584]
        onesrow = acsb[64:65, 0:128]
        zeros64 = acsb[64:128, 256:384]

        dtg4 = sb("dtg4", [128, GTOK], F32)[:, :]
        dtg5 = sb("dtg5", [128, GTOK], F32)[:, :]
        dsdt_b = sb("dsdt_b", [128, 64], BF16)[:, :]
        e1 = sb("e1c", [128, 64], F32)[:, :]
        gam = sb("gam", [128, 64], F32)[:, :]
        Gm = sb("Gm", [128, 128], F32)[:, :]
        B_sb = sb("B_sb", [128, 128], BF16)[:, :]
        stats = sb("stats", [128, 16], F32)[:, :]
        halo_sb = sb("halo_sb", [128, NCT * 3], BF16)[:, :].rearrange("p (c t) -> p c t", t=3)
        gmr = acsb[64:65, 0:192]
        c30 = sb("c30", [128, 1], F32)[:, :]
        sgtmp = r23b[:, (TB + 12288) // 2:(TB + 12288) // 2 + 512]
        sgtmp2 = r23b[:, (TB + 13312) // 2:(TB + 13312) // 2 + 512]
        convacc = None

        ps_tr = e(nc.psum_tensor("ps_tr", [128, 1024], BF16))[:, :]
        ps_a = e(nc.psum_tensor("ps_a", [128, 512], F32))[:, :]
        ps_b = e(nc.psum_tensor("ps_b", [128, 512], F32))[:, :]
        ps_q = e(nc.psum_tensor("ps_q", [128, 2048], F32))[:, :]
        ps_x = e(nc.psum_tensor("ps_x", [128, 512], F32))[:, :]
        convacc = [ps_tr.bitcast(F32)[:, 0:512], ps_x]

        for s_ in ("ld", "l2", "wq", "wqa", "wqb", "xq", "fqa", "fqb", "fqc", "fqd",
                   "gq", "t", "v", "s", "g", "y", "st", "wsa", "wsb", "wsc", "wsd"):
            P.sems[s_] = e(nc.semaphore(s_))

        def dma(queue, dst, src, sem):
            op(queue, lambda e, d=dst, s2=src: e.dma_start(out=d, in_=s2), inc=sem, inc_n=16)

        def ln_stats(src, W, nrm_out):
            wait("scalar", "v")
            wait("scalar", "t")
            wait("vector", "s")
            op("scalar", lambda e, src=src: e.activation(nrm_out, src, AF.Square,
                                                         accum_out=stats[:, 0:1]), inc="s")
            op("vector", lambda e, src=src: e.reduce_sum(stats[:, 1:2], src, AX.X), inc="v")
            op("vector", lambda e: e.tensor_scalar(stats[:, 2:3], stats[:, 1:2], 1.0 / W, None, ALU.mult), inc="v")
            op("vector", lambda e: e.tensor_tensor(stats[:, 3:4], stats[:, 2:3], stats[:, 2:3], ALU.mult), inc="v")
            wait("vector", "s")
            op("vector", lambda e: e.tensor_scalar(stats[:, 4:5], stats[:, 0:1], 1.0 / W, None, ALU.mult), inc="v")
            op("vector", lambda e: e.tensor_tensor(stats[:, 4:5], stats[:, 4:5], stats[:, 3:4], ALU.subtract), inc="v")
            op("vector", lambda e: e.tensor_scalar(stats[:, 4:5], stats[:, 4:5], EPS, None, ALU.add), inc="v")
            wait("scalar", "v")
            op("scalar", lambda e: e.activation(stats[:, 5:6], stats[:, 4:5], AF.Sqrt), inc="s")
            wait("vector", "s")
            op("vector", lambda e: e.reciprocal(stats[:, 6:7], stats[:, 5:6]), inc="v")
            op("vector", lambda e: e.tensor_tensor(stats[:, 7:8], stats[:, 2:3], stats[:, 6:7], ALU.mult), inc="v")
            op("vector", lambda e: e.tensor_scalar(stats[:, 7:8], stats[:, 7:8], -1.0, None, ALU.mult), inc="v")
            op("vector", lambda e, src=src: e.tensor_scalar(
                nrm_out, src, stats[:, 6:7], stats[:, 7:8], ALU.mult, ALU.add), inc="v")

        def transpose_batch(src_tiles, dst_fn):
            i, n = 0, len(src_tiles)
            while i < n:
                m = min(8, n - i)
                wait("tensor", "v")
                for k in range(m):
                    op("tensor", lambda e, s2=src_tiles[i + k], k=k:
                       e.transpose(ps_tr[:, k * 128:(k + 1) * 128], s2, idb), inc="t")
                wait("vector", "t")
                op("vector", lambda e, i0=i, m=m: e.tensor_copy(
                    dst_fn(i0, m), ps_tr[:, 0:m * 128].rearrange("p (n t) -> p n t", t=128)), inc="v")
                i += m

        def tap(tag, gi, src_ap):
            if (tag, gi) in dbg:
                P.barrier()
                dma("sync", dbg[(tag, gi)], src_ap, "st")
                wait("sync", "st")
                P.barrier()

        for dst, src in ((idf, idf_in),
                         (convw, convw_in.rearrange("p (c k) -> p c k", k=4)),
                         (convb, convb_in), (dtbias[0:64, :], dtb_in), (alog[0:64, :], alog_in),
                         (gscale, gs_in), (uscale, us_in)):
            dma("sync", dst, src, "ld")
        dma("gpsimd", idb, idb_in, "wq")
        dma("gpsimd", mask01, mask_in, "wq")
        wait("vector", "ld")
        for z in (r1[:, :], r23.bitcast(F32)[:, 0:R23B // 4], r4[:, :],
                  acsb[:, :], dtTg, S_f, S_b, stats, dtg4, dtg5,
                  dsdt_b, e1, gam, Gm, B_sb, xg_flat):
            op("vector", lambda e, z=z: e.memset(z, 0.0), inc="v")
        op("vector", lambda e: e.memset(onesrow, 1.0), inc="v")
        op("vector", lambda e: e.memset(c30, 30.0), inc="v")
        wait("scalar", "ld")
        op("scalar", lambda e: e.activation(negA[0:64, :], alog[0:64, :], AF.Exp), inc="s")
        wait("vector", "s")
        op("vector", lambda e: e.tensor_scalar(negA[0:64, :], negA[0:64, :], -1.0, None, ALU.mult), inc="v")
        stages = []
        for i in range(4):
            stages.append((nwrow_in[0:1, i * 1024:(i + 1) * 1024], 1024,
                           (lambda i=i: normw_b[:, i * 1024:(i + 1) * 1024])))
        for i in range(2):
            stages.append((dsrow_in[0:1, i * 1024:(i + 1) * 1024], 1024,
                           (lambda i=i: dsb[:, i * 1024:(i + 1) * 1024])))
        stages.append((drow_in, 64, (lambda: D_b)))
        datarow = acsb[64:65, 128:128 + 1024]
        wait("sync", "v")
        for src, W, dstf in stages:
            wait("sync", "t")
            dma("sync", datarow[0:1, 0:W], src, "l2")
            wait("tensor", "l2")
            wait("tensor", "v")
            nmm = (W + 511) // 512
            for i in range(nmm):
                w2 = min(512, W - i * 512)
                op("tensor", lambda e, i=i, w2=w2: e.matmul(
                    ps_q[:, i * 512:i * 512 + w2], onesrow,
                    datarow[0:1, i * 512:i * 512 + w2], start=True, stop=True), inc="t")
            wait("vector", "t")
            op("vector", lambda e, dstf=dstf, W=W: e.tensor_copy(dstf(), ps_q[:, 0:W]), inc="v")
        wait("vector", "t")
        wait("sync", "t")
        for k2base, k2c0 in ((0, 0), (32, 0), (0, 1280), (32, 1280)):
            dma("sync", acsb_bf[k2base:k2base + 10, k2c0:k2c0 + 1280], k2c_in, "l2")
            wait("sync", "l2")
        wait("vector", "l2")

        op("vector", lambda e: e.memset(zeros64, 0.0), inc="v")
        wait("tensor", "wq")
        P.barrier()

        K2T = {}

        def scan_chunk(j):
            js = slice(j * 128, (j + 1) * 128)
            jlast = j * 128 + 127

            op("vector", lambda e: e.tensor_scalar(dtg5[0:64, js], AG[0:64, js],
                                                   AG[0:64, jlast:jlast + 1], None, ALU.subtract), inc="v")
            wait("scalar", "v")
            op("scalar", lambda e: e.activation(dtg5[0:64, js], dtg5[0:64, js], AF.Exp, scale=-1.0), inc="s")
            wait("vector", "s")
            op("vector", lambda e: e.tensor_tensor(dtg5[0:64, js], dtg5[0:64, js],
                                                   dtg4[0:64, js], ALU.mult), inc="v")
            v_dsdt = C["v"]

            wait("tensor", "v", v_dsdt)
            wait("tensor", "s")
            op("tensor", lambda e: e.transpose(ps_x[:, 0:64], AG[0:64, js], idf), inc="t")
            op("tensor", lambda e: e.transpose(ps_x[:, 64:128], dtg5[0:64, js], idf), inc="t")
            t_tr = C["t"]
            wait("scalar", "t")
            dma("scalar", acsb[64:65, 128:192], AG[0:64, jlast:jlast + 1], "gq")
            fq_gam = C["gq"]
            wait("vector", "t", t_tr)
            op("vector", lambda e: e.tensor_copy(dsdt_b, ps_x[:, 64:128]), inc="v")
            v_tr = C["v"]
            wait("scalar", "t", t_tr)
            op("scalar", lambda e: e.activation(e1, ps_x[:, 0:64], AF.Exp), inc="s")
            s_e1 = C["s"]
            wait("tensor", "gq", fq_gam)
            wait("tensor", "v", v_tr)
            wait("tensor", "s", s_e1)
            op("tensor", lambda e: e.matmul(ps_x[:, 192:256], onesrow, acsb[64:65, 128:192], start=True, stop=True), inc="t")
            t_gam = C["t"]
            wait("scalar", "t", t_gam)
            op("scalar", lambda e: e.activation(gam, ps_x[:, 192:256], AF.Exp), inc="s")
            s_gam = C["s"]

            wait("vector", "s")
            transpose_batch([xbcact[:, d, js] for d in range(NXT)],
                            lambda i0, m: x_act.rearrange("p (n t) -> p n t", t=128)[:, i0:i0 + m, :])
            wait("tensor", "v")
            op("tensor", lambda e: e.transpose(ps_tr[:, 0:128], xbcact[:, 32, js], idb), inc="t")
            wait("vector", "t")
            op("vector", lambda e: e.tensor_copy(B_sb, ps_tr[:, 0:128]), inc="v")
            op("tensor", lambda e: e.matmul(ps_x[:, 256:384], xbcact[:, 32, js], xbcact[:, 33, js],
                                            start=True, stop=True), inc="t")
            t_g = C["t"]
            wait("vector", "t", t_g)
            op("vector", lambda e: e.scalar_tensor_tensor(Gm, ps_x[:, 256:384], 1.0, mask01,
                                                          ALU.mult, ALU.mult), inc="v")
            op("vector", lambda e: e.tensor_tensor(
                xs_t.rearrange("p (h d) -> p h d", d=64), x_act.rearrange("p (h d) -> p h d", d=64),
                dsdt_b.broadcast_to((128, 64, 64)), ALU.mult), inc="v")
            v_gm = C["v"]

            k2_hist = {}
            v_mq_hist = {}

            def emit_flatten(og):
                oct_id = j * 8 + og
                st_ = oct_id % 4
                base, c0, fsem = K2BASE[st_], K2COL[st_], K2SEM[st_]
                wait("sync", "t", K2T.get(oct_id - 4, 0))
                for q in range(2):
                    hq = og * 8 + q * 4
                    dma("sync", acsb_bf[base:base + 4, c0 + q * 128:c0 + (q + 1) * 128],
                        BGhi_g[hq:hq + 4, js], fsem)
                    dma("sync", acsb_bf[base + 4:base + 8, c0 + q * 128:c0 + (q + 1) * 128],
                        BGlo_g[hq:hq + 4, js], fsem)
                    dma("sync", acsb_bf[base + 8:base + 9, c0 + 256 + q * 512:c0 + 768 + q * 512]
                        .rearrange("p (h t) -> p h t", t=128), AGhi_g[hq:hq + 4, js], fsem)
                    dma("sync", acsb_bf[base + 9:base + 10, c0 + 256 + q * 512:c0 + 768 + q * 512]
                        .rearrange("p (h t) -> p h t", t=128), AGlo_g[hq:hq + 4, js], fsem)
                return C[fsem]

            def emit_k2(og, fq_val):
                oct_id = j * 8 + og
                st_ = oct_id % 4
                base, c0, fsem = K2BASE[st_], K2COL[st_], K2SEM[st_]
                wait("tensor", fsem, fq_val)
                wait("tensor", "s")
                for q in range(2):
                    psd = ps_a if q == 0 else ps_b
                    op("tensor", lambda e, psd=psd, q=q, base=base, c0=c0: e.matmul(
                        psd[:, :], acsb_bf[base:base + 10, c0 + q * 128:c0 + (q + 1) * 128],
                        acsb_bf[base:base + 10, c0 + 256 + q * 512:c0 + 768 + q * 512],
                        start=True, stop=True), inc="t")
                K2T[oct_id] = C["t"]
                k2_hist[og] = C["t"]

            def emit_exps_mq(og):
                oc = og % 2
                wait("scalar", "t", k2_hist[og])
                wait("scalar", "v")
                op("scalar", lambda e: e.activation(ps_a, ps_a, AF.Relu, scale=-1.0, bias=c30), inc="s")
                op("scalar", lambda e: e.activation(ps_b, ps_b, AF.Relu, scale=-1.0, bias=c30), inc="s")
                op("scalar", lambda e: e.activation(expM[:, 0:512], ps_a, AF.Exp, scale=-1.0, bias=c30), inc="s")
                op("scalar", lambda e: e.activation(expM[:, 512:1024], ps_b, AF.Exp, scale=-1.0, bias=c30), inc="s")
                wait("vector", "s")
                wait("vector", "t")
                op("vector", lambda e, oc=oc: e.tensor_tensor(
                    mq[oc][0:128, 0:1024].rearrange("p (h t) -> p h t", t=128),
                    expM.rearrange("p (h t) -> p h t", t=128),
                    Gm.broadcast_to((128, 128, 8)).rearrange("p t h -> p h t"),
                    ALU.mult), inc="v")
                v_mq_hist[og] = C["v"]

            vstate = {"yf": 0}

            def emit_ydiag(og):
                oc = og % 2
                wait("tensor", "v", max(v_mq_hist[og], vstate["yf"]))
                for hh in range(8):
                    h = og * 8 + hh
                    slot = oc * 512 + hh * 64
                    op("tensor", lambda e, oc=oc, hh=hh, h=h, slot=slot: e.matmul(
                        ps_q[:, slot:slot + 64], mq[oc][:, hh * 128:(hh + 1) * 128],
                        x_act[:, h * 64:(h + 1) * 64], start=(hh == 0), stop=False), inc="t")

            fqv = {}
            for og in range(3):
                fqv[og] = emit_flatten(og)
            emit_k2(0, fqv[0])
            emit_exps_mq(0)
            for og in range(1, 8):
                if og + 2 <= 7:
                    fqv[og + 2] = emit_flatten(og + 2)
                emit_k2(og, fqv[og])
                emit_ydiag(og - 1)
                emit_exps_mq(og)
                if og % 2 == 0:
                    quarter_tail(j, og // 2 - 1, s_gam, vstate)
            emit_ydiag(7)
            quarter_tail(j, 3, s_gam, vstate)
            chunk_norm(j)
            return

        def dt_batch():
            dtraw = dtTg[0:64, :]
            for s_ in ("fqa", "fqb", "fqc", "fqd", "gq"):
                wait("scalar", s_)
                wait("vector", s_)
            wait("vector", "t")
            op("vector", lambda e: e.tensor_scalar(dtg4[0:64, :], dtraw, dtbias[0:64, :],
                                                   None, ALU.add), inc="v")
            wait("scalar", "v")
            op("scalar", lambda e: e.activation(dtg5[0:64, :], dtg4[0:64, :], AF.Abs), inc="s")
            op("scalar", lambda e: e.activation(dtg5[0:64, :], dtg5[0:64, :], AF.Exp, scale=-1.0), inc="s")
            op("scalar", lambda e: e.activation(dtg5[0:64, :], dtg5[0:64, :], AF.Ln, bias=1.0), inc="s")
            wait("vector", "s")
            op("vector", lambda e: e.scalar_tensor_tensor(dtg4[0:64, :], dtg4[0:64, :], 0.0,
                                                          dtg5[0:64, :], ALU.max, ALU.add), inc="v")
            wait("scalar", "v")
            op("scalar", lambda e: e.activation(dtg5[0:64, :], dtg4[0:64, :], AF.Ln), inc="s")
            wait("vector", "s")
            op("vector", lambda e: e.tensor_scalar(BG[0:64, :], dtg4[0:64, :], negA[0:64, :],
                                                   None, ALU.mult), inc="v")
            op("vector", lambda e: e.memset(Gm[0:64, :], 0.0), inc="v")
            for jj in range(NCH):
                op("vector", lambda e, jj=jj: e.tensor_tensor_scan(
                    AG[0:64, jj * 128:(jj + 1) * 128], BG[0:64, jj * 128:(jj + 1) * 128],
                    Gm[0:64, :], 0.0, ALU.add, ALU.add), inc="v")
            op("vector", lambda e: e.tensor_tensor(BG[0:64, :], dtg5[0:64, :], AG[0:64, :],
                                                   ALU.subtract), inc="v")
            op("vector", lambda e: e.tensor_copy(AGhi_g, AG[0:64, :]), inc="v")
            op("vector", lambda e: e.tensor_tensor(AGlo_g, AG[0:64, :], AGhi_g, ALU.subtract), inc="v")
            op("vector", lambda e: e.tensor_copy(BGhi_g, BG[0:64, :]), inc="v")
            op("vector", lambda e: e.tensor_tensor(BGlo_g, BG[0:64, :], BGhi_g, ALU.subtract), inc="v")

        def chunk_norm(j):
            wait("scalar", "v")
            wait("scalar", "t")
            op("scalar", lambda e: e.activation(x_act, yf, AF.Square,
                                                accum_out=stats[:, 8:9]), inc="s")
            wait("vector", "s")
            op("vector", lambda e: e.tensor_scalar(stats[:, 9:10], stats[:, 8:9], 1.0 / DIN, EPS,
                                                   ALU.mult, ALU.add), inc="v")
            v_ss = C["v"]
            wait("scalar", "v", v_ss)
            op("scalar", lambda e: e.activation(stats[:, 10:11], stats[:, 9:10], AF.Ln), inc="s")
            op("scalar", lambda e: e.activation(stats[:, 11:12], stats[:, 10:11], AF.Exp, scale=-0.5), inc="s")
            s_rstd = C["s"]
            wait("vector", "s", s_rstd)
            op("vector", lambda e: e.scalar_tensor_tensor(xs_t, yf, stats[:, 11:12], normw_b,
                                                          ALU.mult, ALU.mult), inc="v")
            transpose_batch([xs_t[:, d * 128:(d + 1) * 128] for d in range(NXT)],
                            lambda i0, m, j=j: ynT[:, i0:i0 + m, j * 128:(j + 1) * 128])

        def quarter_tail(j, q, s_gam, vstate):
            js = slice(j * 128, (j + 1) * 128)
            qs = slice(q * 1024, (q + 1) * 1024)
            for half in range(2):
                op("tensor", lambda e, q=q, half=half: e.matmul(
                    ps_q[:, 1024 + half * 512:1536 + half * 512], xbcact[:, 33, js],
                    S_b[:, q * 1024 + half * 512:q * 1024 + (half + 1) * 512],
                    start=True, stop=True), inc="t")
            t_yoff = C["t"]
            wait("vector", "t", t_yoff)
            op("vector", lambda e, q=q: e.tensor_tensor(
                tqb[:, 0:1024].rearrange("p (h d) -> p h d", d=64),
                ps_q[:, 1024:2048].rearrange("p (h d) -> p h d", d=64),
                e1[:, q * 16:(q + 1) * 16].broadcast_to((128, 16, 64)), ALU.mult), inc="v")
            v_a1 = C["v"]
            op("vector", lambda e, q=q: e.tensor_tensor(
                tqb[:, 1024:2048].rearrange("p (h d) -> p h d", d=64),
                x_act[:, q * 1024:(q + 1) * 1024].rearrange("p (h d) -> p h d", d=64),
                D_b[:, q * 16:(q + 1) * 16].broadcast_to((128, 16, 64)), ALU.mult), inc="v")
            v_a2 = C["v"]
            wait("tensor", "v", v_a2)
            for half in range(2):
                for sl in range(8):
                    slot = half * 512 + sl * 64
                    op("tensor", lambda e, slot=slot: e.matmul(
                        ps_q[:, slot:slot + 64], idb,
                        tqb[:, slot:slot + 64], start=False, stop=False), inc="t")
                    op("tensor", lambda e, slot=slot, sl=sl: e.matmul(
                        ps_q[:, slot:slot + 64], idb,
                        tqb[:, 1024 + slot:1024 + slot + 64],
                        start=False, stop=(sl == 7)), inc="t")
            t_acc = C["t"]
            wait("vector", "t", t_acc)
            op("vector", lambda e, j=j, qs=qs: e.scalar_tensor_tensor(
                yf[:, qs], ps_q[:, 0:1024], 1.0, sgate[:, j, qs], ALU.mult, ALU.mult), inc="v")
            vstate["yf"] = C["v"]
            wait("tensor", "v", v_a1)
            for half in range(2):
                op("tensor", lambda e, q=q, half=half: e.matmul(
                    ps_q[:, 1024 + half * 512:1536 + half * 512], B_sb,
                    xs_t[:, q * 1024 + half * 512:q * 1024 + (half + 1) * 512],
                    start=True, stop=True), inc="t")
            t_st = C["t"]
            wait("vector", "t", t_st)
            wait("vector", "s", s_gam)
            op("vector", lambda e, q=q, qs=qs: e.tensor_tensor(
                sftmp.rearrange("p (h d) -> p h d", d=64),
                S_f[:, qs].rearrange("p (h d) -> p h d", d=64),
                gam[:, q * 16:(q + 1) * 16].broadcast_to((128, 16, 64)), ALU.mult), inc="v")
            op("vector", lambda e, qs=qs: e.tensor_tensor(S_f[:, qs], sftmp, ps_q[:, 1024:2048],
                                                          ALU.add), inc="v")
            wait("scalar", "v")
            op("scalar", lambda e, qs=qs: e.copy(S_b[:, qs], S_f[:, qs]), inc="s")


        sid_map = {}
        store_pending = {}
        WSEM = {("stat", 0): "wsa", ("stat", 1): "wsb", ("mov", 0): "wsc", ("mov", 1): "wsd"}
        BUF_FLAT = {("stat", 0): statW[0][:, :], ("stat", 1): statW[1][:, :],
                    ("mov", 0): movW[0][:, :], ("mov", 1): movW[1][:, :]}
        BUF_FLAT8 = {("stat", 0): statW8f[0], ("stat", 1): statW8f[1],
                     ("mov", 0): movW8f[0], ("mov", 1): movW8f[1]}
        sid8_map = {}

        def war_wait(buf_id):
            if buf_id in store_pending:
                sem_, val_ = store_pending.pop(buf_id)
                wait("gpsimd", sem_, val_)

        def emit_store(key, buf_id, qsem):
            sid = sid_map.setdefault(key, len(sid_map))
            ssem = WSEM[buf_id]
            wait("sync", qsem, C[qsem])
            dma("sync", wbf[sid], BUF_FLAT[buf_id], ssem)
            store_pending[buf_id] = (ssem, C[ssem])

        t_slab = {}
        for g in range(n_groups):
            seq, grp = divmod(g, NGRP)
            t0 = grp * GTOK
            if g:
                wait("vector", "st", C["st"])
                wait("scalar", "st", C["st"])
            if grp == 0:
                op("vector", lambda e: e.memset(S_f, 0.0), inc="v")
                op("vector", lambda e: e.memset(S_b, 0.0), inc="v")
                op("vector", lambda e: e.memset(halo_sb, 0.0), inc="v")

            for j in range(NCH):
                dma("gpsimd", xg[:, j, :], x_in[seq, t0 + j * CHUNK:t0 + (j + 1) * CHUNK, :], "xq")
            ld_x = C["xq"]
            wait("scalar", "xq", ld_x)
            wait("vector", "xq", ld_x)
            for j in range(NCH):
                ln_stats(xg[:, j, :], D, lnchunk)
                wait("tensor", "s")
                transpose_batch([lnchunk[:, k * 128:(k + 1) * 128] for k in range(16)],
                                lambda i0, m, j=j: ln0T[:, i0:i0 + m, j * 128:(j + 1) * 128])
            tap("ln0T", g, r1[:, :])

            v_ln0 = C["v"]
            evac_cnt = {0: 0, 1: 0}
            for ot in range(NCT + 1):
                buf = ot % 2
                width = 128 if ot < NCT else 64
                c0 = 4096 + ot * 128
                wait("gpsimd", "t", t_slab.get(("p1a", g, ot - 2), 0))
                wsem = "wqa" if buf == 0 else "wqb"
                if g == 0:
                    war_wait(("stat", buf))
                    dma("gpsimd", statWv[buf][:, :, 0:width],
                        wipT[:, c0:c0 + width].rearrange("(k p) o -> p k o", p=128), wsem)
                    emit_store(("p1a", ot), ("stat", buf), wsem)
                else:
                    war_wait(("stat", buf))
                    dma("gpsimd", statW[buf][:, :], wbf[sid_map[("p1a", ot)]], wsem)
                wait("tensor", wsem, C[wsem])
                wait("tensor", "v", max(v_ln0, evac_cnt[buf]))
                psd = ps_a if buf == 0 else ps_b
                for k in range(16):
                    op("tensor", lambda e, k=k, buf=buf, w=width, psd=psd: e.matmul(
                        psd[0:w, :], statWv[buf][:, k, 0:w], ln0T[:, k, :],
                        start=(k == 0), stop=(k == 15)), inc="t")
                t_slab[("p1a", g, ot)] = C["t"]
                wait("vector", "t", C["t"])
                if ot < NCT:
                    op("vector", lambda e, ot=ot, psd=psd: e.tensor_copy(xbcraw[:, ot, 4:516], psd), inc="v")
                else:
                    op("vector", lambda e, psd=psd: e.tensor_copy(dtTg[0:64, :], psd[0:64, :]), inc="v")
                evac_cnt[buf] = C["v"]
            tap("xbcraw", g, r23b[:, 16384:16384 + NCT * 516])
            tap("dtTg", g, dtTg)

            op("vector", lambda e: e.tensor_copy(xbcraw[:, :, 1:4], halo_sb), inc="v")
            op("vector", lambda e: e.tensor_copy(halo_sb, xbcraw[:, :, 513:516]), inc="v")

            def conv_ct(ct):
                wait("vector", "s")
                op("vector", lambda e, ct=ct: e.tensor_scalar(
                    convacc[0], xbcraw[:, ct, 1:513], convw[:, ct, 0:1], None, ALU.mult), inc="v")
                op("vector", lambda e, ct=ct: e.scalar_tensor_tensor(
                    convacc[1], xbcraw[:, ct, 2:514], convw[:, ct, 1:2], convacc[0],
                    ALU.mult, ALU.add), inc="v")
                op("vector", lambda e, ct=ct: e.scalar_tensor_tensor(
                    convacc[0], xbcraw[:, ct, 3:515], convw[:, ct, 2:3], convacc[1],
                    ALU.mult, ALU.add), inc="v")
                op("vector", lambda e, ct=ct: e.scalar_tensor_tensor(
                    convacc[1], xbcraw[:, ct, 4:516], convw[:, ct, 3:4], convacc[0],
                    ALU.mult, ALU.add), inc="v")
                wait("scalar", "v")
                op("scalar", lambda e, ct=ct: e.activation(
                    xbcact[:, ct, :], convacc[1], AF.Silu, bias=convb[:, ct:ct + 1]), inc="s")

            conv_plan = [[31, 32, 33, 0]] + [list(range(4 * o - 3, 4 * o + 1)) for o in range(1, 8)]
            conv_plan[7] = list(range(25, 31))
            s_gate_evac = 0
            for ot in range(8):
                for ks in range(4):
                    buf = ks % 2
                    wait("gpsimd", "t", t_slab.get(("p1b", ot, ks - 2),
                                                   t_slab.get(("p1b", ot - 1, ks + 2),
                                                              t_slab.get(("p1a", g, NCT - ks), 0))))
                    wsem = "wqa" if buf == 0 else "wqb"
                    if g == 0:
                        war_wait(("mov", buf))
                        dma("gpsimd", movWv[buf],
                            wipT[ks * 512:(ks + 1) * 512, ot * 512:(ot + 1) * 512]
                            .rearrange("(k p) o -> p k o", p=128), wsem)
                        emit_store(("p1b", ot, ks), ("mov", buf), wsem)
                    else:
                        war_wait(("mov", buf))
                        dma("gpsimd", movW[buf][:, :], wbf[sid_map[("p1b", ot, ks)]], wsem)
                    wait("tensor", wsem, C[wsem])
                    if ks == 0 and s_gate_evac:
                        wait("tensor", "s", s_gate_evac)
                    for kk in range(4):
                        k = ks * 4 + kk
                        for j in range(NCH):
                            op("tensor", lambda e, k=k, kk=kk, j=j, buf=buf: e.matmul(
                                ps_q[:, j * 512:(j + 1) * 512],
                                ln0T[:, k, j * 128:(j + 1) * 128], movWv[buf][:, kk, :],
                                start=(k == 0), stop=(k == 15)), inc="t")
                    t_slab[("p1b", ot, ks)] = C["t"]
                for ct in conv_plan[ot]:
                    conv_ct(ct)
                if ot == 0:
                    dt_batch()
                wait("scalar", "t", C["t"])
                for j in range(NCH):
                    op("scalar", lambda e, ot=ot, j=j: e.activation(
                        sgate[:, j, ot * 512:(ot + 1) * 512], ps_q[:, j * 512:(j + 1) * 512],
                        AF.Silu), inc="s")
                s_gate_evac = C["s"]
            P.barrier()
            tap("sgate", g, r23b[:, 0:NCH * DIN])
            tap("xbcact", g, r4[:, :])
            tap("dtTg", 50 + g, AG)
            tap("dtTg", 60 + g, BG)
            tap("dtTg", 70 + g, dtg4)

            for j in range(NCH):
                scan_chunk(j)
                if any(k[1] == g * 10 + j for k in dbg):
                    P.barrier()
                tap("yf", g * 10 + j, yf)
                tap("xact", g * 10 + j, x_act)
                tap("S", g * 10 + j, S_f)
                tap("sm64", g * 10 + j, e1)
                tap("sm64", 100 + g * 10 + j, gam)
                tap("sm128", g * 10 + j, Gm)
                tap("mqd", g * 10 + j, mq[1])
                tap("expd", g * 10 + j, expM)
                tap("sm64b", 200 + g * 10 + j, dsdt_b)
                tap("xact", 300 + g * 10 + j, xs_t)
            tap("ynT", g, r4[:, 0:NXT * GTOK])

            v_ynT = C["v"]
            s_p3 = 0
            for ot in range(4):
                for ks in range(8):
                    buf = ks % 2
                    wait("gpsimd", "t", t_slab.get(("p3", ot, ks - 2), t_slab.get(("p3", ot - 1, ks + 6), 0)))
                    wsem = "wqa" if buf == 0 else "wqb"
                    if g == 0:
                        war_wait(("mov", buf))
                        dma("gpsimd", movWv[buf],
                            wopT[ks * 512:(ks + 1) * 512, ot * 512:(ot + 1) * 512]
                            .rearrange("(k p) o -> p k o", p=128), wsem)
                        emit_store(("p3", ot, ks), ("mov", buf), wsem)
                    else:
                        war_wait(("mov", buf))
                        dma("gpsimd", movW[buf][:, :], wbf[sid_map[("p3", ot, ks)]], wsem)
                    wait("tensor", wsem, C[wsem])
                    if ks == 0:
                        wait("tensor", "v", v_ynT)
                        if s_p3:
                            wait("tensor", "s", s_p3)
                    for kk in range(4):
                        k = ks * 4 + kk
                        for j in range(NCH):
                            op("tensor", lambda e, k=k, kk=kk, j=j, buf=buf: e.matmul(
                                ps_q[:, j * 512:(j + 1) * 512],
                                ynT[:, k, j * 128:(j + 1) * 128], movWv[buf][:, kk, :],
                                start=(k == 0), stop=(k == 31)), inc="t")
                    t_slab[("p3", ot, ks)] = C["t"]
                wait("scalar", "t", C["t"])
                for j in range(NCH):
                    op("scalar", lambda e, ot=ot, j=j: e.copy(
                        attn_stage[:, j, ot * 512:(ot + 1) * 512], ps_q[:, j * 512:(j + 1) * 512]), inc="s")
                s_p3 = C["s"]
            tap("attns", g, r23f[:, 0:NCH * D])
            for j in range(NCH):
                wait("scalar", "s", s_p3)
                wait("vector", "s", s_p3)
                ln_stats(attn_stage[:, j, :], D, lnchunk)
                wait("vector", "s")
                op("vector", lambda e, j=j: e.tensor_tensor(xg[:, j, :], xg[:, j, :], lnchunk, ALU.add), inc="v")
            tap("x1", g, xg_flat)

            for j in range(NCH):
                ln_stats(xg[:, j, :], D, lnchunk)
                wait("tensor", "s")
                transpose_batch([lnchunk[:, k * 128:(k + 1) * 128] for k in range(16)],
                                lambda i0, m, j=j: ln0T[:, i0:i0 + m, j * 128:(j + 1) * 128])
            tap("ln0T", 100 + g, r1[:, :])
            v_ln2 = C["v"]
            vu_evac = 0
            for ot in range(44):
                pair = ot % 2
                gsl = statW8_stat[0] if pair == 0 else movW8_stat[0]
                usl = statW8_stat[1] if pair == 0 else movW8_stat[1]
                wait("gpsimd", "t", t_slab.get(("p4gu", g, ot - 2),
                                               t_slab.get(("p3", 3, 6 + ot), 0)))
                wsem = "wqa" if pair == 0 else "wqb"
                gb = ("stat", 0) if pair == 0 else ("mov", 0)
                ub = ("stat", 1) if pair == 0 else ("mov", 1)
                if g == 0:
                    war_wait(gb)
                    war_wait(ub)
                    dma("gpsimd", gsl,
                        wgT[:, ot * 128:(ot + 1) * 128].rearrange("(k p) o -> p k o", p=128), wsem)
                    dma("gpsimd", usl,
                        wuT[:, ot * 128:(ot + 1) * 128].rearrange("(k p) o -> p k o", p=128), wsem)
                    sid_g = sid8_map.setdefault(("gu_g", ot), len(sid8_map))
                    sid_u = sid8_map.setdefault(("gu_u", ot), len(sid8_map))
                    wait("sync", wsem, C[wsem])
                    dma("sync", wbf8[sid_g], BUF_FLAT8[gb], WSEM[gb])
                    store_pending[gb] = (WSEM[gb], C[WSEM[gb]])
                    dma("sync", wbf8[sid_u], BUF_FLAT8[ub], WSEM[ub])
                    store_pending[ub] = (WSEM[ub], C[WSEM[ub]])
                else:
                    war_wait(gb)
                    war_wait(ub)
                    dma("gpsimd", BUF_FLAT8[gb], wbf8[sid8_map[("gu_g", ot)]], wsem)
                    dma("gpsimd", BUF_FLAT8[ub], wbf8[sid8_map[("gu_u", ot)]], wsem)
                wait("tensor", wsem, C[wsem])
                wait("tensor", "v", max(v_ln2, vu_evac))
                for k in range(16):
                    op("tensor", lambda e, k=k, gsl=gsl: e.matmul(
                        ps_a, gsl[:, k, :], ln0T[:, k, :], start=(k == 0), stop=(k == 15)), inc="t")
                for k in range(16):
                    op("tensor", lambda e, k=k, usl=usl: e.matmul(
                        ps_b, usl[:, k, :], ln0T[:, k, :], start=(k == 0), stop=(k == 15)), inc="t")
                t_slab[("p4gu", g, ot)] = C["t"]
                wait("scalar", "t", C["t"])
                if vu_evac:
                    wait("scalar", "v", vu_evac)
                op("scalar", lambda e, ot=ot: e.activation(sgtmp, ps_a, AF.Silu,
                                                           scale=gscale[:, ot:ot + 1]), inc="s")
                wait("vector", "s")
                op("vector", lambda e, ot=ot: e.tensor_scalar(sgtmp2, ps_b, uscale[:, ot:ot + 1],
                                                              None, ALU.mult), inc="v")
                op("vector", lambda e, ot=ot: e.tensor_tensor(hT[:, ot, :], sgtmp2, sgtmp, ALU.mult), inc="v")
                vu_evac = C["v"]
            tap("hT", g, r23b[:, 0:44 * GTOK])

            v_p4d = 0
            for ot in range(4):
                for ks in range(11):
                    buf = ks % 2
                    wait("gpsimd", "t", t_slab.get(("p4d", ot, ks - 2),
                                                   t_slab.get(("p4d", ot - 1, 10 - ks),
                                                              t_slab.get(("p4gu", g, 43), 0))))
                    wsem = "wqa" if buf == 0 else "wqb"
                    if g == 0:
                        war_wait(("mov", buf))
                        dma("gpsimd", movW8_mov[buf],
                            wdT[ks * 512:(ks + 1) * 512, ot * 512:(ot + 1) * 512]
                            .rearrange("(k p) o -> p k o", p=128), wsem)
                        sid8 = sid8_map.setdefault(("p4d", ot, ks), len(sid8_map))
                        wait("sync", wsem, C[wsem])
                        dma("sync", wbf8[sid8], BUF_FLAT8[("mov", buf)], WSEM[("mov", buf)])
                        store_pending[("mov", buf)] = (WSEM[("mov", buf)], C[WSEM[("mov", buf)]])
                    else:
                        war_wait(("mov", buf))
                        dma("gpsimd", BUF_FLAT8[("mov", buf)], wbf8[sid8_map[("p4d", ot, ks)]], wsem)
                    wait("tensor", wsem, C[wsem])
                    if ot == 0 and ks == 0:
                        wait("tensor", "v", vu_evac)
                    if ks == 0 and v_p4d:
                        wait("tensor", "v", v_p4d)
                    for kk in range(4):
                        k = ks * 4 + kk
                        for j in range(NCH):
                            op("tensor", lambda e, k=k, kk=kk, j=j, buf=buf: e.matmul(
                                ps_q[:, j * 512:(j + 1) * 512],
                                hT[:, k, j * 128:(j + 1) * 128], movW8_mov[buf][:, kk, :],
                                start=(k == 0), stop=(k == 43)), inc="t")
                    t_slab[("p4d", ot, ks)] = C["t"]
                wait("vector", "t", C["t"])
                for j in range(NCH):
                    op("vector", lambda e, ot=ot, j=j: e.tensor_tensor(
                        ffn_stage[:, j, ot * 512:(ot + 1) * 512], ps_q[:, j * 512:(j + 1) * 512],
                        dsb[:, ot * 512:(ot + 1) * 512], ALU.mult), inc="v")
                v_p4d = C["v"]
            tap("ffns", g, r4.bitcast(F32)[:, 0:NCH * D])
            st_prev = 0
            for j in range(NCH):
                wait("scalar", "v", v_p4d)
                ln_stats(ffn_stage[:, j, :], D, lnchunk)
                wait("vector", "s")
                wait("vector", "st", st_prev)
                op("vector", lambda e, j=j: e.tensor_tensor(outst, xg[:, j, :], lnchunk, ALU.add), inc="v")
                wait("sync", "v")
                dma("sync", out_t[seq, t0 + j * CHUNK:t0 + (j + 1) * CHUNK, :], outst, "st")
                st_prev = C["st"]
            P.barrier()

        wait("sync", "st", C["st"])

        blk = e(nc.Block())
        blk.sync(lambda eng: [f(eng) for f in P.streams["sync"]])
        blk.tensor(lambda eng: [f(eng) for f in P.streams["tensor"]])
        blk.vector(lambda eng: [f(eng) for f in P.streams["vector"]])
        blk.scalar(lambda eng: [f(eng) for f in P.streams["scalar"]])
        blk.gpsimd(lambda eng: [f(eng) for f in P.streams["gpsimd"]])

    return nc


def _host_arrays(inputs):
    f32 = np.float32
    am = {}
    am["wipT"] = np.ascontiguousarray(inputs["L0_attn_in_proj_w"].T.astype(f32, copy=False))
    am["wopT"] = np.ascontiguousarray(inputs["L0_attn_out_proj_w"].T.astype(f32, copy=False))
    am["wgT"] = np.ascontiguousarray(inputs["L0_ffn_gate_w"].T.astype(f32, copy=False))
    am["wuT"] = np.ascontiguousarray(inputs["L0_ffn_up_w"].T.astype(f32, copy=False))
    am["wdT"] = np.ascontiguousarray(inputs["L0_ffn_down_w"].T.astype(f32, copy=False))
    am["convw"] = np.ascontiguousarray(
        inputs["L0_attn_conv_w"].reshape(NCT, 128, 4).transpose(1, 0, 2).reshape(128, NCT * 4))
    am["convb"] = np.ascontiguousarray(inputs["L0_attn_conv_b"].reshape(NCT, 128).T)
    am["dtbias"] = np.ascontiguousarray(inputs["L0_attn_dt_bias"].reshape(64, 1))
    am["alog"] = np.ascontiguousarray(inputs["L0_attn_A_log"].reshape(64, 1))
    am["drow"] = np.ascontiguousarray(inputs["L0_attn_D"].reshape(1, 64))
    am["nwrow"] = np.ascontiguousarray(inputs["L0_attn_norm_w"].reshape(1, DIN))
    am["dsrow"] = np.ascontiguousarray(inputs["L0_ffn_down_s"].reshape(1, D))
    am["gs"] = np.ascontiguousarray(inputs["L0_ffn_gate_s"].reshape(44, 128).T)
    am["us"] = np.ascontiguousarray(inputs["L0_ffn_up_s"].reshape(44, 128).T)
    am["idf"] = np.eye(64, dtype=f32)
    k2c = np.zeros((10, 1280), np.float32)
    k2c[8:10, 0:256] = 1.0
    for m in range(4):
        k2c[m, 256 + m * 128:256 + (m + 1) * 128] = 1.0
        k2c[4 + m, 256 + m * 128:256 + (m + 1) * 128] = 1.0
        k2c[m, 768 + m * 128:768 + (m + 1) * 128] = 1.0
        k2c[4 + m, 768 + m * 128:768 + (m + 1) * 128] = 1.0
    am["k2c"] = k2c.astype(ml_dtypes.bfloat16)
    am["idb"] = np.eye(128, dtype=f32)
    am["mask01"] = np.triu(np.ones((128, 128), dtype=f32))
    return am


def kernel(**inputs):
    x = np.asarray(inputs["L0_attn_hidden_states"], dtype=np.float32)
    am = _host_arrays({k: np.asarray(v) for k, v in inputs.items()})
    nc = build_program()
    in_maps = []
    for c in range(8):
        m = dict(am)
        m["x"] = np.ascontiguousarray(x[2 * c:2 * c + 2])
        in_maps.append(m)
    res = run_bass_kernel_spmd(nc, in_maps, core_ids=list(range(8)))
    return np.concatenate([r["out"] for r in res.results], axis=0)


if __name__ == "__main__":
    pass


# version 15
# speedup vs baseline: 1.0391x; 1.0206x over previous
import numpy as np
import ml_dtypes
from contextlib import ExitStack

import concourse.bass as bass
import concourse.mybir as mybir
from concourse.bass_utils import run_bass_kernel_spmd

F32 = mybir.dt.float32
BF16 = mybir.dt.bfloat16
FP8 = mybir.dt.float8e4
AF = mybir.ActivationFunctionType
ALU = mybir.AluOpType
AX = mybir.AxisListType

NSEQ, L, D, DIN = 2, 2048, 2048, 4096
CONVD, PROJ, INTER, CHUNK, GTOK = 4352, 8512, 5632, 128, 512
NCH, NGRP, EPS = GTOK // CHUNK, L // GTOK, 1e-5
NCT, NXT = CONVD // 128, DIN // 128
NGROUPS_FULL = NGRP * NSEQ


class Prog:
    def __init__(self):
        self.streams = {k: [] for k in ("sync", "tensor", "vector", "scalar", "gpsimd")}
        self.C = {k: 0 for k in ("ld", "l2", "wq", "wqa", "wqb", "xq", "fqa", "fqb",
                                 "fqc", "fqd", "gq", "t", "v", "s", "g", "y", "st",
                                 "wsa", "wsb", "wsc", "wsd")}
        self.sems = {}

    def emit(self, eng, fn):
        self.streams[eng].append(fn)

    SELF_SEM = {"vector": "v", "scalar": "s"}

    def op(self, eng, make, inc=None, inc_n=1):
        own = self.SELF_SEM.get(eng)
        if own is not None:
            self.wait(eng, own)
            if inc is None:
                inc, inc_n = own, 1
        if inc is not None:
            self.C[inc] += inc_n
        sems = self.sems

        def f(e, make=make, inc=inc, inc_n=inc_n):
            i = make(e)
            if inc is not None:
                i.then_inc(sems[inc], inc_n)
        self.emit(eng, f)

    def wait(self, eng, sem, val=None):
        if val is None:
            val = self.C[sem]
        if val <= 0:
            return
        sems = self.sems
        self.emit(eng, lambda e, sem=sem, val=val: e.wait_ge(sems[sem], val))

    def inc_only(self, eng, sem, n=1):
        self.C[sem] += n
        sems = self.sems
        def f(e, sem=sem, n=n):
            e.drain()
            e.sem_inc(sems[sem], n)
        self.emit(eng, f)

    def barrier(self):
        names = {"tensor": "t", "vector": "v", "scalar": "s", "gpsimd": "g", "sync": "y"}
        for eng, nm in names.items():
            self.inc_only(eng, nm)
        for eng in names:
            for other, onm in names.items():
                if other != eng:
                    self.wait(eng, onm)


def build_program(n_groups=NGROUPS_FULL, taps=()):
    nc = bass.Bass("TRN2", target_bir_lowering=False, debug=False)
    taps = list(taps)

    x_in = nc.dram_tensor("x", [NSEQ, L, D], F32, kind="ExternalInput").ap()
    out_t = nc.dram_tensor("out", [NSEQ, L, D], F32, kind="ExternalOutput").ap()
    wipT = nc.dram_tensor("wipT", [D, PROJ], F32, kind="ExternalInput").ap()
    wopT = nc.dram_tensor("wopT", [DIN, D], F32, kind="ExternalInput").ap()
    wgT = nc.dram_tensor("wgT", [D, INTER], F32, kind="ExternalInput").ap()
    wuT = nc.dram_tensor("wuT", [D, INTER], F32, kind="ExternalInput").ap()
    wdT = nc.dram_tensor("wdT", [INTER, D], F32, kind="ExternalInput").ap()
    convw_in = nc.dram_tensor("convw", [128, NCT * 4], F32, kind="ExternalInput").ap()
    convb_in = nc.dram_tensor("convb", [128, NCT], F32, kind="ExternalInput").ap()
    dtb_in = nc.dram_tensor("dtbias", [64, 1], F32, kind="ExternalInput").ap()
    alog_in = nc.dram_tensor("alog", [64, 1], F32, kind="ExternalInput").ap()
    drow_in = nc.dram_tensor("drow", [1, 64], F32, kind="ExternalInput").ap()
    nwrow_in = nc.dram_tensor("nwrow", [1, DIN], F32, kind="ExternalInput").ap()
    dsrow_in = nc.dram_tensor("dsrow", [1, D], F32, kind="ExternalInput").ap()
    gs_in = nc.dram_tensor("gs", [128, 44], F32, kind="ExternalInput").ap()
    us_in = nc.dram_tensor("us", [128, 44], F32, kind="ExternalInput").ap()
    idf_in = nc.dram_tensor("idf", [64, 64], F32, kind="ExternalInput").ap()
    idb_in = nc.dram_tensor("idb", [128, 128], F32, kind="ExternalInput").ap()
    mask_in = nc.dram_tensor("mask01", [128, 128], F32, kind="ExternalInput").ap()
    k2c_in = nc.dram_tensor("k2c", [10, 1280], BF16, kind="ExternalInput").ap()
    wbf = nc.dram_tensor("wbf", [99, 128, 2048], BF16, kind="Internal").ap()
    wbf8 = nc.dram_tensor("wbf8", [132, 128, 2048], FP8, kind="Internal").ap()

    dbg_shapes = {
        "xbcraw": ([128, NCT * 516], BF16), "dtTg": ([128, GTOK], F32),
        "sgate": ([128, NCH * DIN], BF16), "xbcact": ([128, NCT * 512], BF16),
        "yf": ([128, DIN], BF16), "ynT": ([128, NXT * GTOK], BF16),
        "S": ([128, DIN], F32), "x1": ([128, NCH * D], BF16),
        "hT": ([128, 44 * GTOK], BF16), "ffns": ([128, NCH * D], F32),
        "ln0T": ([128, 16 * GTOK], BF16), "xact": ([128, DIN], BF16),
        "attns": ([128, NCH * D], F32), "sm64": ([128, 64], F32), "sm64b": ([128, 64], BF16),
        "sm128": ([128, 128], F32), "mqd": ([128, 1024], BF16), "expd": ([128, 1024], F32),
    }
    dbg = {}
    for tag, gi in taps:
        sh, dt = dbg_shapes[tag]
        dbg[(tag, gi)] = nc.dram_tensor(f"dbg_{tag}_{gi}", sh, dt, kind="ExternalOutput").ap()

    P = Prog()
    op, wait, C = P.op, P.wait, P.C

    with ExitStack() as ctx:
        e = ctx.enter_context

        def sb(name, shape, dt):
            return e(nc.sbuf_tensor(name, shape, dt))

        r1 = sb("r1", [128, 16 * GTOK], BF16)
        ln0T = r1[:, :].rearrange("p (k t) -> p k t", t=GTOK)
        x_act = r1[:, 0:DIN]
        xs_t = r1[:, DIN:2 * DIN]

        R23B = 32768 + 36928
        r23 = sb("r23", [128, R23B], mybir.dt.uint8)
        r23b, r23f = r23.bitcast(BF16), r23.bitcast(F32)
        sgate = r23b[:, 0:NCH * DIN].rearrange("p (j o) -> p j o", o=DIN)
        attn_stage = r23f[:, 0:NCH * D].rearrange("p (j o) -> p j o", o=D)
        hT = r23b[:, 0:44 * GTOK].rearrange("p (k t) -> p k t", t=GTOK)
        xbcraw = r23b[:, 16384:16384 + NCT * 516].rearrange("p (c t) -> p c t", t=516)
        TB = 32768
        expM = r23f[:, TB // 4:TB // 4 + 1024]
        mq = [r23b[:, (TB + 4096 + i * 2048) // 2:(TB + 4096 + i * 2048) // 2 + 1024] for i in range(2)]
        tqf = r23f[:, (TB + 8192) // 4:(TB + 8192) // 4 + 1024]
        tqb = r23b[:, (TB + 8192) // 2:(TB + 8192) // 2 + 2048]
        yf = r23b[:, (TB + 12288) // 2:(TB + 12288) // 2 + DIN]
        sftmp = r23f[:, (TB + 20480) // 4:(TB + 20480) // 4 + 1024]
        outst = r23f[:, (TB + 24576) // 4:(TB + 24576) // 4 + D]
        lnchunk = r23b[:, (TB + 32768) // 2:(TB + 32768) // 2 + D]
        AG = r23f[:, (TB + 32768) // 4:(TB + 32768) // 4 + GTOK]
        BG = r23f[:, (TB + 32768 + 2048) // 4:(TB + 32768 + 2048) // 4 + GTOK]
        assert TB + 32768 + 2 * D <= R23B

        r4 = sb("r4", [128, NCT * 512], BF16)
        xbcact = r4[:, :].rearrange("p (c t) -> p c t", t=512)
        ynT = r4[:, 0:NXT * GTOK].rearrange("p (k t) -> p k t", t=GTOK)
        ffn_stage = r4.bitcast(F32)[:, 0:NCH * D].rearrange("p (j o) -> p j o", o=D)

        xg_flat = sb("xg", [128, NCH * D], BF16)[:, :]
        xg = xg_flat.rearrange("p (j o) -> p j o", o=D)
        S_f = sb("S_f", [128, DIN], F32)[:, :]
        S_b = sb("S_b", [128, DIN], BF16)[:, :]
        dtTg = sb("dtTg", [128, GTOK], F32)[:, :]
        acsb = sb("acsb", [128, 2048], F32)

        statW = [sb(f"statW{i}", [128, 16 * 128], BF16) for i in range(2)]
        movW = [sb(f"movW{i}", [128, 4 * 512], BF16) for i in range(2)]
        statWv = [w[:, :].rearrange("p (k o) -> p k o", o=128) for w in statW]
        movWv = [w[:, :].rearrange("p (k o) -> p k o", o=512) for w in movW]
        movW_as_stat = [w[:, 0:2048].rearrange("p (k o) -> p k o", o=128) for w in movW]
        statW_as_mov = [w[:, :].rearrange("p (k o) -> p k o", o=512) for w in statW]
        statW8f = [w.bitcast(FP8)[:, 0:2048] for w in statW]
        movW8f = [w.bitcast(FP8)[:, 0:2048] for w in movW]
        statW8_stat = [v.rearrange("p (k o) -> p k o", o=128) for v in statW8f]
        movW8_stat = [v.rearrange("p (k o) -> p k o", o=128) for v in movW8f]
        movW8_mov = [v.rearrange("p (k o) -> p k o", o=512) for v in movW8f]

        idf = sb("idf_c", [64, 64], F32)[:, :]
        idb = sb("idb_c", [128, 128], BF16)[:, :]
        mask01 = sb("mask_c", [128, 128], BF16)[:, :]
        convw = sb("convw_c", [128, NCT * 4], F32)[:, :].rearrange("p (c k) -> p c k", k=4)
        convb = sb("convb_c", [128, NCT], F32)[:, :]
        dtbias = sb("dtbias_c", [128, 1], F32)[:, :]
        alog = sb("alog_c", [128, 1], F32)[:, :]
        negA = sb("negA", [128, 1], F32)[:, :]
        gscale = sb("gscale", [128, 44], F32)[:, :]
        uscale = sb("uscale", [128, 44], F32)[:, :]
        D_b = sb("D_b", [128, 64], BF16)[:, :]
        normw_b = sb("normw_b", [128, DIN], BF16)[:, :]
        dsb = sb("dsb", [128, D], F32)[:, :]
        acsb_bf = acsb.bitcast(BF16)
        K2BASE = (0, 32, 0, 32)
        K2COL = (0, 0, 1280, 1280)
        K2SEM = ("fqa", "fqb", "fqc", "fqd")
        dtTg_bf = dtTg.bitcast(BF16)
        AGhi_g = dtTg_bf[0:64, 0:512]
        AGlo_g = dtTg_bf[0:64, 512:1024]
        BGhi_g = acsb_bf[0:64, 2560:3072]
        BGlo_g = acsb_bf[0:64, 3072:3584]
        onesrow = acsb[64:65, 0:128]
        zeros64 = acsb[64:128, 256:384]

        dtg4 = sb("dtg4", [128, GTOK], F32)[:, :]
        dtg5 = sb("dtg5", [128, GTOK], F32)[:, :]
        dsdt_b = sb("dsdt_b", [128, 64], BF16)[:, :]
        e1 = sb("e1c", [128, 64], F32)[:, :]
        gam = sb("gam", [128, 64], F32)[:, :]
        Gm = sb("Gm", [128, 128], F32)[:, :]
        B_sb = sb("B_sb", [128, 128], BF16)[:, :]
        stats = sb("stats", [128, 16], F32)[:, :]
        halo_sb = sb("halo_sb", [128, NCT * 3], BF16)[:, :].rearrange("p (c t) -> p c t", t=3)
        gmr = acsb[64:65, 0:192]
        c30 = sb("c30", [128, 1], F32)[:, :]
        sgtmp = r23b[:, (TB + 12288) // 2:(TB + 12288) // 2 + 512]
        sgtmp2 = r23b[:, (TB + 13312) // 2:(TB + 13312) // 2 + 512]
        convacc = None

        ps_tr = e(nc.psum_tensor("ps_tr", [128, 1024], BF16))[:, :]
        ps_a = e(nc.psum_tensor("ps_a", [128, 512], F32))[:, :]
        ps_b = e(nc.psum_tensor("ps_b", [128, 512], F32))[:, :]
        ps_q = e(nc.psum_tensor("ps_q", [128, 2048], F32))[:, :]
        ps_x = e(nc.psum_tensor("ps_x", [128, 512], F32))[:, :]
        convacc = [ps_tr.bitcast(F32)[:, 0:512], ps_x]

        for s_ in ("ld", "l2", "wq", "wqa", "wqb", "xq", "fqa", "fqb", "fqc", "fqd",
                   "gq", "t", "v", "s", "g", "y", "st", "wsa", "wsb", "wsc", "wsd"):
            P.sems[s_] = e(nc.semaphore(s_))

        def dma(queue, dst, src, sem):
            op(queue, lambda e, d=dst, s2=src: e.dma_start(out=d, in_=s2), inc=sem, inc_n=16)

        def ln_stats(src, W, nrm_out):
            wait("scalar", "v")
            wait("scalar", "t")
            wait("vector", "s")
            op("scalar", lambda e, src=src: e.activation(nrm_out, src, AF.Square,
                                                         accum_out=stats[:, 0:1]), inc="s")
            op("vector", lambda e, src=src: e.reduce_sum(stats[:, 1:2], src, AX.X), inc="v")
            op("vector", lambda e: e.tensor_scalar(stats[:, 2:3], stats[:, 1:2], 1.0 / W, None, ALU.mult), inc="v")
            op("vector", lambda e: e.tensor_tensor(stats[:, 3:4], stats[:, 2:3], stats[:, 2:3], ALU.mult), inc="v")
            wait("vector", "s")
            op("vector", lambda e: e.tensor_scalar(stats[:, 4:5], stats[:, 0:1], 1.0 / W, None, ALU.mult), inc="v")
            op("vector", lambda e: e.tensor_tensor(stats[:, 4:5], stats[:, 4:5], stats[:, 3:4], ALU.subtract), inc="v")
            op("vector", lambda e: e.tensor_scalar(stats[:, 4:5], stats[:, 4:5], EPS, None, ALU.add), inc="v")
            wait("scalar", "v")
            op("scalar", lambda e: e.activation(stats[:, 5:6], stats[:, 4:5], AF.Sqrt), inc="s")
            wait("vector", "s")
            op("vector", lambda e: e.reciprocal(stats[:, 6:7], stats[:, 5:6]), inc="v")
            op("vector", lambda e: e.tensor_tensor(stats[:, 7:8], stats[:, 2:3], stats[:, 6:7], ALU.mult), inc="v")
            op("vector", lambda e: e.tensor_scalar(stats[:, 7:8], stats[:, 7:8], -1.0, None, ALU.mult), inc="v")
            op("vector", lambda e, src=src: e.tensor_scalar(
                nrm_out, src, stats[:, 6:7], stats[:, 7:8], ALU.mult, ALU.add), inc="v")

        def transpose_batch(src_tiles, dst_fn):
            i, n = 0, len(src_tiles)
            while i < n:
                m = min(8, n - i)
                wait("tensor", "v")
                for k in range(m):
                    op("tensor", lambda e, s2=src_tiles[i + k], k=k:
                       e.transpose(ps_tr[:, k * 128:(k + 1) * 128], s2, idb), inc="t")
                wait("vector", "t")
                op("vector", lambda e, i0=i, m=m: e.tensor_copy(
                    dst_fn(i0, m), ps_tr[:, 0:m * 128].rearrange("p (n t) -> p n t", t=128)), inc="v")
                i += m

        def tap(tag, gi, src_ap):
            if (tag, gi) in dbg:
                P.barrier()
                dma("sync", dbg[(tag, gi)], src_ap, "st")
                wait("sync", "st")
                P.barrier()

        for dst, src in ((idf, idf_in),
                         (convw, convw_in.rearrange("p (c k) -> p c k", k=4)),
                         (convb, convb_in), (dtbias[0:64, :], dtb_in), (alog[0:64, :], alog_in),
                         (gscale, gs_in), (uscale, us_in)):
            dma("sync", dst, src, "ld")
        dma("gpsimd", idb, idb_in, "wq")
        dma("gpsimd", mask01, mask_in, "wq")
        wait("vector", "ld")
        for z in (r1[:, :], r23.bitcast(F32)[:, 0:R23B // 4], r4[:, :],
                  acsb[:, :], dtTg, S_f, S_b, stats, dtg4, dtg5,
                  dsdt_b, e1, gam, Gm, B_sb, xg_flat):
            op("vector", lambda e, z=z: e.memset(z, 0.0), inc="v")
        op("vector", lambda e: e.memset(onesrow, 1.0), inc="v")
        op("vector", lambda e: e.memset(c30, 30.0), inc="v")
        wait("scalar", "ld")
        op("scalar", lambda e: e.activation(negA[0:64, :], alog[0:64, :], AF.Exp), inc="s")
        wait("vector", "s")
        op("vector", lambda e: e.tensor_scalar(negA[0:64, :], negA[0:64, :], -1.0, None, ALU.mult), inc="v")
        stages = []
        for i in range(4):
            stages.append((nwrow_in[0:1, i * 1024:(i + 1) * 1024], 1024,
                           (lambda i=i: normw_b[:, i * 1024:(i + 1) * 1024])))
        for i in range(2):
            stages.append((dsrow_in[0:1, i * 1024:(i + 1) * 1024], 1024,
                           (lambda i=i: dsb[:, i * 1024:(i + 1) * 1024])))
        stages.append((drow_in, 64, (lambda: D_b)))
        datarow = acsb[64:65, 128:128 + 1024]
        wait("sync", "v")
        for src, W, dstf in stages:
            wait("sync", "t")
            dma("sync", datarow[0:1, 0:W], src, "l2")
            wait("tensor", "l2")
            wait("tensor", "v")
            nmm = (W + 511) // 512
            for i in range(nmm):
                w2 = min(512, W - i * 512)
                op("tensor", lambda e, i=i, w2=w2: e.matmul(
                    ps_q[:, i * 512:i * 512 + w2], onesrow,
                    datarow[0:1, i * 512:i * 512 + w2], start=True, stop=True), inc="t")
            wait("vector", "t")
            op("vector", lambda e, dstf=dstf, W=W: e.tensor_copy(dstf(), ps_q[:, 0:W]), inc="v")
        wait("vector", "t")
        wait("sync", "t")
        for k2base, k2c0 in ((0, 0), (32, 0), (0, 1280), (32, 1280)):
            dma("sync", acsb_bf[k2base:k2base + 10, k2c0:k2c0 + 1280], k2c_in, "l2")
            wait("sync", "l2")
        wait("vector", "l2")

        op("vector", lambda e: e.memset(zeros64, 0.0), inc="v")
        wait("tensor", "wq")
        P.barrier()

        K2T = {}
        K2S = {}
        ps_k2 = [(ps_a, ps_b), (ps_x, ps_tr.bitcast(F32)[:, 0:512])]

        def scan_chunk(j):
            js = slice(j * 128, (j + 1) * 128)
            jlast = j * 128 + 127

            op("vector", lambda e: e.tensor_scalar(dtg5[0:64, js], AG[0:64, js],
                                                   AG[0:64, jlast:jlast + 1], None, ALU.subtract), inc="v")
            wait("scalar", "v")
            op("scalar", lambda e: e.activation(dtg5[0:64, js], dtg5[0:64, js], AF.Exp, scale=-1.0), inc="s")
            wait("vector", "s")
            op("vector", lambda e: e.tensor_tensor(dtg5[0:64, js], dtg5[0:64, js],
                                                   dtg4[0:64, js], ALU.mult), inc="v")
            v_dsdt = C["v"]

            wait("tensor", "v", v_dsdt)
            wait("tensor", "s")
            op("tensor", lambda e: e.transpose(ps_x[:, 0:64], AG[0:64, js], idf), inc="t")
            op("tensor", lambda e: e.transpose(ps_x[:, 64:128], dtg5[0:64, js], idf), inc="t")
            t_tr = C["t"]
            wait("scalar", "t")
            dma("scalar", acsb[64:65, 128:192], AG[0:64, jlast:jlast + 1], "gq")
            fq_gam = C["gq"]
            wait("vector", "t", t_tr)
            op("vector", lambda e: e.tensor_copy(dsdt_b, ps_x[:, 64:128]), inc="v")
            v_tr = C["v"]
            wait("scalar", "t", t_tr)
            op("scalar", lambda e: e.activation(e1, ps_x[:, 0:64], AF.Exp), inc="s")
            s_e1 = C["s"]
            wait("tensor", "gq", fq_gam)
            wait("tensor", "v", v_tr)
            wait("tensor", "s", s_e1)
            op("tensor", lambda e: e.matmul(ps_x[:, 192:256], onesrow, acsb[64:65, 128:192], start=True, stop=True), inc="t")
            t_gam = C["t"]
            wait("scalar", "t", t_gam)
            op("scalar", lambda e: e.activation(gam, ps_x[:, 192:256], AF.Exp), inc="s")
            s_gam = C["s"]

            wait("vector", "s")
            transpose_batch([xbcact[:, d, js] for d in range(NXT)],
                            lambda i0, m: x_act.rearrange("p (n t) -> p n t", t=128)[:, i0:i0 + m, :])
            wait("tensor", "v")
            op("tensor", lambda e: e.transpose(ps_tr[:, 0:128], xbcact[:, 32, js], idb), inc="t")
            wait("vector", "t")
            op("vector", lambda e: e.tensor_copy(B_sb, ps_tr[:, 0:128]), inc="v")
            op("tensor", lambda e: e.matmul(ps_x[:, 256:384], xbcact[:, 32, js], xbcact[:, 33, js],
                                            start=True, stop=True), inc="t")
            t_g = C["t"]
            wait("vector", "t", t_g)
            op("vector", lambda e: e.scalar_tensor_tensor(Gm, ps_x[:, 256:384], 1.0, mask01,
                                                          ALU.mult, ALU.mult), inc="v")
            op("vector", lambda e: e.tensor_tensor(
                xs_t.rearrange("p (h d) -> p h d", d=64), x_act.rearrange("p (h d) -> p h d", d=64),
                dsdt_b.broadcast_to((128, 64, 64)), ALU.mult), inc="v")
            v_gm = C["v"]

            k2_hist = {}
            v_mq_hist = {}

            def emit_flatten(og):
                oct_id = j * 8 + og
                st_ = oct_id % 4
                base, c0, fsem = K2BASE[st_], K2COL[st_], K2SEM[st_]
                wait("sync", "t", K2T.get(oct_id - 4, 0))
                for q in range(2):
                    hq = og * 8 + q * 4
                    dma("sync", acsb_bf[base:base + 4, c0 + q * 128:c0 + (q + 1) * 128],
                        BGhi_g[hq:hq + 4, js], fsem)
                    dma("sync", acsb_bf[base + 4:base + 8, c0 + q * 128:c0 + (q + 1) * 128],
                        BGlo_g[hq:hq + 4, js], fsem)
                    dma("sync", acsb_bf[base + 8:base + 9, c0 + 256 + q * 512:c0 + 768 + q * 512]
                        .rearrange("p (h t) -> p h t", t=128), AGhi_g[hq:hq + 4, js], fsem)
                    dma("sync", acsb_bf[base + 9:base + 10, c0 + 256 + q * 512:c0 + 768 + q * 512]
                        .rearrange("p (h t) -> p h t", t=128), AGlo_g[hq:hq + 4, js], fsem)
                return C[fsem]

            def emit_k2(og, fq_val):
                oct_id = j * 8 + og
                st_ = oct_id % 4
                base, c0, fsem = K2BASE[st_], K2COL[st_], K2SEM[st_]
                wait("tensor", fsem, fq_val)
                wait("tensor", "s", K2S.get(oct_id - 2, 0))
                if og == 1:
                    wait("tensor", "s", s_gam)
                    wait("tensor", "v", v_gm)
                pa, pb = ps_k2[og % 2]
                for q in range(2):
                    psd = pa if q == 0 else pb
                    op("tensor", lambda e, psd=psd, q=q, base=base, c0=c0: e.matmul(
                        psd[:, :], acsb_bf[base:base + 10, c0 + q * 128:c0 + (q + 1) * 128],
                        acsb_bf[base:base + 10, c0 + 256 + q * 512:c0 + 768 + q * 512],
                        start=True, stop=True), inc="t")
                K2T[oct_id] = C["t"]
                k2_hist[og] = C["t"]

            def emit_exps_mq(og):
                oc = og % 2
                pa, pb = ps_k2[og % 2]
                wait("scalar", "t", k2_hist[og])
                wait("scalar", "v")
                op("scalar", lambda e, pa=pa: e.activation(pa, pa, AF.Relu, scale=-1.0, bias=c30), inc="s")
                op("scalar", lambda e, pb=pb: e.activation(pb, pb, AF.Relu, scale=-1.0, bias=c30), inc="s")
                op("scalar", lambda e, pa=pa: e.activation(expM[:, 0:512], pa, AF.Exp, scale=-1.0, bias=c30), inc="s")
                op("scalar", lambda e, pb=pb: e.activation(expM[:, 512:1024], pb, AF.Exp, scale=-1.0, bias=c30), inc="s")
                K2S[j * 8 + og] = C["s"]
                wait("vector", "s")
                wait("vector", "t")
                op("vector", lambda e, oc=oc: e.tensor_tensor(
                    mq[oc][0:128, 0:1024].rearrange("p (h t) -> p h t", t=128),
                    expM.rearrange("p (h t) -> p h t", t=128),
                    Gm.broadcast_to((128, 128, 8)).rearrange("p t h -> p h t"),
                    ALU.mult), inc="v")
                v_mq_hist[og] = C["v"]

            vstate = {"yf": 0}

            def emit_ydiag(og):
                oc = og % 2
                wait("tensor", "v", max(v_mq_hist[og], vstate["yf"]))
                for hh in range(8):
                    h = og * 8 + hh
                    slot = oc * 512 + hh * 64
                    op("tensor", lambda e, oc=oc, hh=hh, h=h, slot=slot: e.matmul(
                        ps_q[:, slot:slot + 64], mq[oc][:, hh * 128:(hh + 1) * 128],
                        x_act[:, h * 64:(h + 1) * 64], start=(hh == 0), stop=False), inc="t")

            fqv = {}
            for og in range(3):
                fqv[og] = emit_flatten(og)
            emit_k2(0, fqv[0])
            emit_exps_mq(0)
            for og in range(1, 8):
                if og + 2 <= 7:
                    fqv[og + 2] = emit_flatten(og + 2)
                emit_k2(og, fqv[og])
                emit_ydiag(og - 1)
                emit_exps_mq(og)
                if og % 2 == 0:
                    quarter_tail(j, og // 2 - 1, s_gam, vstate)
            emit_ydiag(7)
            quarter_tail(j, 3, s_gam, vstate)
            chunk_norm(j)
            return

        def dt_batch():
            dtraw = dtTg[0:64, :]
            for s_ in ("fqa", "fqb", "fqc", "fqd", "gq"):
                wait("scalar", s_)
                wait("vector", s_)
            wait("vector", "t")
            op("vector", lambda e: e.tensor_scalar(dtg4[0:64, :], dtraw, dtbias[0:64, :],
                                                   None, ALU.add), inc="v")
            wait("scalar", "v")
            op("scalar", lambda e: e.activation(dtg5[0:64, :], dtg4[0:64, :], AF.Abs), inc="s")
            op("scalar", lambda e: e.activation(dtg5[0:64, :], dtg5[0:64, :], AF.Exp, scale=-1.0), inc="s")
            op("scalar", lambda e: e.activation(dtg5[0:64, :], dtg5[0:64, :], AF.Ln, bias=1.0), inc="s")
            wait("vector", "s")
            op("vector", lambda e: e.scalar_tensor_tensor(dtg4[0:64, :], dtg4[0:64, :], 0.0,
                                                          dtg5[0:64, :], ALU.max, ALU.add), inc="v")
            wait("scalar", "v")
            op("scalar", lambda e: e.activation(dtg5[0:64, :], dtg4[0:64, :], AF.Ln), inc="s")
            wait("vector", "s")
            op("vector", lambda e: e.tensor_scalar(BG[0:64, :], dtg4[0:64, :], negA[0:64, :],
                                                   None, ALU.mult), inc="v")
            op("vector", lambda e: e.memset(Gm[0:64, :], 0.0), inc="v")
            for jj in range(NCH):
                op("vector", lambda e, jj=jj: e.tensor_tensor_scan(
                    AG[0:64, jj * 128:(jj + 1) * 128], BG[0:64, jj * 128:(jj + 1) * 128],
                    Gm[0:64, :], 0.0, ALU.add, ALU.add), inc="v")
            op("vector", lambda e: e.tensor_tensor(BG[0:64, :], dtg5[0:64, :], AG[0:64, :],
                                                   ALU.subtract), inc="v")
            op("vector", lambda e: e.tensor_copy(AGhi_g, AG[0:64, :]), inc="v")
            op("vector", lambda e: e.tensor_tensor(AGlo_g, AG[0:64, :], AGhi_g, ALU.subtract), inc="v")
            op("vector", lambda e: e.tensor_copy(BGhi_g, BG[0:64, :]), inc="v")
            op("vector", lambda e: e.tensor_tensor(BGlo_g, BG[0:64, :], BGhi_g, ALU.subtract), inc="v")

        def chunk_norm(j):
            wait("scalar", "v")
            wait("scalar", "t")
            op("scalar", lambda e: e.activation(x_act, yf, AF.Square,
                                                accum_out=stats[:, 8:9]), inc="s")
            wait("vector", "s")
            op("vector", lambda e: e.tensor_scalar(stats[:, 9:10], stats[:, 8:9], 1.0 / DIN, EPS,
                                                   ALU.mult, ALU.add), inc="v")
            v_ss = C["v"]
            wait("scalar", "v", v_ss)
            op("scalar", lambda e: e.activation(stats[:, 10:11], stats[:, 9:10], AF.Ln), inc="s")
            op("scalar", lambda e: e.activation(stats[:, 11:12], stats[:, 10:11], AF.Exp, scale=-0.5), inc="s")
            s_rstd = C["s"]
            wait("vector", "s", s_rstd)
            op("vector", lambda e: e.scalar_tensor_tensor(xs_t, yf, stats[:, 11:12], normw_b,
                                                          ALU.mult, ALU.mult), inc="v")
            transpose_batch([xs_t[:, d * 128:(d + 1) * 128] for d in range(NXT)],
                            lambda i0, m, j=j: ynT[:, i0:i0 + m, j * 128:(j + 1) * 128])

        def quarter_tail(j, q, s_gam, vstate):
            js = slice(j * 128, (j + 1) * 128)
            qs = slice(q * 1024, (q + 1) * 1024)
            for half in range(2):
                op("tensor", lambda e, q=q, half=half: e.matmul(
                    ps_q[:, 1024 + half * 512:1536 + half * 512], xbcact[:, 33, js],
                    S_b[:, q * 1024 + half * 512:q * 1024 + (half + 1) * 512],
                    start=True, stop=True), inc="t")
            t_yoff = C["t"]
            wait("vector", "t", t_yoff)
            op("vector", lambda e, q=q: e.tensor_tensor(
                tqb[:, 0:1024].rearrange("p (h d) -> p h d", d=64),
                ps_q[:, 1024:2048].rearrange("p (h d) -> p h d", d=64),
                e1[:, q * 16:(q + 1) * 16].broadcast_to((128, 16, 64)), ALU.mult), inc="v")
            v_a1 = C["v"]
            op("vector", lambda e, q=q: e.tensor_tensor(
                tqb[:, 1024:2048].rearrange("p (h d) -> p h d", d=64),
                x_act[:, q * 1024:(q + 1) * 1024].rearrange("p (h d) -> p h d", d=64),
                D_b[:, q * 16:(q + 1) * 16].broadcast_to((128, 16, 64)), ALU.mult), inc="v")
            v_a2 = C["v"]
            wait("tensor", "v", v_a2)
            for half in range(2):
                for sl in range(8):
                    slot = half * 512 + sl * 64
                    op("tensor", lambda e, slot=slot: e.matmul(
                        ps_q[:, slot:slot + 64], idb,
                        tqb[:, slot:slot + 64], start=False, stop=False), inc="t")
                    op("tensor", lambda e, slot=slot, sl=sl: e.matmul(
                        ps_q[:, slot:slot + 64], idb,
                        tqb[:, 1024 + slot:1024 + slot + 64],
                        start=False, stop=(sl == 7)), inc="t")
            t_acc = C["t"]
            wait("vector", "t", t_acc)
            op("vector", lambda e, j=j, qs=qs: e.scalar_tensor_tensor(
                yf[:, qs], ps_q[:, 0:1024], 1.0, sgate[:, j, qs], ALU.mult, ALU.mult), inc="v")
            vstate["yf"] = C["v"]
            wait("tensor", "v", v_a1)
            for half in range(2):
                op("tensor", lambda e, q=q, half=half: e.matmul(
                    ps_q[:, 1024 + half * 512:1536 + half * 512], B_sb,
                    xs_t[:, q * 1024 + half * 512:q * 1024 + (half + 1) * 512],
                    start=True, stop=True), inc="t")
            t_st = C["t"]
            wait("vector", "t", t_st)
            wait("vector", "s", s_gam)
            op("vector", lambda e, q=q, qs=qs: e.tensor_tensor(
                sftmp.rearrange("p (h d) -> p h d", d=64),
                S_f[:, qs].rearrange("p (h d) -> p h d", d=64),
                gam[:, q * 16:(q + 1) * 16].broadcast_to((128, 16, 64)), ALU.mult), inc="v")
            op("vector", lambda e, qs=qs: e.tensor_tensor(S_f[:, qs], sftmp, ps_q[:, 1024:2048],
                                                          ALU.add), inc="v")
            wait("scalar", "v")
            op("scalar", lambda e, qs=qs: e.copy(S_b[:, qs], S_f[:, qs]), inc="s")


        sid_map = {}
        store_pending = {}
        WSEM = {("stat", 0): "wsa", ("stat", 1): "wsb", ("mov", 0): "wsc", ("mov", 1): "wsd"}
        BUF_FLAT = {("stat", 0): statW[0][:, :], ("stat", 1): statW[1][:, :],
                    ("mov", 0): movW[0][:, :], ("mov", 1): movW[1][:, :]}
        BUF_FLAT8 = {("stat", 0): statW8f[0], ("stat", 1): statW8f[1],
                     ("mov", 0): movW8f[0], ("mov", 1): movW8f[1]}
        sid8_map = {}

        def war_wait(buf_id):
            if buf_id in store_pending:
                sem_, val_ = store_pending.pop(buf_id)
                wait("gpsimd", sem_, val_)

        def emit_store(key, buf_id, qsem):
            sid = sid_map.setdefault(key, len(sid_map))
            ssem = WSEM[buf_id]
            wait("sync", qsem, C[qsem])
            dma("sync", wbf[sid], BUF_FLAT[buf_id], ssem)
            store_pending[buf_id] = (ssem, C[ssem])

        t_slab = {}
        xq_pref = {}
        p1a_pref = {}
        for g in range(n_groups):
            seq, grp = divmod(g, NGRP)
            t0 = grp * GTOK
            if g:
                wait("vector", "st", C["st"])
                wait("scalar", "st", C["st"])
            if grp == 0:
                op("vector", lambda e: e.memset(S_f, 0.0), inc="v")
                op("vector", lambda e: e.memset(S_b, 0.0), inc="v")
                op("vector", lambda e: e.memset(halo_sb, 0.0), inc="v")

            if g in xq_pref:
                ld_x = xq_pref.pop(g)
            else:
                for j in range(NCH):
                    dma("gpsimd", xg[:, j, :], x_in[seq, t0 + j * CHUNK:t0 + (j + 1) * CHUNK, :], "xq")
                ld_x = C["xq"]
            wait("scalar", "xq", ld_x)
            wait("vector", "xq", ld_x)
            for j in range(NCH):
                ln_stats(xg[:, j, :], D, lnchunk)
                wait("tensor", "s")
                transpose_batch([lnchunk[:, k * 128:(k + 1) * 128] for k in range(16)],
                                lambda i0, m, j=j: ln0T[:, i0:i0 + m, j * 128:(j + 1) * 128])
            tap("ln0T", g, r1[:, :])

            v_ln0 = C["v"]
            evac_cnt = {0: 0, 1: 0}
            for ot in range(NCT + 1):
                buf = ot % 2
                width = 128 if ot < NCT else 64
                c0 = 4096 + ot * 128
                wsem = "wqa" if buf == 0 else "wqb"
                if (g, ot) in p1a_pref:
                    wait("tensor", wsem, p1a_pref.pop((g, ot)))
                    psd = ps_a if buf == 0 else ps_b
                    wait("tensor", "v", max(v_ln0, evac_cnt[buf]))
                    for k in range(16):
                        op("tensor", lambda e, k=k, buf=buf, w=width, psd=psd: e.matmul(
                            psd[0:w, :], statWv[buf][:, k, 0:w], ln0T[:, k, :],
                            start=(k == 0), stop=(k == 15)), inc="t")
                    t_slab[("p1a", g, ot)] = C["t"]
                    wait("vector", "t", C["t"])
                    op("vector", lambda e, ot=ot, psd=psd: e.tensor_copy(xbcraw[:, ot, 4:516], psd), inc="v")
                    evac_cnt[buf] = C["v"]
                    continue
                wait("gpsimd", "t", t_slab.get(("p1a", g, ot - 2), 0))
                if g == 0:
                    war_wait(("stat", buf))
                    dma("gpsimd", statWv[buf][:, :, 0:width],
                        wipT[:, c0:c0 + width].rearrange("(k p) o -> p k o", p=128), wsem)
                    emit_store(("p1a", ot), ("stat", buf), wsem)
                else:
                    war_wait(("stat", buf))
                    dma("gpsimd", statW[buf][:, :], wbf[sid_map[("p1a", ot)]], wsem)
                wait("tensor", wsem, C[wsem])
                wait("tensor", "v", max(v_ln0, evac_cnt[buf]))
                psd = ps_a if buf == 0 else ps_b
                for k in range(16):
                    op("tensor", lambda e, k=k, buf=buf, w=width, psd=psd: e.matmul(
                        psd[0:w, :], statWv[buf][:, k, 0:w], ln0T[:, k, :],
                        start=(k == 0), stop=(k == 15)), inc="t")
                t_slab[("p1a", g, ot)] = C["t"]
                wait("vector", "t", C["t"])
                if ot < NCT:
                    op("vector", lambda e, ot=ot, psd=psd: e.tensor_copy(xbcraw[:, ot, 4:516], psd), inc="v")
                else:
                    op("vector", lambda e, psd=psd: e.tensor_copy(dtTg[0:64, :], psd[0:64, :]), inc="v")
                evac_cnt[buf] = C["v"]
            tap("xbcraw", g, r23b[:, 16384:16384 + NCT * 516])
            tap("dtTg", g, dtTg)

            op("vector", lambda e: e.tensor_copy(xbcraw[:, :, 1:4], halo_sb), inc="v")
            op("vector", lambda e: e.tensor_copy(halo_sb, xbcraw[:, :, 513:516]), inc="v")

            def conv_ct(ct):
                wait("vector", "s")
                op("vector", lambda e, ct=ct: e.tensor_scalar(
                    convacc[0], xbcraw[:, ct, 1:513], convw[:, ct, 0:1], None, ALU.mult), inc="v")
                op("vector", lambda e, ct=ct: e.scalar_tensor_tensor(
                    convacc[1], xbcraw[:, ct, 2:514], convw[:, ct, 1:2], convacc[0],
                    ALU.mult, ALU.add), inc="v")
                op("vector", lambda e, ct=ct: e.scalar_tensor_tensor(
                    convacc[0], xbcraw[:, ct, 3:515], convw[:, ct, 2:3], convacc[1],
                    ALU.mult, ALU.add), inc="v")
                op("vector", lambda e, ct=ct: e.scalar_tensor_tensor(
                    convacc[1], xbcraw[:, ct, 4:516], convw[:, ct, 3:4], convacc[0],
                    ALU.mult, ALU.add), inc="v")
                wait("scalar", "v")
                op("scalar", lambda e, ct=ct: e.activation(
                    xbcact[:, ct, :], convacc[1], AF.Silu, bias=convb[:, ct:ct + 1]), inc="s")

            conv_plan = [[31, 32, 33, 0]] + [list(range(4 * o - 3, 4 * o + 1)) for o in range(1, 8)]
            conv_plan[7] = list(range(25, 31))
            s_gate_evac = 0
            for ot in range(8):
                for ks in range(4):
                    buf = ks % 2
                    wait("gpsimd", "t", t_slab.get(("p1b", ot, ks - 2),
                                                   t_slab.get(("p1b", ot - 1, ks + 2),
                                                              t_slab.get(("p1a", g, NCT - ks), 0))))
                    wsem = "wqa" if buf == 0 else "wqb"
                    if g == 0:
                        war_wait(("mov", buf))
                        dma("gpsimd", movWv[buf],
                            wipT[ks * 512:(ks + 1) * 512, ot * 512:(ot + 1) * 512]
                            .rearrange("(k p) o -> p k o", p=128), wsem)
                        emit_store(("p1b", ot, ks), ("mov", buf), wsem)
                    else:
                        war_wait(("mov", buf))
                        dma("gpsimd", movW[buf][:, :], wbf[sid_map[("p1b", ot, ks)]], wsem)
                    wait("tensor", wsem, C[wsem])
                    if ks == 0 and s_gate_evac:
                        wait("tensor", "s", s_gate_evac)
                    for kk in range(4):
                        k = ks * 4 + kk
                        for j in range(NCH):
                            op("tensor", lambda e, k=k, kk=kk, j=j, buf=buf: e.matmul(
                                ps_q[:, j * 512:(j + 1) * 512],
                                ln0T[:, k, j * 128:(j + 1) * 128], movWv[buf][:, kk, :],
                                start=(k == 0), stop=(k == 15)), inc="t")
                    t_slab[("p1b", ot, ks)] = C["t"]
                for ct in conv_plan[ot]:
                    conv_ct(ct)
                if ot == 0:
                    dt_batch()
                wait("scalar", "t", C["t"])
                for j in range(NCH):
                    op("scalar", lambda e, ot=ot, j=j: e.activation(
                        sgate[:, j, ot * 512:(ot + 1) * 512], ps_q[:, j * 512:(j + 1) * 512],
                        AF.Silu), inc="s")
                s_gate_evac = C["s"]
            P.barrier()
            tap("sgate", g, r23b[:, 0:NCH * DIN])
            tap("xbcact", g, r4[:, :])
            tap("dtTg", 50 + g, AG)
            tap("dtTg", 60 + g, BG)
            tap("dtTg", 70 + g, dtg4)

            for j in range(NCH):
                scan_chunk(j)
                if any(k[1] == g * 10 + j for k in dbg):
                    P.barrier()
                tap("yf", g * 10 + j, yf)
                tap("xact", g * 10 + j, x_act)
                tap("S", g * 10 + j, S_f)
                tap("sm64", g * 10 + j, e1)
                tap("sm64", 100 + g * 10 + j, gam)
                tap("sm128", g * 10 + j, Gm)
                tap("mqd", g * 10 + j, mq[1])
                tap("expd", g * 10 + j, expM)
                tap("sm64b", 200 + g * 10 + j, dsdt_b)
                tap("xact", 300 + g * 10 + j, xs_t)
            tap("ynT", g, r4[:, 0:NXT * GTOK])

            v_ynT = C["v"]
            s_p3 = 0
            for ot in range(4):
                for ks in range(8):
                    buf = ks % 2
                    wait("gpsimd", "t", t_slab.get(("p3", ot, ks - 2), t_slab.get(("p3", ot - 1, ks + 6), 0)))
                    wsem = "wqa" if buf == 0 else "wqb"
                    if g == 0:
                        war_wait(("mov", buf))
                        dma("gpsimd", movWv[buf],
                            wopT[ks * 512:(ks + 1) * 512, ot * 512:(ot + 1) * 512]
                            .rearrange("(k p) o -> p k o", p=128), wsem)
                        emit_store(("p3", ot, ks), ("mov", buf), wsem)
                    else:
                        war_wait(("mov", buf))
                        dma("gpsimd", movW[buf][:, :], wbf[sid_map[("p3", ot, ks)]], wsem)
                    wait("tensor", wsem, C[wsem])
                    if ks == 0:
                        wait("tensor", "v", v_ynT)
                        if s_p3:
                            wait("tensor", "s", s_p3)
                    for kk in range(4):
                        k = ks * 4 + kk
                        for j in range(NCH):
                            op("tensor", lambda e, k=k, kk=kk, j=j, buf=buf: e.matmul(
                                ps_q[:, j * 512:(j + 1) * 512],
                                ynT[:, k, j * 128:(j + 1) * 128], movWv[buf][:, kk, :],
                                start=(k == 0), stop=(k == 31)), inc="t")
                    t_slab[("p3", ot, ks)] = C["t"]
                wait("scalar", "t", C["t"])
                for j in range(NCH):
                    op("scalar", lambda e, ot=ot, j=j: e.copy(
                        attn_stage[:, j, ot * 512:(ot + 1) * 512], ps_q[:, j * 512:(j + 1) * 512]), inc="s")
                s_p3 = C["s"]
            tap("attns", g, r23f[:, 0:NCH * D])
            for j in range(NCH):
                wait("scalar", "s", s_p3)
                wait("vector", "s", s_p3)
                ln_stats(attn_stage[:, j, :], D, lnchunk)
                wait("vector", "s")
                op("vector", lambda e, j=j: e.tensor_tensor(xg[:, j, :], xg[:, j, :], lnchunk, ALU.add), inc="v")
            tap("x1", g, xg_flat)

            for j in range(NCH):
                ln_stats(xg[:, j, :], D, lnchunk)
                wait("tensor", "s")
                transpose_batch([lnchunk[:, k * 128:(k + 1) * 128] for k in range(16)],
                                lambda i0, m, j=j: ln0T[:, i0:i0 + m, j * 128:(j + 1) * 128])
            tap("ln0T", 100 + g, r1[:, :])
            v_ln2 = C["v"]
            vu_evac = 0
            for ot in range(44):
                pair = ot % 2
                gsl = statW8_stat[0] if pair == 0 else movW8_stat[0]
                usl = statW8_stat[1] if pair == 0 else movW8_stat[1]
                wait("gpsimd", "t", t_slab.get(("p4gu", g, ot - 2),
                                               t_slab.get(("p3", 3, 6 + ot), 0)))
                wsem = "wqa" if pair == 0 else "wqb"
                gb = ("stat", 0) if pair == 0 else ("mov", 0)
                ub = ("stat", 1) if pair == 0 else ("mov", 1)
                if g == 0:
                    war_wait(gb)
                    war_wait(ub)
                    dma("gpsimd", gsl,
                        wgT[:, ot * 128:(ot + 1) * 128].rearrange("(k p) o -> p k o", p=128), wsem)
                    dma("gpsimd", usl,
                        wuT[:, ot * 128:(ot + 1) * 128].rearrange("(k p) o -> p k o", p=128), wsem)
                    sid_g = sid8_map.setdefault(("gu_g", ot), len(sid8_map))
                    sid_u = sid8_map.setdefault(("gu_u", ot), len(sid8_map))
                    wait("sync", wsem, C[wsem])
                    dma("sync", wbf8[sid_g], BUF_FLAT8[gb], WSEM[gb])
                    store_pending[gb] = (WSEM[gb], C[WSEM[gb]])
                    dma("sync", wbf8[sid_u], BUF_FLAT8[ub], WSEM[ub])
                    store_pending[ub] = (WSEM[ub], C[WSEM[ub]])
                else:
                    war_wait(gb)
                    war_wait(ub)
                    dma("gpsimd", BUF_FLAT8[gb], wbf8[sid8_map[("gu_g", ot)]], wsem)
                    dma("gpsimd", BUF_FLAT8[ub], wbf8[sid8_map[("gu_u", ot)]], wsem)
                wait("tensor", wsem, C[wsem])
                wait("tensor", "v", max(v_ln2, vu_evac))
                for k in range(16):
                    op("tensor", lambda e, k=k, gsl=gsl: e.matmul(
                        ps_a, gsl[:, k, :], ln0T[:, k, :], start=(k == 0), stop=(k == 15)), inc="t")
                for k in range(16):
                    op("tensor", lambda e, k=k, usl=usl: e.matmul(
                        ps_b, usl[:, k, :], ln0T[:, k, :], start=(k == 0), stop=(k == 15)), inc="t")
                t_slab[("p4gu", g, ot)] = C["t"]
                wait("scalar", "t", C["t"])
                if vu_evac:
                    wait("scalar", "v", vu_evac)
                op("scalar", lambda e, ot=ot: e.activation(sgtmp, ps_a, AF.Silu,
                                                           scale=gscale[:, ot:ot + 1]), inc="s")
                wait("vector", "s")
                op("vector", lambda e, ot=ot: e.tensor_scalar(sgtmp2, ps_b, uscale[:, ot:ot + 1],
                                                              None, ALU.mult), inc="v")
                op("vector", lambda e, ot=ot: e.tensor_tensor(hT[:, ot, :], sgtmp2, sgtmp, ALU.mult), inc="v")
                vu_evac = C["v"]
            tap("hT", g, r23b[:, 0:44 * GTOK])

            v_p4d = 0
            for ot in range(4):
                for ks in range(11):
                    buf = ks % 2
                    wait("gpsimd", "t", t_slab.get(("p4d", ot, ks - 2),
                                                   t_slab.get(("p4d", ot - 1, 10 - ks),
                                                              t_slab.get(("p4gu", g, 43), 0))))
                    wsem = "wqa" if buf == 0 else "wqb"
                    if g == 0:
                        war_wait(("mov", buf))
                        dma("gpsimd", movW8_mov[buf],
                            wdT[ks * 512:(ks + 1) * 512, ot * 512:(ot + 1) * 512]
                            .rearrange("(k p) o -> p k o", p=128), wsem)
                        sid8 = sid8_map.setdefault(("p4d", ot, ks), len(sid8_map))
                        wait("sync", wsem, C[wsem])
                        dma("sync", wbf8[sid8], BUF_FLAT8[("mov", buf)], WSEM[("mov", buf)])
                        store_pending[("mov", buf)] = (WSEM[("mov", buf)], C[WSEM[("mov", buf)]])
                    else:
                        war_wait(("mov", buf))
                        dma("gpsimd", BUF_FLAT8[("mov", buf)], wbf8[sid8_map[("p4d", ot, ks)]], wsem)
                    wait("tensor", wsem, C[wsem])
                    if ot == 0 and ks == 0:
                        wait("tensor", "v", vu_evac)
                    if ks == 0 and v_p4d:
                        wait("tensor", "v", v_p4d)
                    for kk in range(4):
                        k = ks * 4 + kk
                        for j in range(NCH):
                            op("tensor", lambda e, k=k, kk=kk, j=j, buf=buf: e.matmul(
                                ps_q[:, j * 512:(j + 1) * 512],
                                hT[:, k, j * 128:(j + 1) * 128], movW8_mov[buf][:, kk, :],
                                start=(k == 0), stop=(k == 43)), inc="t")
                    t_slab[("p4d", ot, ks)] = C["t"]
                wait("vector", "t", C["t"])
                for j in range(NCH):
                    op("vector", lambda e, ot=ot, j=j: e.tensor_tensor(
                        ffn_stage[:, j, ot * 512:(ot + 1) * 512], ps_q[:, j * 512:(j + 1) * 512],
                        dsb[:, ot * 512:(ot + 1) * 512], ALU.mult), inc="v")
                v_p4d = C["v"]
            tap("ffns", g, r4.bitcast(F32)[:, 0:NCH * D])
            if g + 1 < n_groups:
                wait("gpsimd", "t", t_slab[("p4d", 3, 10)])
                for ot2 in (0, 1):
                    wsem = "wqa" if ot2 == 0 else "wqb"
                    war_wait(("stat", ot2))
                    dma("gpsimd", statW[ot2][:, :], wbf[sid_map[("p1a", ot2)]], wsem)
                    p1a_pref[(g + 1, ot2)] = C[wsem]
            st_prev = 0
            seq2, grp2 = divmod(g + 1, NGRP)
            t02 = grp2 * GTOK
            for j in range(NCH):
                wait("scalar", "v", v_p4d)
                ln_stats(ffn_stage[:, j, :], D, lnchunk)
                wait("vector", "s")
                wait("vector", "st", st_prev)
                op("vector", lambda e, j=j: e.tensor_tensor(outst, xg[:, j, :], lnchunk, ALU.add), inc="v")
                v_out_j = C["v"]
                wait("sync", "v")
                dma("sync", out_t[seq, t0 + j * CHUNK:t0 + (j + 1) * CHUNK, :], outst, "st")
                st_prev = C["st"]
                if g + 1 < n_groups:
                    wait("gpsimd", "v", v_out_j)
                    dma("gpsimd", xg[:, j, :],
                        x_in[seq2, t02 + j * CHUNK:t02 + (j + 1) * CHUNK, :], "xq")
            if g + 1 < n_groups:
                xq_pref[g + 1] = C["xq"]
            P.barrier()

        wait("sync", "st", C["st"])

        blk = e(nc.Block())
        blk.sync(lambda eng: [f(eng) for f in P.streams["sync"]])
        blk.tensor(lambda eng: [f(eng) for f in P.streams["tensor"]])
        blk.vector(lambda eng: [f(eng) for f in P.streams["vector"]])
        blk.scalar(lambda eng: [f(eng) for f in P.streams["scalar"]])
        blk.gpsimd(lambda eng: [f(eng) for f in P.streams["gpsimd"]])

    return nc


def _host_arrays(inputs):
    f32 = np.float32
    am = {}
    am["wipT"] = np.ascontiguousarray(inputs["L0_attn_in_proj_w"].T.astype(f32, copy=False))
    am["wopT"] = np.ascontiguousarray(inputs["L0_attn_out_proj_w"].T.astype(f32, copy=False))
    am["wgT"] = np.ascontiguousarray(inputs["L0_ffn_gate_w"].T.astype(f32, copy=False))
    am["wuT"] = np.ascontiguousarray(inputs["L0_ffn_up_w"].T.astype(f32, copy=False))
    am["wdT"] = np.ascontiguousarray(inputs["L0_ffn_down_w"].T.astype(f32, copy=False))
    am["convw"] = np.ascontiguousarray(
        inputs["L0_attn_conv_w"].reshape(NCT, 128, 4).transpose(1, 0, 2).reshape(128, NCT * 4))
    am["convb"] = np.ascontiguousarray(inputs["L0_attn_conv_b"].reshape(NCT, 128).T)
    am["dtbias"] = np.ascontiguousarray(inputs["L0_attn_dt_bias"].reshape(64, 1))
    am["alog"] = np.ascontiguousarray(inputs["L0_attn_A_log"].reshape(64, 1))
    am["drow"] = np.ascontiguousarray(inputs["L0_attn_D"].reshape(1, 64))
    am["nwrow"] = np.ascontiguousarray(inputs["L0_attn_norm_w"].reshape(1, DIN))
    am["dsrow"] = np.ascontiguousarray(inputs["L0_ffn_down_s"].reshape(1, D))
    am["gs"] = np.ascontiguousarray(inputs["L0_ffn_gate_s"].reshape(44, 128).T)
    am["us"] = np.ascontiguousarray(inputs["L0_ffn_up_s"].reshape(44, 128).T)
    am["idf"] = np.eye(64, dtype=f32)
    k2c = np.zeros((10, 1280), np.float32)
    k2c[8:10, 0:256] = 1.0
    for m in range(4):
        k2c[m, 256 + m * 128:256 + (m + 1) * 128] = 1.0
        k2c[4 + m, 256 + m * 128:256 + (m + 1) * 128] = 1.0
        k2c[m, 768 + m * 128:768 + (m + 1) * 128] = 1.0
        k2c[4 + m, 768 + m * 128:768 + (m + 1) * 128] = 1.0
    am["k2c"] = k2c.astype(ml_dtypes.bfloat16)
    am["idb"] = np.eye(128, dtype=f32)
    am["mask01"] = np.triu(np.ones((128, 128), dtype=f32))
    return am


def kernel(**inputs):
    x = np.asarray(inputs["L0_attn_hidden_states"], dtype=np.float32)
    am = _host_arrays({k: np.asarray(v) for k, v in inputs.items()})
    nc = build_program()
    in_maps = []
    for c in range(8):
        m = dict(am)
        m["x"] = np.ascontiguousarray(x[2 * c:2 * c + 2])
        in_maps.append(m)
    res = run_bass_kernel_spmd(nc, in_maps, core_ids=list(range(8)))
    return np.concatenate([r["out"] for r in res.results], axis=0)


if __name__ == "__main__":
    pass


# version 16 (final)
# speedup vs baseline: 1.0996x; 1.0583x over previous
import numpy as np
import ml_dtypes
from contextlib import ExitStack

import concourse.bass as bass
import concourse.mybir as mybir
from concourse.bass_utils import run_bass_kernel_spmd

F32 = mybir.dt.float32
BF16 = mybir.dt.bfloat16
FP8 = mybir.dt.float8e4
AF = mybir.ActivationFunctionType
ALU = mybir.AluOpType
AX = mybir.AxisListType

NSEQ, L, D, DIN = 2, 2048, 2048, 4096
CONVD, PROJ, INTER, CHUNK, GTOK = 4352, 8512, 5632, 128, 512
NCH, NGRP, EPS = GTOK // CHUNK, L // GTOK, 1e-5
NCT, NXT = CONVD // 128, DIN // 128
NGROUPS_FULL = NGRP * NSEQ


class Prog:
    def __init__(self):
        self.streams = {k: [] for k in ("sync", "tensor", "vector", "scalar", "gpsimd")}
        self.C = {k: 0 for k in ("ld", "l2", "wq", "wqa", "wqb", "xq", "fqa", "fqb",
                                 "fqc", "fqd", "gq", "t", "v", "s", "g", "y", "st",
                                 "wsa", "wsb", "wsc", "wsd")}
        self.sems = {}

    def emit(self, eng, fn):
        self.streams[eng].append(fn)

    SELF_SEM = {"vector": "v", "scalar": "s"}

    def op(self, eng, make, inc=None, inc_n=1):
        own = self.SELF_SEM.get(eng)
        if own is not None:
            self.wait(eng, own)
            if inc is None:
                inc, inc_n = own, 1
        if inc is not None:
            self.C[inc] += inc_n
        sems = self.sems

        def f(e, make=make, inc=inc, inc_n=inc_n):
            i = make(e)
            if inc is not None:
                i.then_inc(sems[inc], inc_n)
        self.emit(eng, f)

    def wait(self, eng, sem, val=None):
        if val is None:
            val = self.C[sem]
        if val <= 0:
            return
        sems = self.sems
        self.emit(eng, lambda e, sem=sem, val=val: e.wait_ge(sems[sem], val))

    def inc_only(self, eng, sem, n=1):
        self.C[sem] += n
        sems = self.sems
        def f(e, sem=sem, n=n):
            e.drain()
            e.sem_inc(sems[sem], n)
        self.emit(eng, f)

    def barrier(self):
        names = {"tensor": "t", "vector": "v", "scalar": "s", "gpsimd": "g", "sync": "y"}
        for eng, nm in names.items():
            self.inc_only(eng, nm)
        for eng in names:
            for other, onm in names.items():
                if other != eng:
                    self.wait(eng, onm)


def build_program(n_groups=NGROUPS_FULL, taps=()):
    nc = bass.Bass("TRN2", target_bir_lowering=False, debug=False)
    taps = list(taps)

    x_in = nc.dram_tensor("x", [NSEQ, L, D], F32, kind="ExternalInput").ap()
    out_t = nc.dram_tensor("out", [NSEQ, L, D], F32, kind="ExternalOutput").ap()
    wipT = nc.dram_tensor("wipT", [D, PROJ], F32, kind="ExternalInput").ap()
    wopT = nc.dram_tensor("wopT", [DIN, D], F32, kind="ExternalInput").ap()
    wgT = nc.dram_tensor("wgT", [D, INTER], F32, kind="ExternalInput").ap()
    wuT = nc.dram_tensor("wuT", [D, INTER], F32, kind="ExternalInput").ap()
    wdT = nc.dram_tensor("wdT", [INTER, D], F32, kind="ExternalInput").ap()
    convw_in = nc.dram_tensor("convw", [128, NCT * 4], F32, kind="ExternalInput").ap()
    convb_in = nc.dram_tensor("convb", [128, NCT], F32, kind="ExternalInput").ap()
    dtb_in = nc.dram_tensor("dtbias", [64, 1], F32, kind="ExternalInput").ap()
    alog_in = nc.dram_tensor("alog", [64, 1], F32, kind="ExternalInput").ap()
    drow_in = nc.dram_tensor("drow", [1, 64], F32, kind="ExternalInput").ap()
    nwrow_in = nc.dram_tensor("nwrow", [1, DIN], F32, kind="ExternalInput").ap()
    dsrow_in = nc.dram_tensor("dsrow", [1, D], F32, kind="ExternalInput").ap()
    gs_in = nc.dram_tensor("gs", [128, 44], F32, kind="ExternalInput").ap()
    us_in = nc.dram_tensor("us", [128, 44], F32, kind="ExternalInput").ap()
    idf_in = nc.dram_tensor("idf", [64, 64], F32, kind="ExternalInput").ap()
    idb_in = nc.dram_tensor("idb", [128, 128], F32, kind="ExternalInput").ap()
    mask_in = nc.dram_tensor("mask01", [128, 128], F32, kind="ExternalInput").ap()
    k2c_in = nc.dram_tensor("k2c", [10, 1280], BF16, kind="ExternalInput").ap()
    wbf = nc.dram_tensor("wbf", [99, 128, 2048], BF16, kind="Internal").ap()
    wbf8 = nc.dram_tensor("wbf8", [132, 128, 2048], FP8, kind="Internal").ap()

    dbg_shapes = {
        "xbcraw": ([128, NCT * 516], BF16), "dtTg": ([128, GTOK], F32),
        "sgate": ([128, NCH * DIN], BF16), "xbcact": ([128, NCT * 512], BF16),
        "yf": ([128, DIN], BF16), "ynT": ([128, NXT * GTOK], BF16),
        "S": ([128, DIN], F32), "x1": ([128, NCH * D], BF16),
        "hT": ([128, 44 * GTOK], BF16), "ffns": ([128, NCH * D], F32),
        "ln0T": ([128, 16 * GTOK], BF16), "xact": ([128, DIN], BF16),
        "attns": ([128, NCH * D], F32), "sm64": ([128, 64], F32), "sm64b": ([128, 64], BF16),
        "sm128": ([128, 128], F32), "mqd": ([128, 1024], BF16), "expd": ([128, 1024], F32),
    }
    dbg = {}
    for tag, gi in taps:
        sh, dt = dbg_shapes[tag]
        dbg[(tag, gi)] = nc.dram_tensor(f"dbg_{tag}_{gi}", sh, dt, kind="ExternalOutput").ap()

    P = Prog()
    op, wait, C = P.op, P.wait, P.C

    with ExitStack() as ctx:
        e = ctx.enter_context

        def sb(name, shape, dt):
            return e(nc.sbuf_tensor(name, shape, dt))

        r1 = sb("r1", [128, 16 * GTOK], BF16)
        ln0T = r1[:, :].rearrange("p (k t) -> p k t", t=GTOK)
        x_act = r1[:, 0:DIN]
        xs_t = r1[:, DIN:2 * DIN]

        R23B = 32768 + 36928
        r23 = sb("r23", [128, R23B], mybir.dt.uint8)
        r23b, r23f = r23.bitcast(BF16), r23.bitcast(F32)
        sgate = r23b[:, 0:NCH * DIN].rearrange("p (j o) -> p j o", o=DIN)
        attn_stage = r23f[:, 0:NCH * D].rearrange("p (j o) -> p j o", o=D)
        hT = r23b[:, 0:44 * GTOK].rearrange("p (k t) -> p k t", t=GTOK)
        xbcraw = r23b[:, 16384:16384 + NCT * 516].rearrange("p (c t) -> p c t", t=516)
        TB = 32768
        expM = r23f[:, TB // 4:TB // 4 + 1024]
        mq = [r23b[:, (TB + 4096 + i * 2048) // 2:(TB + 4096 + i * 2048) // 2 + 1024] for i in range(2)]
        tqf = r23f[:, (TB + 8192) // 4:(TB + 8192) // 4 + 1024]
        tqb = r23b[:, (TB + 8192) // 2:(TB + 8192) // 2 + 2048]
        yf = r23b[:, (TB + 12288) // 2:(TB + 12288) // 2 + DIN]
        sftmp = r23f[:, (TB + 20480) // 4:(TB + 20480) // 4 + 1024]
        outst = r23f[:, (TB + 24576) // 4:(TB + 24576) // 4 + D]
        lnchunk = r23b[:, (TB + 32768) // 2:(TB + 32768) // 2 + D]
        AG = r23f[:, (TB + 32768) // 4:(TB + 32768) // 4 + GTOK]
        BG = r23f[:, (TB + 32768 + 2048) // 4:(TB + 32768 + 2048) // 4 + GTOK]
        assert TB + 32768 + 2 * D <= R23B

        r4 = sb("r4", [128, NCT * 512], BF16)
        xbcact = r4[:, :].rearrange("p (c t) -> p c t", t=512)
        ynT = r4[:, 0:NXT * GTOK].rearrange("p (k t) -> p k t", t=GTOK)
        ffn_stage = r4.bitcast(F32)[:, 0:NCH * D].rearrange("p (j o) -> p j o", o=D)

        xg_flat = sb("xg", [128, NCH * D], BF16)[:, :]
        xg = xg_flat.rearrange("p (j o) -> p j o", o=D)
        S_f = sb("S_f", [128, DIN], F32)[:, :]
        S_b = sb("S_b", [128, DIN], BF16)[:, :]
        dtTg = sb("dtTg", [128, GTOK], F32)[:, :]
        acsb = sb("acsb", [128, 2048], F32)

        statW = [sb(f"statW{i}", [128, 16 * 128], BF16) for i in range(2)]
        movW = [sb(f"movW{i}", [128, 4 * 512], BF16) for i in range(2)]
        statWv = [w[:, :].rearrange("p (k o) -> p k o", o=128) for w in statW]
        movWv = [w[:, :].rearrange("p (k o) -> p k o", o=512) for w in movW]
        movW_as_stat = [w[:, 0:2048].rearrange("p (k o) -> p k o", o=128) for w in movW]
        statW_as_mov = [w[:, :].rearrange("p (k o) -> p k o", o=512) for w in statW]
        statW8f = [w.bitcast(FP8)[:, 0:2048] for w in statW]
        movW8f = [w.bitcast(FP8)[:, 0:2048] for w in movW]
        statW8_stat = [v.rearrange("p (k o) -> p k o", o=128) for v in statW8f]
        movW8_stat = [v.rearrange("p (k o) -> p k o", o=128) for v in movW8f]
        movW8_mov = [v.rearrange("p (k o) -> p k o", o=512) for v in movW8f]

        idf = sb("idf_c", [64, 64], F32)[:, :]
        idb = sb("idb_c", [128, 128], BF16)[:, :]
        mask01 = sb("mask_c", [128, 128], BF16)[:, :]
        convw = sb("convw_c", [128, NCT * 4], F32)[:, :].rearrange("p (c k) -> p c k", k=4)
        convb = sb("convb_c", [128, NCT], F32)[:, :]
        dtbias = sb("dtbias_c", [128, 1], F32)[:, :]
        alog = sb("alog_c", [128, 1], F32)[:, :]
        negA = sb("negA", [128, 1], F32)[:, :]
        gscale = sb("gscale", [128, 44], F32)[:, :]
        uscale = sb("uscale", [128, 44], F32)[:, :]
        D_b = sb("D_b", [128, 64], BF16)[:, :]
        normw_b = sb("normw_b", [128, DIN], BF16)[:, :]
        dsb = sb("dsb", [128, D], F32)[:, :]
        acsb_bf = acsb.bitcast(BF16)
        K2BASE = (0, 32, 0, 32)
        K2COL = (0, 0, 1280, 1280)
        K2SEM = ("fqa", "fqb", "fqc", "fqd")
        dtTg_bf = dtTg.bitcast(BF16)
        AGhi_g = dtTg_bf[0:64, 0:512]
        AGlo_g = dtTg_bf[0:64, 512:1024]
        BGhi_g = acsb_bf[0:64, 2560:3072]
        BGlo_g = acsb_bf[0:64, 3072:3584]
        onesrow = acsb[64:65, 0:128]
        zeros64 = acsb[64:128, 256:384]

        dtg4 = sb("dtg4", [128, GTOK], F32)[:, :]
        dtg5 = sb("dtg5", [128, GTOK], F32)[:, :]
        dsdt_b = sb("dsdt_b", [128, 64], BF16)[:, :]
        e1 = sb("e1c", [128, 64], F32)[:, :]
        gam = sb("gam", [128, 64], F32)[:, :]
        Gm = sb("Gm", [128, 128], F32)[:, :]
        B_sb = sb("B_sb", [128, 128], BF16)[:, :]
        stats = sb("stats", [128, 16], F32)[:, :]
        halo_sb = sb("halo_sb", [128, NCT * 3], BF16)[:, :].rearrange("p (c t) -> p c t", t=3)
        gmr = acsb[64:65, 0:192]
        c30 = sb("c30", [128, 1], F32)[:, :]
        sgtmp = r23b[:, (TB + 12288) // 2:(TB + 12288) // 2 + 512]
        sgtmp2 = r23b[:, (TB + 13312) // 2:(TB + 13312) // 2 + 512]
        convacc = None

        ps_tr = e(nc.psum_tensor("ps_tr", [128, 1024], BF16))[:, :]
        ps_a = e(nc.psum_tensor("ps_a", [128, 512], F32))[:, :]
        ps_b = e(nc.psum_tensor("ps_b", [128, 512], F32))[:, :]
        ps_q = e(nc.psum_tensor("ps_q", [128, 2048], F32))[:, :]
        ps_x = e(nc.psum_tensor("ps_x", [128, 512], F32))[:, :]
        convacc = [ps_tr.bitcast(F32)[:, 0:512], ps_x]

        for s_ in ("ld", "l2", "wq", "wqa", "wqb", "xq", "fqa", "fqb", "fqc", "fqd",
                   "gq", "t", "v", "s", "g", "y", "st", "wsa", "wsb", "wsc", "wsd"):
            P.sems[s_] = e(nc.semaphore(s_))

        def dma(queue, dst, src, sem):
            op(queue, lambda e, d=dst, s2=src: e.dma_start(out=d, in_=s2), inc=sem, inc_n=16)

        def ln_stats(src, W, nrm_out):
            wait("scalar", "v")
            wait("scalar", "t")
            wait("vector", "s")
            op("scalar", lambda e, src=src: e.activation(nrm_out, src, AF.Square,
                                                         accum_out=stats[:, 0:1]), inc="s")
            op("vector", lambda e, src=src: e.reduce_sum(stats[:, 1:2], src, AX.X), inc="v")
            op("vector", lambda e: e.tensor_scalar(stats[:, 2:3], stats[:, 1:2], 1.0 / W, None, ALU.mult), inc="v")
            op("vector", lambda e: e.tensor_tensor(stats[:, 3:4], stats[:, 2:3], stats[:, 2:3], ALU.mult), inc="v")
            wait("vector", "s")
            op("vector", lambda e: e.tensor_scalar(stats[:, 4:5], stats[:, 0:1], 1.0 / W, None, ALU.mult), inc="v")
            op("vector", lambda e: e.tensor_tensor(stats[:, 4:5], stats[:, 4:5], stats[:, 3:4], ALU.subtract), inc="v")
            op("vector", lambda e: e.tensor_scalar(stats[:, 4:5], stats[:, 4:5], EPS, None, ALU.add), inc="v")
            wait("scalar", "v")
            op("scalar", lambda e: e.activation(stats[:, 5:6], stats[:, 4:5], AF.Sqrt), inc="s")
            wait("vector", "s")
            op("vector", lambda e: e.reciprocal(stats[:, 6:7], stats[:, 5:6]), inc="v")
            op("vector", lambda e: e.tensor_tensor(stats[:, 7:8], stats[:, 2:3], stats[:, 6:7], ALU.mult), inc="v")
            op("vector", lambda e: e.tensor_scalar(stats[:, 7:8], stats[:, 7:8], -1.0, None, ALU.mult), inc="v")
            op("vector", lambda e, src=src: e.tensor_scalar(
                nrm_out, src, stats[:, 6:7], stats[:, 7:8], ALU.mult, ALU.add), inc="v")

        def transpose_batch(src_tiles, dst_fn):
            i, n = 0, len(src_tiles)
            while i < n:
                m = min(8, n - i)
                wait("tensor", "v")
                for k in range(m):
                    op("tensor", lambda e, s2=src_tiles[i + k], k=k:
                       e.transpose(ps_tr[:, k * 128:(k + 1) * 128], s2, idb), inc="t")
                wait("vector", "t")
                op("vector", lambda e, i0=i, m=m: e.tensor_copy(
                    dst_fn(i0, m), ps_tr[:, 0:m * 128].rearrange("p (n t) -> p n t", t=128)), inc="v")
                i += m

        def tap(tag, gi, src_ap):
            if (tag, gi) in dbg:
                P.barrier()
                dma("sync", dbg[(tag, gi)], src_ap, "st")
                wait("sync", "st")
                P.barrier()

        for dst, src in ((idf, idf_in),
                         (convw, convw_in.rearrange("p (c k) -> p c k", k=4)),
                         (convb, convb_in), (dtbias[0:64, :], dtb_in), (alog[0:64, :], alog_in),
                         (gscale, gs_in), (uscale, us_in)):
            dma("sync", dst, src, "ld")
        dma("gpsimd", idb, idb_in, "wq")
        dma("gpsimd", mask01, mask_in, "wq")
        wait("vector", "ld")
        for z in (r1[:, :], r23.bitcast(F32)[:, 0:R23B // 4], r4[:, :],
                  acsb[:, :], dtTg, S_f, S_b, stats, dtg4, dtg5,
                  dsdt_b, e1, gam, Gm, B_sb, xg_flat):
            op("vector", lambda e, z=z: e.memset(z, 0.0), inc="v")
        op("vector", lambda e: e.memset(onesrow, 1.0), inc="v")
        op("vector", lambda e: e.memset(c30, 30.0), inc="v")
        wait("scalar", "ld")
        op("scalar", lambda e: e.activation(negA[0:64, :], alog[0:64, :], AF.Exp), inc="s")
        wait("vector", "s")
        op("vector", lambda e: e.tensor_scalar(negA[0:64, :], negA[0:64, :], -1.0, None, ALU.mult), inc="v")
        stages = []
        for i in range(4):
            stages.append((nwrow_in[0:1, i * 1024:(i + 1) * 1024], 1024,
                           (lambda i=i: normw_b[:, i * 1024:(i + 1) * 1024])))
        for i in range(2):
            stages.append((dsrow_in[0:1, i * 1024:(i + 1) * 1024], 1024,
                           (lambda i=i: dsb[:, i * 1024:(i + 1) * 1024])))
        stages.append((drow_in, 64, (lambda: D_b)))
        datarow = acsb[64:65, 128:128 + 1024]
        wait("sync", "v")
        for src, W, dstf in stages:
            wait("sync", "t")
            dma("sync", datarow[0:1, 0:W], src, "l2")
            wait("tensor", "l2")
            wait("tensor", "v")
            nmm = (W + 511) // 512
            for i in range(nmm):
                w2 = min(512, W - i * 512)
                op("tensor", lambda e, i=i, w2=w2: e.matmul(
                    ps_q[:, i * 512:i * 512 + w2], onesrow,
                    datarow[0:1, i * 512:i * 512 + w2], start=True, stop=True), inc="t")
            wait("vector", "t")
            op("vector", lambda e, dstf=dstf, W=W: e.tensor_copy(dstf(), ps_q[:, 0:W]), inc="v")
        wait("vector", "t")
        wait("sync", "t")
        for k2base, k2c0 in ((0, 0), (32, 0), (0, 1280), (32, 1280)):
            dma("sync", acsb_bf[k2base:k2base + 10, k2c0:k2c0 + 1280], k2c_in, "l2")
            wait("sync", "l2")
        wait("vector", "l2")

        op("vector", lambda e: e.memset(zeros64, 0.0), inc="v")
        wait("tensor", "wq")
        P.barrier()

        K2T = {}
        K2S = {}
        ps_k2 = [(ps_a, ps_b), (ps_x, ps_tr.bitcast(F32)[:, 0:512])]

        def scan_chunk(j):
            js = slice(j * 128, (j + 1) * 128)
            jlast = j * 128 + 127

            op("vector", lambda e: e.tensor_scalar(dtg5[0:64, js], AG[0:64, js],
                                                   AG[0:64, jlast:jlast + 1], None, ALU.subtract), inc="v")
            wait("scalar", "v")
            op("scalar", lambda e: e.activation(dtg5[0:64, js], dtg5[0:64, js], AF.Exp, scale=-1.0), inc="s")
            wait("vector", "s")
            op("vector", lambda e: e.tensor_tensor(dtg5[0:64, js], dtg5[0:64, js],
                                                   dtg4[0:64, js], ALU.mult), inc="v")
            v_dsdt = C["v"]

            wait("tensor", "v", v_dsdt)
            wait("tensor", "s")
            op("tensor", lambda e: e.transpose(ps_x[:, 0:64], AG[0:64, js], idf), inc="t")
            op("tensor", lambda e: e.transpose(ps_x[:, 64:128], dtg5[0:64, js], idf), inc="t")
            t_tr = C["t"]
            wait("scalar", "t")
            dma("scalar", acsb[64:65, 128:192], AG[0:64, jlast:jlast + 1], "gq")
            fq_gam = C["gq"]
            wait("vector", "t", t_tr)
            op("vector", lambda e: e.tensor_copy(dsdt_b, ps_x[:, 64:128]), inc="v")
            v_tr = C["v"]
            wait("scalar", "t", t_tr)
            op("scalar", lambda e: e.activation(e1, ps_x[:, 0:64], AF.Exp), inc="s")
            s_e1 = C["s"]
            wait("tensor", "gq", fq_gam)
            wait("tensor", "v", v_tr)
            wait("tensor", "s", s_e1)
            op("tensor", lambda e: e.matmul(ps_x[:, 192:256], onesrow, acsb[64:65, 128:192], start=True, stop=True), inc="t")
            t_gam = C["t"]
            wait("scalar", "t", t_gam)
            op("scalar", lambda e: e.activation(gam, ps_x[:, 192:256], AF.Exp), inc="s")
            s_gam = C["s"]

            wait("vector", "s")
            transpose_batch([xbcact[:, d, js] for d in range(NXT)],
                            lambda i0, m: x_act.rearrange("p (n t) -> p n t", t=128)[:, i0:i0 + m, :])
            wait("tensor", "v")
            op("tensor", lambda e: e.transpose(ps_tr[:, 0:128], xbcact[:, 32, js], idb), inc="t")
            wait("vector", "t")
            op("vector", lambda e: e.tensor_copy(B_sb, ps_tr[:, 0:128]), inc="v")
            op("tensor", lambda e: e.matmul(ps_x[:, 256:384], xbcact[:, 32, js], xbcact[:, 33, js],
                                            start=True, stop=True), inc="t")
            t_g = C["t"]
            wait("vector", "t", t_g)
            op("vector", lambda e: e.scalar_tensor_tensor(Gm, ps_x[:, 256:384], 1.0, mask01,
                                                          ALU.mult, ALU.mult), inc="v")
            op("vector", lambda e: e.tensor_tensor(
                xs_t.rearrange("p (h d) -> p h d", d=64), x_act.rearrange("p (h d) -> p h d", d=64),
                dsdt_b.broadcast_to((128, 64, 64)), ALU.mult), inc="v")
            v_gm = C["v"]

            k2_hist = {}
            v_mq_hist = {}

            def emit_flatten(og):
                oct_id = j * 8 + og
                st_ = oct_id % 4
                base, c0, fsem = K2BASE[st_], K2COL[st_], K2SEM[st_]
                wait("sync", "t", K2T.get(oct_id - 4, 0))
                for q in range(2):
                    hq = og * 8 + q * 4
                    dma("sync", acsb_bf[base:base + 4, c0 + q * 128:c0 + (q + 1) * 128],
                        BGhi_g[hq:hq + 4, js], fsem)
                    dma("sync", acsb_bf[base + 4:base + 8, c0 + q * 128:c0 + (q + 1) * 128],
                        BGlo_g[hq:hq + 4, js], fsem)
                    dma("sync", acsb_bf[base + 8:base + 9, c0 + 256 + q * 512:c0 + 768 + q * 512]
                        .rearrange("p (h t) -> p h t", t=128), AGhi_g[hq:hq + 4, js], fsem)
                    dma("sync", acsb_bf[base + 9:base + 10, c0 + 256 + q * 512:c0 + 768 + q * 512]
                        .rearrange("p (h t) -> p h t", t=128), AGlo_g[hq:hq + 4, js], fsem)
                return C[fsem]

            def emit_k2(og, fq_val):
                oct_id = j * 8 + og
                st_ = oct_id % 4
                base, c0, fsem = K2BASE[st_], K2COL[st_], K2SEM[st_]
                wait("tensor", fsem, fq_val)
                wait("tensor", "s", K2S.get(oct_id - 2, 0))
                if og == 1:
                    wait("tensor", "s", s_gam)
                    wait("tensor", "v", v_gm)
                pa, pb = ps_k2[og % 2]
                for q in range(2):
                    psd = pa if q == 0 else pb
                    op("tensor", lambda e, psd=psd, q=q, base=base, c0=c0: e.matmul(
                        psd[:, :], acsb_bf[base:base + 10, c0 + q * 128:c0 + (q + 1) * 128],
                        acsb_bf[base:base + 10, c0 + 256 + q * 512:c0 + 768 + q * 512],
                        start=True, stop=True), inc="t")
                K2T[oct_id] = C["t"]
                k2_hist[og] = C["t"]

            def emit_exps_mq(og):
                oc = og % 2
                pa, pb = ps_k2[og % 2]
                wait("scalar", "t", k2_hist[og])
                wait("scalar", "v")
                op("scalar", lambda e, pa=pa: e.activation(pa, pa, AF.Relu, scale=-1.0, bias=c30), inc="s")
                op("scalar", lambda e, pb=pb: e.activation(pb, pb, AF.Relu, scale=-1.0, bias=c30), inc="s")
                op("scalar", lambda e, pa=pa: e.activation(expM[:, 0:512], pa, AF.Exp, scale=-1.0, bias=c30), inc="s")
                op("scalar", lambda e, pb=pb: e.activation(expM[:, 512:1024], pb, AF.Exp, scale=-1.0, bias=c30), inc="s")
                K2S[j * 8 + og] = C["s"]
                wait("vector", "s")
                wait("vector", "t")
                op("vector", lambda e, oc=oc: e.tensor_tensor(
                    mq[oc][0:128, 0:1024].rearrange("p (h t) -> p h t", t=128),
                    expM.rearrange("p (h t) -> p h t", t=128),
                    Gm.broadcast_to((128, 128, 8)).rearrange("p t h -> p h t"),
                    ALU.mult), inc="v")
                v_mq_hist[og] = C["v"]

            vstate = {"yf": 0}

            def emit_ydiag(og):
                oc = og % 2
                wait("tensor", "v", max(v_mq_hist[og], vstate["yf"]))
                for hh in range(8):
                    h = og * 8 + hh
                    slot = oc * 512 + hh * 64
                    op("tensor", lambda e, oc=oc, hh=hh, h=h, slot=slot: e.matmul(
                        ps_q[:, slot:slot + 64], mq[oc][:, hh * 128:(hh + 1) * 128],
                        x_act[:, h * 64:(h + 1) * 64], start=(hh == 0), stop=False), inc="t")

            fqv = {}
            for og in range(3):
                fqv[og] = emit_flatten(og)
            emit_k2(0, fqv[0])
            emit_exps_mq(0)
            for og in range(1, 8):
                if og + 2 <= 7:
                    fqv[og + 2] = emit_flatten(og + 2)
                emit_k2(og, fqv[og])
                emit_ydiag(og - 1)
                emit_exps_mq(og)
                if og % 2 == 0:
                    quarter_tail(j, og // 2 - 1, s_gam, vstate)
            emit_ydiag(7)
            quarter_tail(j, 3, s_gam, vstate)
            chunk_norm(j)
            return

        def dt_batch():
            dtraw = dtTg[0:64, :]
            for s_ in ("fqa", "fqb", "fqc", "fqd", "gq"):
                wait("scalar", s_)
                wait("vector", s_)
            wait("vector", "t")
            op("vector", lambda e: e.tensor_scalar(dtg4[0:64, :], dtraw, dtbias[0:64, :],
                                                   None, ALU.add), inc="v")
            wait("scalar", "v")
            op("scalar", lambda e: e.activation(dtg5[0:64, :], dtg4[0:64, :], AF.Abs), inc="s")
            op("scalar", lambda e: e.activation(dtg5[0:64, :], dtg5[0:64, :], AF.Exp, scale=-1.0), inc="s")
            op("scalar", lambda e: e.activation(dtg5[0:64, :], dtg5[0:64, :], AF.Ln, bias=1.0), inc="s")
            wait("vector", "s")
            op("vector", lambda e: e.scalar_tensor_tensor(dtg4[0:64, :], dtg4[0:64, :], 0.0,
                                                          dtg5[0:64, :], ALU.max, ALU.add), inc="v")
            wait("scalar", "v")
            op("scalar", lambda e: e.activation(dtg5[0:64, :], dtg4[0:64, :], AF.Ln), inc="s")
            wait("vector", "s")
            op("vector", lambda e: e.tensor_scalar(BG[0:64, :], dtg4[0:64, :], negA[0:64, :],
                                                   None, ALU.mult), inc="v")
            op("vector", lambda e: e.memset(Gm[0:64, :], 0.0), inc="v")
            for jj in range(NCH):
                op("vector", lambda e, jj=jj: e.tensor_tensor_scan(
                    AG[0:64, jj * 128:(jj + 1) * 128], BG[0:64, jj * 128:(jj + 1) * 128],
                    Gm[0:64, :], 0.0, ALU.add, ALU.add), inc="v")
            op("vector", lambda e: e.tensor_tensor(BG[0:64, :], dtg5[0:64, :], AG[0:64, :],
                                                   ALU.subtract), inc="v")
            op("vector", lambda e: e.tensor_copy(AGhi_g, AG[0:64, :]), inc="v")
            op("vector", lambda e: e.tensor_tensor(AGlo_g, AG[0:64, :], AGhi_g, ALU.subtract), inc="v")
            op("vector", lambda e: e.tensor_copy(BGhi_g, BG[0:64, :]), inc="v")
            op("vector", lambda e: e.tensor_tensor(BGlo_g, BG[0:64, :], BGhi_g, ALU.subtract), inc="v")

        def chunk_norm(j):
            wait("scalar", "v")
            wait("scalar", "t")
            op("scalar", lambda e: e.activation(x_act, yf, AF.Square,
                                                accum_out=stats[:, 8:9]), inc="s")
            wait("vector", "s")
            op("vector", lambda e: e.tensor_scalar(stats[:, 9:10], stats[:, 8:9], 1.0 / DIN, EPS,
                                                   ALU.mult, ALU.add), inc="v")
            v_ss = C["v"]
            wait("scalar", "v", v_ss)
            op("scalar", lambda e: e.activation(stats[:, 10:11], stats[:, 9:10], AF.Ln), inc="s")
            op("scalar", lambda e: e.activation(stats[:, 11:12], stats[:, 10:11], AF.Exp, scale=-0.5), inc="s")
            s_rstd = C["s"]
            wait("vector", "s", s_rstd)
            op("vector", lambda e: e.scalar_tensor_tensor(xs_t, yf, stats[:, 11:12], normw_b,
                                                          ALU.mult, ALU.mult), inc="v")
            transpose_batch([xs_t[:, d * 128:(d + 1) * 128] for d in range(NXT)],
                            lambda i0, m, j=j: ynT[:, i0:i0 + m, j * 128:(j + 1) * 128])

        def quarter_tail(j, q, s_gam, vstate):
            js = slice(j * 128, (j + 1) * 128)
            qs = slice(q * 1024, (q + 1) * 1024)
            for half in range(2):
                op("tensor", lambda e, q=q, half=half: e.matmul(
                    ps_q[:, 1024 + half * 512:1536 + half * 512], xbcact[:, 33, js],
                    S_b[:, q * 1024 + half * 512:q * 1024 + (half + 1) * 512],
                    start=True, stop=True), inc="t")
            t_yoff = C["t"]
            wait("vector", "t", t_yoff)
            op("vector", lambda e, q=q: e.tensor_tensor(
                tqb[:, 0:1024].rearrange("p (h d) -> p h d", d=64),
                ps_q[:, 1024:2048].rearrange("p (h d) -> p h d", d=64),
                e1[:, q * 16:(q + 1) * 16].broadcast_to((128, 16, 64)), ALU.mult), inc="v")
            v_a1 = C["v"]
            op("vector", lambda e, q=q: e.tensor_tensor(
                tqb[:, 1024:2048].rearrange("p (h d) -> p h d", d=64),
                x_act[:, q * 1024:(q + 1) * 1024].rearrange("p (h d) -> p h d", d=64),
                D_b[:, q * 16:(q + 1) * 16].broadcast_to((128, 16, 64)), ALU.mult), inc="v")
            v_a2 = C["v"]
            wait("tensor", "v", v_a2)
            for half in range(2):
                for sl in range(8):
                    slot = half * 512 + sl * 64
                    op("tensor", lambda e, slot=slot: e.matmul(
                        ps_q[:, slot:slot + 64], idb,
                        tqb[:, slot:slot + 64], start=False, stop=False), inc="t")
                    op("tensor", lambda e, slot=slot, sl=sl: e.matmul(
                        ps_q[:, slot:slot + 64], idb,
                        tqb[:, 1024 + slot:1024 + slot + 64],
                        start=False, stop=(sl == 7)), inc="t")
            t_acc = C["t"]
            wait("vector", "t", t_acc)
            op("vector", lambda e, j=j, qs=qs: e.scalar_tensor_tensor(
                yf[:, qs], ps_q[:, 0:1024], 1.0, sgate[:, j, qs], ALU.mult, ALU.mult), inc="v")
            vstate["yf"] = C["v"]
            wait("tensor", "v", v_a1)
            for half in range(2):
                op("tensor", lambda e, q=q, half=half: e.matmul(
                    ps_q[:, 1024 + half * 512:1536 + half * 512], B_sb,
                    xs_t[:, q * 1024 + half * 512:q * 1024 + (half + 1) * 512],
                    start=True, stop=True), inc="t")
            t_st = C["t"]
            wait("vector", "t", t_st)
            wait("vector", "s", s_gam)
            op("vector", lambda e, q=q, qs=qs: e.tensor_tensor(
                sftmp.rearrange("p (h d) -> p h d", d=64),
                S_f[:, qs].rearrange("p (h d) -> p h d", d=64),
                gam[:, q * 16:(q + 1) * 16].broadcast_to((128, 16, 64)), ALU.mult), inc="v")
            op("vector", lambda e, qs=qs: e.tensor_tensor(S_f[:, qs], sftmp, ps_q[:, 1024:2048],
                                                          ALU.add), inc="v")
            wait("scalar", "v")
            op("scalar", lambda e, qs=qs: e.copy(S_b[:, qs], S_f[:, qs]), inc="s")


        sid_map = {}
        store_pending = {}
        WSEM = {("stat", 0): "wsa", ("stat", 1): "wsb", ("mov", 0): "wsc", ("mov", 1): "wsd"}
        BUF_FLAT = {("stat", 0): statW[0][:, :], ("stat", 1): statW[1][:, :],
                    ("mov", 0): movW[0][:, :], ("mov", 1): movW[1][:, :]}
        BUF_FLAT8 = {("stat", 0): statW8f[0], ("stat", 1): statW8f[1],
                     ("mov", 0): movW8f[0], ("mov", 1): movW8f[1]}
        sid8_map = {}

        def war_wait(buf_id):
            if buf_id in store_pending:
                sem_, val_ = store_pending.pop(buf_id)
                wait("gpsimd", sem_, val_)

        def emit_store(key, buf_id, qsem):
            sid = sid_map.setdefault(key, len(sid_map))
            ssem = WSEM[buf_id]
            wait("sync", qsem, C[qsem])
            dma("sync", wbf[sid], BUF_FLAT[buf_id], ssem)
            store_pending[buf_id] = (ssem, C[ssem])

        t_slab = {}
        xq_pref = {}
        p1a_pref = {}
        for g in range(n_groups):
            seq, grp = divmod(g, NGRP)
            t0 = grp * GTOK
            if g:
                wait("vector", "st", C["st"])
                wait("scalar", "st", C["st"])
            if grp == 0:
                op("vector", lambda e: e.memset(S_f, 0.0), inc="v")
                op("vector", lambda e: e.memset(S_b, 0.0), inc="v")
                op("vector", lambda e: e.memset(halo_sb, 0.0), inc="v")

            if g in xq_pref:
                ld_x = xq_pref.pop(g)
            else:
                for j in range(NCH):
                    dma("gpsimd", xg[:, j, :], x_in[seq, t0 + j * CHUNK:t0 + (j + 1) * CHUNK, :], "xq")
                ld_x = C["xq"]
            wait("scalar", "xq", ld_x)
            wait("vector", "xq", ld_x)
            for j in range(NCH):
                ln_stats(xg[:, j, :], D, lnchunk)
                wait("tensor", "s")
                transpose_batch([lnchunk[:, k * 128:(k + 1) * 128] for k in range(16)],
                                lambda i0, m, j=j: ln0T[:, i0:i0 + m, j * 128:(j + 1) * 128])
            tap("ln0T", g, r1[:, :])

            v_ln0 = C["v"]
            evac_cnt = {0: 0, 1: 0}
            for ot in range(NCT + 1):
                buf = ot % 2
                width = 128 if ot < NCT else 64
                c0 = 4096 + ot * 128
                wsem = "wqa" if buf == 0 else "wqb"
                if (g, ot) in p1a_pref:
                    wait("tensor", wsem, p1a_pref.pop((g, ot)))
                    psd = ps_a if buf == 0 else ps_b
                    wait("tensor", "v", max(v_ln0, evac_cnt[buf]))
                    for k in range(16):
                        op("tensor", lambda e, k=k, buf=buf, w=width, psd=psd: e.matmul(
                            psd[0:w, :], statWv[buf][:, k, 0:w], ln0T[:, k, :],
                            start=(k == 0), stop=(k == 15)), inc="t")
                    t_slab[("p1a", g, ot)] = C["t"]
                    wait("vector", "t", C["t"])
                    op("vector", lambda e, ot=ot, psd=psd: e.tensor_copy(xbcraw[:, ot, 4:516], psd), inc="v")
                    evac_cnt[buf] = C["v"]
                    continue
                wait("gpsimd", "t", t_slab.get(("p1a", g, ot - 2), 0))
                if g == 0:
                    war_wait(("stat", buf))
                    dma("gpsimd", statWv[buf][:, :, 0:width],
                        wipT[:, c0:c0 + width].rearrange("(k p) o -> p k o", p=128), wsem)
                    emit_store(("p1a", ot), ("stat", buf), wsem)
                else:
                    war_wait(("stat", buf))
                    dma("gpsimd", statW[buf][:, :], wbf[sid_map[("p1a", ot)]], wsem)
                wait("tensor", wsem, C[wsem])
                wait("tensor", "v", max(v_ln0, evac_cnt[buf]))
                psd = ps_a if buf == 0 else ps_b
                for k in range(16):
                    op("tensor", lambda e, k=k, buf=buf, w=width, psd=psd: e.matmul(
                        psd[0:w, :], statWv[buf][:, k, 0:w], ln0T[:, k, :],
                        start=(k == 0), stop=(k == 15)), inc="t")
                t_slab[("p1a", g, ot)] = C["t"]
                wait("vector", "t", C["t"])
                if ot < NCT:
                    op("vector", lambda e, ot=ot, psd=psd: e.tensor_copy(xbcraw[:, ot, 4:516], psd), inc="v")
                else:
                    op("vector", lambda e, psd=psd: e.tensor_copy(dtTg[0:64, :], psd[0:64, :]), inc="v")
                evac_cnt[buf] = C["v"]
            tap("xbcraw", g, r23b[:, 16384:16384 + NCT * 516])
            tap("dtTg", g, dtTg)

            op("vector", lambda e: e.tensor_copy(xbcraw[:, :, 1:4], halo_sb), inc="v")
            op("vector", lambda e: e.tensor_copy(halo_sb, xbcraw[:, :, 513:516]), inc="v")

            def conv_ct(ct):
                wait("vector", "s")
                op("vector", lambda e, ct=ct: e.tensor_scalar(
                    convacc[0], xbcraw[:, ct, 1:513], convw[:, ct, 0:1], None, ALU.mult), inc="v")
                op("vector", lambda e, ct=ct: e.scalar_tensor_tensor(
                    convacc[1], xbcraw[:, ct, 2:514], convw[:, ct, 1:2], convacc[0],
                    ALU.mult, ALU.add), inc="v")
                op("vector", lambda e, ct=ct: e.scalar_tensor_tensor(
                    convacc[0], xbcraw[:, ct, 3:515], convw[:, ct, 2:3], convacc[1],
                    ALU.mult, ALU.add), inc="v")
                op("vector", lambda e, ct=ct: e.scalar_tensor_tensor(
                    convacc[1], xbcraw[:, ct, 4:516], convw[:, ct, 3:4], convacc[0],
                    ALU.mult, ALU.add), inc="v")
                wait("scalar", "v")
                op("scalar", lambda e, ct=ct: e.activation(
                    xbcact[:, ct, :], convacc[1], AF.Silu, bias=convb[:, ct:ct + 1]), inc="s")

            conv_plan = [[31, 32, 33, 0]] + [list(range(4 * o - 3, 4 * o + 1)) for o in range(1, 8)]
            conv_plan[7] = list(range(25, 31))
            s_gate_evac = 0
            for ot in range(8):
                for ks in range(4):
                    buf = ks % 2
                    wait("gpsimd", "t", t_slab.get(("p1b", ot, ks - 2),
                                                   t_slab.get(("p1b", ot - 1, ks + 2),
                                                              t_slab.get(("p1a", g, NCT - ks), 0))))
                    wsem = "wqa" if buf == 0 else "wqb"
                    if g == 0:
                        war_wait(("mov", buf))
                        dma("gpsimd", movWv[buf],
                            wipT[ks * 512:(ks + 1) * 512, ot * 512:(ot + 1) * 512]
                            .rearrange("(k p) o -> p k o", p=128), wsem)
                        emit_store(("p1b", ot, ks), ("mov", buf), wsem)
                    else:
                        war_wait(("mov", buf))
                        dma("gpsimd", movW[buf][:, :], wbf[sid_map[("p1b", ot, ks)]], wsem)
                    wait("tensor", wsem, C[wsem])
                    if ks == 0 and s_gate_evac:
                        wait("tensor", "s", s_gate_evac)
                    for kk in range(4):
                        k = ks * 4 + kk
                        for j in range(NCH):
                            op("tensor", lambda e, k=k, kk=kk, j=j, buf=buf: e.matmul(
                                ps_q[:, j * 512:(j + 1) * 512],
                                ln0T[:, k, j * 128:(j + 1) * 128], movWv[buf][:, kk, :],
                                start=(k == 0), stop=(k == 15)), inc="t")
                    t_slab[("p1b", ot, ks)] = C["t"]
                for ct in conv_plan[ot]:
                    conv_ct(ct)
                if ot == 0:
                    dt_batch()
                wait("scalar", "t", C["t"])
                for j in range(NCH):
                    op("scalar", lambda e, ot=ot, j=j: e.activation(
                        sgate[:, j, ot * 512:(ot + 1) * 512], ps_q[:, j * 512:(j + 1) * 512],
                        AF.Silu), inc="s")
                s_gate_evac = C["s"]
            P.barrier()
            tap("sgate", g, r23b[:, 0:NCH * DIN])
            tap("xbcact", g, r4[:, :])
            tap("dtTg", 50 + g, AG)
            tap("dtTg", 60 + g, BG)
            tap("dtTg", 70 + g, dtg4)

            for j in range(NCH):
                scan_chunk(j)
                if any(k[1] == g * 10 + j for k in dbg):
                    P.barrier()
                tap("yf", g * 10 + j, yf)
                tap("xact", g * 10 + j, x_act)
                tap("S", g * 10 + j, S_f)
                tap("sm64", g * 10 + j, e1)
                tap("sm64", 100 + g * 10 + j, gam)
                tap("sm128", g * 10 + j, Gm)
                tap("mqd", g * 10 + j, mq[1])
                tap("expd", g * 10 + j, expM)
                tap("sm64b", 200 + g * 10 + j, dsdt_b)
                tap("xact", 300 + g * 10 + j, xs_t)
            tap("ynT", g, r4[:, 0:NXT * GTOK])

            v_ynT = C["v"]
            s_p3 = 0
            for ot in range(4):
                for ks in range(8):
                    buf = ks % 2
                    wait("gpsimd", "t", t_slab.get(("p3", ot, ks - 2), t_slab.get(("p3", ot - 1, ks + 6), 0)))
                    wsem = "wqa" if buf == 0 else "wqb"
                    if g == 0:
                        war_wait(("mov", buf))
                        dma("gpsimd", movWv[buf],
                            wopT[ks * 512:(ks + 1) * 512, ot * 512:(ot + 1) * 512]
                            .rearrange("(k p) o -> p k o", p=128), wsem)
                        emit_store(("p3", ot, ks), ("mov", buf), wsem)
                    else:
                        war_wait(("mov", buf))
                        dma("gpsimd", movW[buf][:, :], wbf[sid_map[("p3", ot, ks)]], wsem)
                    wait("tensor", wsem, C[wsem])
                    if ks == 0:
                        wait("tensor", "v", v_ynT)
                        if s_p3:
                            wait("tensor", "s", s_p3)
                    for kk in range(4):
                        k = ks * 4 + kk
                        for j in range(NCH):
                            op("tensor", lambda e, k=k, kk=kk, j=j, buf=buf: e.matmul(
                                ps_q[:, j * 512:(j + 1) * 512],
                                ynT[:, k, j * 128:(j + 1) * 128], movWv[buf][:, kk, :],
                                start=(k == 0), stop=(k == 31)), inc="t")
                    t_slab[("p3", ot, ks)] = C["t"]
                wait("scalar", "t", C["t"])
                for j in range(NCH):
                    op("scalar", lambda e, ot=ot, j=j: e.copy(
                        attn_stage[:, j, ot * 512:(ot + 1) * 512], ps_q[:, j * 512:(j + 1) * 512]), inc="s")
                s_p3 = C["s"]
            tap("attns", g, r23f[:, 0:NCH * D])
            for j in range(NCH):
                wait("scalar", "s", s_p3)
                wait("vector", "s", s_p3)
                ln_stats(attn_stage[:, j, :], D, lnchunk)
                wait("vector", "s")
                op("vector", lambda e, j=j: e.tensor_tensor(xg[:, j, :], xg[:, j, :], lnchunk, ALU.add), inc="v")
            tap("x1", g, xg_flat)

            for j in range(NCH):
                ln_stats(xg[:, j, :], D, lnchunk)
                wait("tensor", "s")
                transpose_batch([lnchunk[:, k * 128:(k + 1) * 128] for k in range(16)],
                                lambda i0, m, j=j: ln0T[:, i0:i0 + m, j * 128:(j + 1) * 128])
            tap("ln0T", 100 + g, r1[:, :])
            v_ln2 = C["v"]
            vu_evac = 0
            vu_evac_par = {0: 0, 1: 0}
            gu_ps = {0: (ps_a, ps_b), 1: (ps_q[:, 0:512], ps_q[:, 512:1024])}
            for ot in range(44):
                pair = ot % 2
                gsl = statW8_stat[0] if pair == 0 else movW8_stat[0]
                usl = statW8_stat[1] if pair == 0 else movW8_stat[1]
                wait("gpsimd", "t", t_slab.get(("p4gu", g, ot - 2),
                                               t_slab.get(("p3", 3, 6 + ot), 0)))
                wsem = "wqa" if pair == 0 else "wqb"
                gb = ("stat", 0) if pair == 0 else ("mov", 0)
                ub = ("stat", 1) if pair == 0 else ("mov", 1)
                if g == 0:
                    war_wait(gb)
                    war_wait(ub)
                    dma("gpsimd", gsl,
                        wgT[:, ot * 128:(ot + 1) * 128].rearrange("(k p) o -> p k o", p=128), wsem)
                    dma("gpsimd", usl,
                        wuT[:, ot * 128:(ot + 1) * 128].rearrange("(k p) o -> p k o", p=128), wsem)
                    sid_g = sid8_map.setdefault(("gu_g", ot), len(sid8_map))
                    sid_u = sid8_map.setdefault(("gu_u", ot), len(sid8_map))
                    wait("sync", wsem, C[wsem])
                    dma("sync", wbf8[sid_g], BUF_FLAT8[gb], WSEM[gb])
                    store_pending[gb] = (WSEM[gb], C[WSEM[gb]])
                    dma("sync", wbf8[sid_u], BUF_FLAT8[ub], WSEM[ub])
                    store_pending[ub] = (WSEM[ub], C[WSEM[ub]])
                else:
                    war_wait(gb)
                    war_wait(ub)
                    dma("gpsimd", BUF_FLAT8[gb], wbf8[sid8_map[("gu_g", ot)]], wsem)
                    dma("gpsimd", BUF_FLAT8[ub], wbf8[sid8_map[("gu_u", ot)]], wsem)
                wait("tensor", wsem, C[wsem])
                psg, psu = gu_ps[pair]
                wait("tensor", "v", max(v_ln2, vu_evac_par[pair]))
                for k in range(16):
                    op("tensor", lambda e, k=k, gsl=gsl, psg=psg: e.matmul(
                        psg, gsl[:, k, :], ln0T[:, k, :], start=(k == 0), stop=(k == 15)), inc="t")
                for k in range(16):
                    op("tensor", lambda e, k=k, usl=usl, psu=psu: e.matmul(
                        psu, usl[:, k, :], ln0T[:, k, :], start=(k == 0), stop=(k == 15)), inc="t")
                t_slab[("p4gu", g, ot)] = C["t"]
                wait("scalar", "t", C["t"])
                if vu_evac:
                    wait("scalar", "v", vu_evac)
                op("scalar", lambda e, ot=ot, psg=psg: e.activation(sgtmp, psg, AF.Silu,
                                                           scale=gscale[:, ot:ot + 1]), inc="s")
                wait("vector", "s")
                op("vector", lambda e, ot=ot, psu=psu: e.tensor_scalar(sgtmp2, psu, uscale[:, ot:ot + 1],
                                                              None, ALU.mult), inc="v")
                op("vector", lambda e, ot=ot: e.tensor_tensor(hT[:, ot, :], sgtmp2, sgtmp, ALU.mult), inc="v")
                vu_evac = C["v"]
                vu_evac_par[pair] = C["v"]
            tap("hT", g, r23b[:, 0:44 * GTOK])

            v_p4d = 0
            for ot in range(4):
                for ks in range(11):
                    buf = ks % 2
                    wait("gpsimd", "t", t_slab.get(("p4d", ot, ks - 2),
                                                   t_slab.get(("p4d", ot - 1, 10 - ks),
                                                              t_slab.get(("p4gu", g, 43), 0))))
                    wsem = "wqa" if buf == 0 else "wqb"
                    if g == 0:
                        war_wait(("mov", buf))
                        dma("gpsimd", movW8_mov[buf],
                            wdT[ks * 512:(ks + 1) * 512, ot * 512:(ot + 1) * 512]
                            .rearrange("(k p) o -> p k o", p=128), wsem)
                        sid8 = sid8_map.setdefault(("p4d", ot, ks), len(sid8_map))
                        wait("sync", wsem, C[wsem])
                        dma("sync", wbf8[sid8], BUF_FLAT8[("mov", buf)], WSEM[("mov", buf)])
                        store_pending[("mov", buf)] = (WSEM[("mov", buf)], C[WSEM[("mov", buf)]])
                    else:
                        war_wait(("mov", buf))
                        dma("gpsimd", BUF_FLAT8[("mov", buf)], wbf8[sid8_map[("p4d", ot, ks)]], wsem)
                    wait("tensor", wsem, C[wsem])
                    if ot == 0 and ks == 0:
                        wait("tensor", "v", vu_evac)
                    if ks == 0 and v_p4d:
                        wait("tensor", "v", v_p4d)
                    for kk in range(4):
                        k = ks * 4 + kk
                        for j in range(NCH):
                            op("tensor", lambda e, k=k, kk=kk, j=j, buf=buf: e.matmul(
                                ps_q[:, j * 512:(j + 1) * 512],
                                hT[:, k, j * 128:(j + 1) * 128], movW8_mov[buf][:, kk, :],
                                start=(k == 0), stop=(k == 43)), inc="t")
                    t_slab[("p4d", ot, ks)] = C["t"]
                wait("vector", "t", C["t"])
                for j in range(NCH):
                    op("vector", lambda e, ot=ot, j=j: e.tensor_tensor(
                        ffn_stage[:, j, ot * 512:(ot + 1) * 512], ps_q[:, j * 512:(j + 1) * 512],
                        dsb[:, ot * 512:(ot + 1) * 512], ALU.mult), inc="v")
                v_p4d = C["v"]
            tap("ffns", g, r4.bitcast(F32)[:, 0:NCH * D])
            if g + 1 < n_groups:
                wait("gpsimd", "t", t_slab[("p4d", 3, 10)])
                for ot2 in (0, 1):
                    wsem = "wqa" if ot2 == 0 else "wqb"
                    war_wait(("stat", ot2))
                    dma("gpsimd", statW[ot2][:, :], wbf[sid_map[("p1a", ot2)]], wsem)
                    p1a_pref[(g + 1, ot2)] = C[wsem]
            st_prev = 0
            seq2, grp2 = divmod(g + 1, NGRP)
            t02 = grp2 * GTOK
            for j in range(NCH):
                wait("scalar", "v", v_p4d)
                ln_stats(ffn_stage[:, j, :], D, lnchunk)
                wait("vector", "s")
                wait("vector", "st", st_prev)
                op("vector", lambda e, j=j: e.tensor_tensor(outst, xg[:, j, :], lnchunk, ALU.add), inc="v")
                v_out_j = C["v"]
                wait("sync", "v")
                dma("sync", out_t[seq, t0 + j * CHUNK:t0 + (j + 1) * CHUNK, :], outst, "st")
                st_prev = C["st"]
                if g + 1 < n_groups:
                    wait("gpsimd", "v", v_out_j)
                    dma("gpsimd", xg[:, j, :],
                        x_in[seq2, t02 + j * CHUNK:t02 + (j + 1) * CHUNK, :], "xq")
            if g + 1 < n_groups:
                xq_pref[g + 1] = C["xq"]
            P.barrier()

        wait("sync", "st", C["st"])

        blk = e(nc.Block())
        blk.sync(lambda eng: [f(eng) for f in P.streams["sync"]])
        blk.tensor(lambda eng: [f(eng) for f in P.streams["tensor"]])
        blk.vector(lambda eng: [f(eng) for f in P.streams["vector"]])
        blk.scalar(lambda eng: [f(eng) for f in P.streams["scalar"]])
        blk.gpsimd(lambda eng: [f(eng) for f in P.streams["gpsimd"]])

    return nc


def _host_arrays(inputs):
    f32 = np.float32
    am = {}
    am["wipT"] = np.ascontiguousarray(inputs["L0_attn_in_proj_w"].T.astype(f32, copy=False))
    am["wopT"] = np.ascontiguousarray(inputs["L0_attn_out_proj_w"].T.astype(f32, copy=False))
    am["wgT"] = np.ascontiguousarray(inputs["L0_ffn_gate_w"].T.astype(f32, copy=False))
    am["wuT"] = np.ascontiguousarray(inputs["L0_ffn_up_w"].T.astype(f32, copy=False))
    am["wdT"] = np.ascontiguousarray(inputs["L0_ffn_down_w"].T.astype(f32, copy=False))
    am["convw"] = np.ascontiguousarray(
        inputs["L0_attn_conv_w"].reshape(NCT, 128, 4).transpose(1, 0, 2).reshape(128, NCT * 4))
    am["convb"] = np.ascontiguousarray(inputs["L0_attn_conv_b"].reshape(NCT, 128).T)
    am["dtbias"] = np.ascontiguousarray(inputs["L0_attn_dt_bias"].reshape(64, 1))
    am["alog"] = np.ascontiguousarray(inputs["L0_attn_A_log"].reshape(64, 1))
    am["drow"] = np.ascontiguousarray(inputs["L0_attn_D"].reshape(1, 64))
    am["nwrow"] = np.ascontiguousarray(inputs["L0_attn_norm_w"].reshape(1, DIN))
    am["dsrow"] = np.ascontiguousarray(inputs["L0_ffn_down_s"].reshape(1, D))
    am["gs"] = np.ascontiguousarray(inputs["L0_ffn_gate_s"].reshape(44, 128).T)
    am["us"] = np.ascontiguousarray(inputs["L0_ffn_up_s"].reshape(44, 128).T)
    am["idf"] = np.eye(64, dtype=f32)
    k2c = np.zeros((10, 1280), np.float32)
    k2c[8:10, 0:256] = 1.0
    for m in range(4):
        k2c[m, 256 + m * 128:256 + (m + 1) * 128] = 1.0
        k2c[4 + m, 256 + m * 128:256 + (m + 1) * 128] = 1.0
        k2c[m, 768 + m * 128:768 + (m + 1) * 128] = 1.0
        k2c[4 + m, 768 + m * 128:768 + (m + 1) * 128] = 1.0
    am["k2c"] = k2c.astype(ml_dtypes.bfloat16)
    am["idb"] = np.eye(128, dtype=f32)
    am["mask01"] = np.triu(np.ones((128, 128), dtype=f32))
    return am


def kernel(**inputs):
    x = np.asarray(inputs["L0_attn_hidden_states"], dtype=np.float32)
    am = _host_arrays({k: np.asarray(v) for k, v in inputs.items()})
    nc = build_program()
    in_maps = []
    for c in range(8):
        m = dict(am)
        m["x"] = np.ascontiguousarray(x[2 * c:2 * c + 2])
        in_maps.append(m)
    res = run_bass_kernel_spmd(nc, in_maps, core_ids=list(range(8)))
    return np.concatenate([r["out"] for r in res.results], axis=0)


if __name__ == "__main__":
    pass
